# version 16
# baseline (speedup 1.0000x reference)
import math
from contextlib import ExitStack

import numpy as np
import concourse.bass as bass
import concourse.mybir as mybir
from concourse.bass_utils import run_bass_kernel_spmd

F32 = mybir.dt.float32
BF16 = mybir.dt.bfloat16
AF = mybir.ActivationFunctionType
ALU = mybir.AluOpType
AX = mybir.AxisListType

B, N, D, HH, DH = 4, 4096, 1024, 16, 64
NQ = 2048
CH = 512
NCH = N // CH
OC = NQ // CH
CC = 256
NCC = NQ // CC
FF = 4096
LOG100 = float(np.float32(math.log(100.0)))
EPS_LN = 1e-5
EPS_Q = 1e-5
EPS_NRM = 1e-12


def _host_consts():
    c = np.zeros((128, 195), dtype=np.float32)
    c[:, 194] = EPS_LN
    c[:, 0] = 1.0
    c[:, 1] = 1.0 / 1024.0
    for m in range(64):
        c[m, 2 + m] = 1.0
        c[m + 64, 2 + m] = 1.0
    for t in range(8):
        c[0:64, 66 + 16 * t + 2 * t] = 1.0
        c[64:128, 66 + 16 * t + 2 * t + 1] = 1.0
    c16 = np.zeros((16, 1152), dtype=np.float32)
    for t in range(8):
        c16[2 * t, 128 * t + 0: 128 * t + 64] = 1.0
        c16[2 * t + 1, 128 * t + 64: 128 * t + 128] = 1.0
    c16[0, 1024:1152] = 1.0
    return c, c16


class Em:

    def __init__(self, active, eng, sems):
        self.active = active
        self.eng = eng
        self.sems = sems
        self.cnt = {k: 0 for k in sems}

    def do(self, eng, fn, inc=None, n=1):
        if eng == self.active:
            inst = fn(self.eng)
            if inc is not None:
                inst.then_inc(self.sems[inc], n)
        if inc is not None:
            self.cnt[inc] += n
        return self.cnt[inc] if inc is not None else None

    def dma(self, eng, out, in_, inc):
        return self.do(eng, lambda e: e.dma_start(out=out, in_=in_), inc=inc, n=16)

    def wait(self, eng, sem, val):
        if self.active == eng and val > 0:
            self.eng.wait_ge(self.sems[sem], val)

    def drain(self, eng):
        if self.active == eng:
            self.eng.drain()

    def selfw(self, eng):
        sem = {"t": "t", "s": "s", "v": "v", "g": "g"}[eng]
        self.wait(eng, sem, self.cnt[sem])


def build_nc(debug=False):
    nc = bass.Bass("TRN2", target_bir_lowering=False, debug=False)

    xT = nc.dram_tensor("xT", [D, N], F32, kind="ExternalInput").ap()
    x_own = nc.dram_tensor("x_own", [NQ, D], F32, kind="ExternalInput").ap()
    w_qkv = nc.dram_tensor("w_qkv", [D, 3 * D], F32, kind="ExternalInput").ap()
    w_out = nc.dram_tensor("w_out", [D, D], F32, kind="ExternalInput").ap()
    W1T = nc.dram_tensor("W1T", [D, FF], F32, kind="ExternalInput").ap()
    W2T = nc.dram_tensor("W2T", [FF, D], F32, kind="ExternalInput").ap()
    b1c_in = nc.dram_tensor("b1c", [128, 32], F32, kind="ExternalInput").ap()
    brow_out = nc.dram_tensor("brow_out", [1, D], F32, kind="ExternalInput").ap()
    brow2 = nc.dram_tensor("brow2", [1, D], F32, kind="ExternalInput").ap()
    ls_in = nc.dram_tensor("ls", [1, HH], F32, kind="ExternalInput").ap()
    consts_in = nc.dram_tensor("consts_in", [128, 195], F32, kind="ExternalInput").ap()
    c16_in = nc.dram_tensor("consts16_in", [16, 1152], F32, kind="ExternalInput").ap()
    y_out = nc.dram_tensor("y", [NQ, D], F32, kind="ExternalOutput").ap()

    wscr_hi = nc.dram_tensor("wscr_hi", [8, 128, 3 * D], BF16, kind="Internal").ap()
    wscr_lo = nc.dram_tensor("wscr_lo", [8, 128, 3 * D], BF16, kind="Internal").ap()
    woscr = nc.dram_tensor("woscr", [2, 8, 128, D], BF16, kind="Internal").ap()
    wq1s = nc.dram_tensor("wq1s", [8, 128, FF], BF16, kind="Internal").ap()
    wq2s = nc.dram_tensor("wq2s", [32, 128, D], BF16, kind="Internal").ap()
    xnscr_hi = nc.dram_tensor("xnscr_hi", [OC, 8, 128, CH], BF16, kind="Internal").ap()
    xnscr_lo = nc.dram_tensor("xnscr_lo", [OC, 8, 128, CH], BF16, kind="Internal").ap()
    kscr_hi = nc.dram_tensor("kscr_hi", [NCH, 8, 128, CH], BF16, kind="Internal").ap()
    kscr_lo = nc.dram_tensor("kscr_lo", [NCH, 8, 128, CH], BF16, kind="Internal").ap()
    qscr_hi = nc.dram_tensor("qscr_hi", [OC, 8, 128, CH], BF16, kind="Internal").ap()
    qscr_lo = nc.dram_tensor("qscr_lo", [OC, 8, 128, CH], BF16, kind="Internal").ap()
    vscr_hi = nc.dram_tensor("vscr_hi", [NCH, 4, 128, D], BF16, kind="Internal").ap()
    vscr_lo = nc.dram_tensor("vscr_lo", [NCH, 4, 128, D], BF16, kind="Internal").ap()

    dbg = {}
    if debug:
        for nm, shp, dt in [("xn_hi", [OC, 8, 128, CH], BF16), ("xn_lo", [OC, 8, 128, CH], BF16),
                            ("q_hi", [OC, 8, 128, CH], BF16), ("q_lo", [OC, 8, 128, CH], BF16),
                            ("k_hi", [NCH, 8, 128, CH], BF16),
                            ("v_hi", [NCH, 4, 128, D], BF16), ("v_lo", [NCH, 4, 128, D], BF16),
                            ("at_hi", [128, 8 * NQ], BF16), ("at_lo", [128, 8 * NQ], BF16),
                            ("wq1d", [8, 128, FF], BF16), ("wq2d", [32, 128, D], BF16)]:
            dbg[nm] = nc.dram_tensor("dbg_" + nm, shp, dt, kind="ExternalOutput").ap()

    SBTOP = 229312
    R0_SZ = 65536
    R1_SZ = 110592
    R0B = SBTOP - R0_SZ
    R1B = R0B - R1_SZ
    RQ_SZ = 24576
    RQB = R1B - RQ_SZ

    class Arena:
        def __init__(self, base, limit):
            self.off = base
            self.limit = limit

        def take(self, name, shape, dtype):
            nb = int(np.prod(shape[1:])) * mybir.dt.size(dtype)
            t = nc.alloc_sbuf_tensor_at(name, shape, dtype, offset=self.off)
            self.off += (nb + 63) // 64 * 64
            assert self.off <= self.limit, (name, self.off, self.limit)
            return t

    SB = nc.alloc_sbuf_tensor
    consts = SB("consts", [128, 195], F32)
    c16 = SB("c16", [16, 1152], F32)
    b1c = SB("b1cs", [128, 32], F32)
    btot = SB("btot", [1, D], F32)
    ls = SB("lss", [1, HH], F32)
    srow = SB("srow", [1, HH], F32)
    scols = SB("scols", [128, HH], F32)
    nscols = SB("nscols", [128, HH], F32)
    onesb = SB("onesb", [128, 1], BF16)
    rcol1 = SB("rcol1", [128, 1], F32)
    rcol2 = SB("rcol2", [128, 1], F32)
    grow = SB("grow", [1, 1], F32)
    g2row = SB("g2row", [1, 1], F32)
    gacc = SB("gacc", [128, 42], F32)
    wt = nc.alloc_sbuf_tensor_at("wt", [128, FF], F32, offset=RQB)
    wqt = nc.alloc_sbuf_tensor_at("wqt", [128, FF], BF16, offset=RQB + 16384)

    r0a = Arena(R0B, SBTOP)
    attnT_hi = r0a.take("attnT_hi", [128, 8 * NQ], BF16)
    attnT_lo = r0a.take("attnT_lo", [128, 8 * NQ], BF16)
    r0b = Arena(R0B, SBTOP)
    xt = [r0b.take(f"xt{i}", [128, 8 * CH], F32) for i in range(2)]
    wbh = [r0b.take(f"wbh{i}", [128, 8 * CH], BF16) for i in range(2)]
    wbl = [r0b.take(f"wbl{i}", [128, 8 * CH], BF16) for i in range(2)]

    ra = Arena(R1B, R0B)
    sqb = ra.take("sqb", [128, 5 * CH], F32)
    t1b2 = [ra.take(f"t1b{i}", [128, CH], F32) for i in range(2)]
    xn32b2 = [ra.take(f"xn32b{i}", [128, CH], F32) for i in range(2)]
    xnh2 = [ra.take(f"xnh{i}", [128, 8 * CH], BF16) for i in range(2)]
    xnl2 = [ra.take(f"xnl{i}", [128, 8 * CH], BF16) for i in range(2)]
    p32 = [ra.take(f"p32_{i}", [128, CH], F32) for i in range(12)]
    kstg_h = ra.take("kstg_h", [128, 2 * CH], BF16)
    kstg_l = ra.take("kstg_l", [128, 2 * CH], BF16)
    vstg_h = ra.take("vstg_h", [128, D], BF16)
    vstg_l = ra.take("vstg_l", [128, D], BF16)
    wbh3 = ra.take("wbh3", [128, 8 * CH], BF16)
    wbl3 = ra.take("wbl3", [128, 8 * CH], BF16)
    mrow = ra.take("mrow", [1, CH], F32)
    m2row = ra.take("m2row", [1, CH], F32)
    rstdrow = ra.take("rstdrow", [1, CH], F32)
    nrmall = ra.take("nrmall", [16, CH], F32)
    rnall = ra.take("rnall", [16, CH], F32)

    rb = Arena(R1B, R0B)
    krh = [rb.take(f"krh{i}", [128, N], BF16) for i in range(2)]
    krl = [rb.take(f"krl{i}", [128, N], BF16) for i in range(2)]
    vpk = [rb.take(f"vpk{i}", [128, 32 * 128], BF16) for i in range(2)]
    qpk = [rb.take(f"qpk{i}", [128, NQ], BF16) for i in range(2)]
    ahi = [rb.take(f"ahi{i}", [128, 1024], BF16) for i in range(4)]
    alo = [rb.take(f"alo{i}", [128, 1024], BF16) for i in range(4)]
    a32 = [rb.take(f"a32_{i}", [128, 1024], F32) for i in range(4)]
    av32 = rb.take("av32", [128, 512], F32)
    rdbc = rb.take("rdbc", [128, 512], F32)
    prod32 = rb.take("prod32", [128, 512], F32)
    lnrow = rb.take("lnrow", [1, 512], F32)
    rdrow = rb.take("rdrow", [1, 512], F32)

    rc_ = Arena(R1B, R0B)
    cxnh = [rc_.take(f"cxnh{i}", [128, 8 * CC], BF16) for i in range(2)]
    cxnl = [rc_.take(f"cxnl{i}", [128, 8 * CC], BF16) for i in range(2)]
    hh_sb = rc_.take("hh_sb", [128, 32 * CC], BF16)
    hl_sb = rc_.take("hl_sb", [128, 32 * CC], BF16)
    h32b = rc_.take("h32b", [128, 4 * CC], F32)
    wq1b = [rc_.take(f"wq1b{i}", [128, 8 * 128], BF16) for i in range(4)]
    wq2b = [rc_.take(f"wq2b{i}", [128, D], BF16) for i in range(4)]
    xob = [rc_.take(f"xob{i}", [128, 2 * D], F32) for i in range(2)]
    outb = [rc_.take(f"outb{i}", [128, 2 * D], F32) for i in range(2)]
    wob_h = [rc_.take(f"wob_h{i}", [128, D], BF16) for i in range(2)]
    wob_l = [rc_.take(f"wob_l{i}", [128, D], BF16) for i in range(2)]

    PS = nc.alloc_psum_tensor
    ps_wa = PS("ps_wa", [128, 1024], F32)
    ps_wb = PS("ps_wb", [128, 1024], F32)
    ps_wc = PS("ps_wc", [128, 1024], F32)
    ps_av = PS("ps_av", [128, 512], F32)
    ps_x = PS("ps_x", [128, 512], F32)

    est = ExitStack()
    sems = {name: est.enter_context(nc.semaphore(name))
            for name in ["dc", "dx0", "dx1", "dwsp0", "dwsp1", "dwb0", "dwb1", "dwb2", "dstg", "dvsp0",
                         "dxnp", "dkv0", "dkv1", "dkw0", "dkw1", "dcx0", "dcx1", "dxo0", "dxo1",
                         "dw10", "dw11", "dw12", "dw13", "dw20", "dw21", "dw22", "dw23", "dwob0", "dwob1", "dout0", "dout1",
                         "dg", "dq", "t", "s", "v", "g"]}

    onescol = consts[:, 0:1]
    cmean = consts[:, 1:2]
    Fc = consts[:, 2:66]

    def Et(t):
        return consts[:, 66 + 16 * t: 82 + 16 * t]

    def Ft(t):
        return c16[:, 128 * t: 128 * (t + 1)]

    ones_r = c16[0:1, 1024:1152]

    def r3(ap3):
        return ap3.transpose([1, 0, 2])

    def walk(em: Em):
        qk_slots = [ps_wa[:, 0:512], ps_wa[:, 512:1024], ps_wb[:, 0:512],
                    ps_wb[:, 512:1024], ps_wc[:, 0:512]]
        slot_free = [0] * 5
        slot_free_s = [0] * 5

        for (sb, ap) in [(consts, consts_in), (c16, c16_in), (b1c, b1c_in),
                         (ls, ls_in)]:
            em.dma("y", sb[:], ap, "dc")
        ck_c0 = em.cnt["dc"]

        em.dma("y", xt[0][0:1, 0:D], brow_out, "dx0")
        ck_br = em.dma("y", xt[0][0:1, D:2 * D], brow2, "dx0")
        em.wait("v", "dx0", ck_br)
        em.do("v", lambda e: e.tensor_add(out=btot[:], in0=xt[0][0:1, 0:D], in1=xt[0][0:1, D:2 * D]), inc="v")
        ck_btot = em.cnt["v"]

        ck_w_ld = [0, 0]
        ck_w_v = [ck_btot, 0]
        ck_wspill = [0, 0]
        for dt in range(8):
            b = dt % 2
            em.wait("y", "v", ck_w_v[b])
            ck_w_ld[b] = em.dma("y", xt[b][:, 0:3 * D], w_qkv[dt * 128:(dt + 1) * 128, :], f"dx{b}")
            em.wait("v", f"dx{b}", ck_w_ld[b])
            em.wait("v", f"dwsp{b}", ck_wspill[b])
            em.do("v", lambda e, b=b: e.tensor_copy(out=wbh[b][:, 0:3 * D], in_=xt[b][:, 0:3 * D]), inc="v")
            em.selfw("v")
            em.do("v", lambda e, b=b: e.scalar_tensor_tensor(
                out=wbl[b][:, 0:3 * D], in0=xt[b][:, 0:3 * D], scalar=1.0,
                in1=wbh[b][:, 0:3 * D], op0=ALU.mult, op1=ALU.subtract), inc="v")
            ck_w_v[b] = em.cnt["v"]
            em.wait("y", "v", ck_w_v[b])
            em.dma("y", wscr_hi[dt], wbh[b][:, 0:3 * D], f"dwsp{b}")
            ck_wspill[b] = em.dma("y", wscr_lo[dt], wbl[b][:, 0:3 * D], f"dwsp{b}")

        for dt in range(8):
            b = dt % 2
            em.wait("y", "v", ck_w_v[b])
            ck_w_ld[b] = em.dma("y", xt[b][:, 0:D], w_out[dt * 128:(dt + 1) * 128, :], f"dx{b}")
            em.wait("v", f"dx{b}", ck_w_ld[b])
            em.wait("v", f"dwsp{b}", ck_wspill[b])
            em.do("v", lambda e, b=b: e.tensor_copy(out=wbh[b][:, 0:D], in_=xt[b][:, 0:D]), inc="v")
            em.selfw("v")
            em.do("v", lambda e, b=b: e.scalar_tensor_tensor(
                out=wbl[b][:, 0:D], in0=xt[b][:, 0:D], scalar=1.0,
                in1=wbh[b][:, 0:D], op0=ALU.mult, op1=ALU.subtract), inc="v")
            ck_w_v[b] = em.cnt["v"]
            em.wait("y", "v", ck_w_v[b])
            em.dma("y", woscr[0, dt], wbh[b][:, 0:D], f"dwsp{b}")
            ck_wspill[b] = em.dma("y", woscr[1, dt], wbl[b][:, 0:D], f"dwsp{b}")
        ck_wsp_tot = [em.cnt["dwsp0"], em.cnt["dwsp1"]]

        em.wait("v", "dc", ck_c0)
        em.do("v", lambda e: e.tensor_scalar_min(out=srow[:], in0=ls[:], scalar1=LOG100), inc="v")
        ck_ls = em.cnt["v"]
        em.wait("s", "v", ck_ls)
        em.do("s", lambda e: e.activation(out=srow[:], in_=srow[:], func=AF.Exp), inc="s")
        ck_s0 = em.cnt["s"]
        em.wait("t", "s", ck_s0)
        em.wait("t", "dc", ck_c0)
        for h in range(HH):
            em.do("t", lambda e, h=h: e.matmul(ps_x[:, h:h + 1], ones_r, srow[0:1, h:h + 1],
                                               start=True, stop=True),
                  inc=("t" if h == HH - 1 else None))
        ck_t0 = em.cnt["t"]
        em.wait("v", "t", ck_t0)
        em.do("v", lambda e: e.tensor_copy(out=onesb[:], in_=consts[:, 0:1]), inc="v")
        em.do("v", lambda e: e.tensor_copy(out=scols[:], in_=ps_x[:, 0:HH]), inc="v")
        em.selfw("v")
        em.do("v", lambda e: e.tensor_scalar_mul(out=nscols[:], in0=scols[:], scalar1=-1.0), inc="v")
        ck_scols = em.cnt["v"]
        ps_x_free_v = ck_scols

        gw_ck_s = [0]
        gw_ck_dg = [0]

        def gamma_tile(i):
            src_ap = W1T[i * 128:(i + 1) * 128, :] if i < 8 else W2T[(i - 8) * 128:(i - 7) * 128, :]
            width = FF if i < 8 else D
            em.wait("g", "s", gw_ck_s[0])
            gw_ck_dg[0] = em.dma("g", wt[:, 0:width], src_ap, "dg")
            em.wait("s", "dg", gw_ck_dg[0])
            em.do("s", lambda e, i=i, width=width: e.activation(
                out=wqt[:, 0:width], in_=wt[:, 0:width], func=AF.Abs,
                accum_out=gacc[:, i:i + 1]), inc="s")
            gw_ck_s[0] = em.cnt["s"]

        ck_xt_free = [em.cnt["v"], em.cnt["v"]]
        ck_xt = [0, 0]
        wb_i = [0]
        ck_wb_ld = [0, 0, 0]
        ck_wb_free_t = [0, 0, 0]
        wbhs = [wbh[0], wbh[1], wbh3]
        wbls = [wbl[0], wbl[1], wbl3]
        ck_kstg_sp = 0
        ck_vstg_sp = [0, 0]
        ck_xn_sp = 0
        ck_sqb_t = [0]
        ck_xn_v = [0, 0]
        ck_xn_done_t = [0, 0]
        ck_psz_t = 0
        ck_ss_free_s = [0, 0]
        ck_psav_v = 0
        ck_rnbcq_v = 0
        si = [0]

        def load_xT(c):
            bx = c % 2
            em.wait("y", "v", ck_xt_free[bx])
            ck_xt[bx] = em.dma("y", xt[bx][:].rearrange("p (a j) -> p a j", a=8),
                               r3(xT.rearrange("(a p) n -> a p n", p=128)[:, :, c * CH:(c + 1) * CH]),
                               f"dx{bx}")

        def header(c):
            nonlocal ck_xn_sp
            bx = c % 2
            xnh, xnl = xnh2[bx], xnl2[bx]
            em.wait("v", f"dx{bx}", ck_xt[bx])
            em.wait("v", "t", max(ck_sqb_t[0], ck_psz_t))
            em.selfw("v")
            for dc in range(8):
                dst = (sqb[:, dc * CH:(dc + 1) * CH] if dc < 5 else
                       [t1b2[0], t1b2[1], xn32b2[0]][dc - 5][:])
                em.do("v", lambda e, dc=dc, bx=bx, dst=dst: e.tensor_mul(
                    out=dst, in0=xt[bx][:, dc * CH:(dc + 1) * CH],
                    in1=xt[bx][:, dc * CH:(dc + 1) * CH]), inc=("v" if dc == 7 else None))
            sq_ck = em.cnt["v"]
            em.wait("t", f"dx{bx}", ck_xt[bx])
            em.wait("t", "v", ck_rnbcq_v)
            em.wait("t", "s", ck_ss_free_s[0])
            for dc in range(8):
                em.do("t", lambda e, dc=dc, bx=bx: e.matmul(
                    ps_x[0:1, 0:CH], cmean, xt[bx][:, dc * CH:(dc + 1) * CH],
                    start=(dc == 0), stop=(dc == 7)), inc=("t" if dc == 7 else None))
            em.wait("t", "v", sq_ck)
            em.wait("t", "v", ck_psav_v)
            for dc in range(8):
                s_ap = (sqb[:, dc * CH:(dc + 1) * CH] if dc < 5 else
                        [t1b2[0], t1b2[1], xn32b2[0]][dc - 5][:])
                em.do("t", lambda e, dc=dc, s_ap=s_ap: e.matmul(
                    ps_av[0:1, 0:CH], cmean, s_ap,
                    start=(dc == 0), stop=(dc == 7)), inc=("t" if dc == 7 else None))
            ck_sqb_t[0] = em.cnt["t"]
            ck_stats_t = em.cnt["t"]
            em.wait("v", "t", ck_stats_t)
            em.do("v", lambda e: e.tensor_copy(out=mrow[:], in_=ps_x[0:1, 0:CH]), inc="v")
            em.selfw("v")
            em.do("v", lambda e: e.tensor_mul(out=m2row[:], in0=mrow[:], in1=mrow[:]), inc="v")
            em.selfw("v")
            em.do("v", lambda e: e.scalar_tensor_tensor(out=rstdrow[:], in0=ps_av[0:1, 0:CH], scalar=1.0,
                                                        in1=m2row[:], op0=ALU.mult, op1=ALU.subtract), inc="v")
            ck_var = em.cnt["v"]
            em.wait("s", "v", ck_var)
            em.do("s", lambda e: e.activation(out=rstdrow[:], in_=rstdrow[:], func=AF.Sqrt, bias=consts[0:1, 194:195]), inc="s")
            ck_sqt = em.cnt["s"]
            em.wait("v", "s", ck_sqt)
            em.do("v", lambda e: e.reciprocal(out=m2row[:], in_=rstdrow[:]), inc="v")
            ck_rstd = em.cnt["v"]
            em.wait("t", "v", ck_rstd)
            em.do("t", lambda e: e.matmul(ps_x[:, 0:CH], ones_r, mrow[:], start=True, stop=True), inc="t")
            em.do("t", lambda e: e.matmul(ps_av[:, 0:CH], ones_r, m2row[:], start=True, stop=True), inc="t")
            ck_bc = em.cnt["t"]
            em.wait("v", "t", ck_bc)
            em.wait("v", "dxnp", ck_xn_sp)
            em.wait("v", "t", ck_xn_done_t[bx])
            for dc in range(8):
                pb = dc % 2
                em.selfw("v")
                em.do("v", lambda e, dc=dc, bx=bx, pb=pb: e.scalar_tensor_tensor(
                    out=t1b2[pb][:], in0=xt[bx][:, dc * CH:(dc + 1) * CH], scalar=1.0,
                    in1=ps_x[:, 0:CH], op0=ALU.mult, op1=ALU.subtract), inc="v")
                em.selfw("v")
                em.do("v", lambda e, pb=pb: e.tensor_mul(out=xn32b2[pb][:], in0=t1b2[pb][:],
                                                         in1=ps_av[:, 0:CH]), inc="v")
                ckv = em.cnt["v"]
                em.wait("s", "v", ckv)
                em.wait("s", "dxnp", ck_xn_sp)
                em.do("s", lambda e, dc=dc, pb=pb, xnh=xnh: e.activation(out=xnh[:, dc * CH:(dc + 1) * CH],
                                                                         in_=xn32b2[pb][:], func=AF.Copy), inc="s")
                cks = em.cnt["s"]
                em.wait("v", "s", cks)
                em.do("v", lambda e, dc=dc, pb=pb, xnh=xnh, xnl=xnl: e.scalar_tensor_tensor(
                    out=xnl[:, dc * CH:(dc + 1) * CH], in0=xn32b2[pb][:], scalar=1.0,
                    in1=xnh[:, dc * CH:(dc + 1) * CH], op0=ALU.mult, op1=ALU.subtract), inc="v")
            ck_xn_v[bx] = em.cnt["v"]
            ck_xt_free[bx] = em.cnt["v"]
            if c < OC:
                em.wait("s", "v", ck_xn_v[bx])
                em.dma("s", r3(xnscr_hi[c]), xnh[:].rearrange("p (a j) -> p a j", a=8), "dxnp")
                ck_xn_sp = em.dma("s", r3(xnscr_lo[c]), xnl[:].rearrange("p (a j) -> p a j", a=8), "dxnp")
                if debug:
                    em.dma("s", r3(dbg["xn_hi"][c]), xnh[:].rearrange("p (a j) -> p a j", a=8), "dxnp")
                    ck_xn_sp = em.dma("s", r3(dbg["xn_lo"][c]), xnl[:].rearrange("p (a j) -> p a j", a=8), "dxnp")

        def stream_w(cg):
            b = wb_i[0] % 3
            wb_i[0] += 1
            em.wait("y", "t", ck_wb_free_t[b])
            em.wait("y", "dwsp0", ck_wsp_tot[0])
            em.wait("y", "dwsp1", ck_wsp_tot[1])
            em.dma("y", wbhs[b][:].rearrange("p (a j) -> p a j", a=8),
                   r3(wscr_hi[:, :, cg * 512:(cg + 1) * 512]), f"dwb{b}")
            ck_wb_ld[b] = em.dma("y", wbls[b][:].rearrange("p (a j) -> p a j", a=8),
                                 r3(wscr_lo[:, :, cg * 512:(cg + 1) * 512]), f"dwb{b}")
            return b

        pending_ss = [None]
        ck_ss_done = [0, 0]

        def flush_ss():
            nonlocal ck_psz_t
            if pending_ss[0] is None:
                return
            t, ss_ap, kind, ckv, par = pending_ss[0]
            pending_ss[0] = None
            em.wait("t", "v", ckv)
            if t == 0:
                em.wait("t", "s", ck_ss_free_s[kind])
            em.do("t", lambda e, t=t, ss_ap=ss_ap, par=par: e.matmul(
                ss_ap, Et(t), sqb[:, par * CH:(par + 1) * CH],
                start=(t == 0), stop=(t == 7)), inc="t")
            ck_psz_t = em.cnt["t"]
            if t == 7:
                ck_ss_done[kind] = ck_psz_t

        def qk_group(c, bw, base_t, pbase, ss_ap, kind):
            nonlocal ck_psz_t
            bx = c % 2
            xnh, xnl = xnh2[bx], xnl2[bx]
            for i in range(4):
                t = base_t + i
                pi = (pbase + t) % 12
                slot = si[0] % 5
                si[0] += 1
                em.wait("t", f"dwb{bw}", ck_wb_ld[bw])
                em.wait("t", "v", max(slot_free[slot], ck_xn_v[bx]))
                em.wait("t", "s", slot_free_s[slot])
                seq = [(dc, wsb, xsb) for dc in range(8)
                       for (wsb, xsb) in ((wbhs[bw], xnh), (wbhs[bw], xnl), (wbls[bw], xnh))]
                for j, (dc, wsb, xsb) in enumerate(seq):
                    last = j == len(seq) - 1
                    em.do("t", lambda e, wsb=wsb, xsb=xsb, dc=dc, i=i, slot=slot, j=j, last=last: e.matmul(
                        qk_slots[slot], wsb[:, dc * 512 + i * 128: dc * 512 + (i + 1) * 128],
                        xsb[:, dc * CH:(dc + 1) * CH], start=(j == 0), stop=last),
                        inc=("t" if last else None))
                ck_mm = em.cnt["t"]
                flush_ss()
                em.wait("v", "t", ck_mm)
                em.do("v", lambda e, pi=pi, slot=slot: e.tensor_copy(out=p32[pi][:], in_=qk_slots[slot]), inc="v")
                em.selfw("v")
                par = t % 2
                em.do("v", lambda e, pi=pi, par=par: e.tensor_mul(
                    out=sqb[:, par * CH:(par + 1) * CH], in0=p32[pi][:], in1=p32[pi][:]), inc="v")
                ckv = em.cnt["v"]
                slot_free[slot] = ckv
                pending_ss[0] = (t, ss_ap, kind, ckv, par)
            ck_wb_free_t[bw] = ck_mm

        def qk_normalize(scr_hi, scr_lo, dhi, dlo, pbase, ss_ap, kind, ck_sst, rn_ps, ck_rnps):
            nonlocal ck_psav_v, ck_rnbcq_v, ck_kstg_sp
            if pending_ss[0] is not None and pending_ss[0][2] == kind:
                flush_ss()
            em.wait("s", "t", ck_ss_done[kind])
            em.do("s", lambda e, ss_ap=ss_ap: e.activation(out=nrmall[:, 0:CH],
                                                           in_=ss_ap, func=AF.Sqrt), inc="s")
            ck_ss_free_s[kind] = em.cnt["s"]
            em.wait("v", "s", ck_ss_free_s[kind])
            em.do("v", lambda e: e.tensor_scalar_max(out=rnall[:, 0:CH], in0=nrmall[:, 0:CH],
                                                     scalar1=EPS_NRM), inc="v")
            em.selfw("v")
            em.do("v", lambda e: e.reciprocal(out=nrmall[:, 0:CH], in_=rnall[:, 0:CH]), inc="v")
            ck_rn = em.cnt["v"]
            rn_b = [ps_x[:, 0:CH], ps_av[:, 0:CH]]
            for t in range(8):
                pb = t % 2
                em.wait("t", "v", max(ck_rn, rn_ck[pb]))
                em.do("t", lambda e, t=t, pb=pb: e.matmul(rn_b[pb], Ft(t), nrmall[:, 0:CH],
                                                          start=True, stop=True), inc="t")
                ckt = em.cnt["t"]
                em.wait("v", "t", ckt)
                em.wait("v", "dstg", ck_kstg_sp)
                em.selfw("v")
                em.do("v", lambda e, t=t, pb=pb, pbase=pbase: e.tensor_mul(
                    out=t1b2[pb][:], in0=p32[(pbase + t) % 12][:],
                    in1=rn_b[pb]), inc="v")
                ckv = em.cnt["v"]
                rn_ck[pb] = ckv
                em.wait("s", "v", ckv)
                em.wait("s", "dstg", ck_kstg_sp)
                em.do("s", lambda e, t=t, pb=pb: e.activation(out=kstg_h[:, (t % 2) * CH:(t % 2 + 1) * CH],
                                                              in_=t1b2[pb][:], func=AF.Copy), inc="s")
                cks = em.cnt["s"]
                em.wait("v", "s", cks)
                em.do("v", lambda e, t=t, pb=pb: e.scalar_tensor_tensor(
                    out=kstg_l[:, (t % 2) * CH:(t % 2 + 1) * CH], in0=t1b2[pb][:], scalar=1.0,
                    in1=kstg_h[:, (t % 2) * CH:(t % 2 + 1) * CH], op0=ALU.mult, op1=ALU.subtract), inc="v")
                if t % 2 == 1:
                    half = t - 1
                    ck_stg = em.cnt["v"]
                    em.wait("s", "v", ck_stg)
                    em.dma("s", r3(scr_hi[half:half + 2]), kstg_h[:].rearrange("p (a j) -> p a j", a=2), "dstg")
                    ck_kstg_sp = em.dma("s", r3(scr_lo[half:half + 2]), kstg_l[:].rearrange("p (a j) -> p a j", a=2), "dstg")
                    if debug and dhi is not None:
                        ck_kstg_sp = em.dma("s", r3(dhi[half:half + 2]), kstg_h[:].rearrange("p (a j) -> p a j", a=2), "dstg")
                    if debug and dlo is not None:
                        ck_kstg_sp = em.dma("s", r3(dlo[half:half + 2]), kstg_l[:].rearrange("p (a j) -> p a j", a=2), "dstg")
            ck_rnbcq_v = max(rn_ck)
            ck_psav_v = max(rn_ck)

        def v_groups(c, b4, b5):
            bx = c % 2
            xnh, xnl = xnh2[bx], xnl2[bx]
            for st in range(4):
                sl = 0
                for cg in (4, 5):
                    bw = b4 if cg == 4 else b5
                    vcg = cg - 4
                    slot = si[0] % 5
                    si[0] += 1
                    em.wait("t", f"dwb{bw}", ck_wb_ld[bw])
                    em.wait("t", "v", max(slot_free[slot], ck_xn_v[bx]))
                    em.wait("t", "s", slot_free_s[slot])
                    seq = [(dc, xsb, wsb) for dc in range(8)
                           for (xsb, wsb) in ((xnh, wbhs[bw]), (xnl, wbhs[bw]), (xnh, wbls[bw]))]
                    for j, (dc, xsb, wsb) in enumerate(seq):
                        last = j == len(seq) - 1
                        em.do("t", lambda e, xsb=xsb, wsb=wsb, dc=dc, st=st, slot=slot, j=j, last=last: e.matmul(
                            qk_slots[slot], xsb[:, dc * CH + st * 128: dc * CH + (st + 1) * 128],
                            wsb[:, dc * 512:(dc + 1) * 512], start=(j == 0), stop=last),
                            inc=("t" if last else None))
                    ck_mm = em.cnt["t"]
                    em.wait("s", "t", ck_mm)
                    em.wait("s", "dvsp0", ck_vstg_sp[0])
                    em.do("s", lambda e, vcg=vcg, slot=slot: e.activation(
                        out=vstg_h[:, vcg * 512:(vcg + 1) * 512],
                        in_=qk_slots[slot], func=AF.Copy), inc="s")
                    cks = em.cnt["s"]
                    em.wait("v", "s", cks)
                    em.wait("v", "dvsp0", ck_vstg_sp[0])
                    em.do("v", lambda e, vcg=vcg, slot=slot: e.scalar_tensor_tensor(
                        out=vstg_l[:, vcg * 512:(vcg + 1) * 512],
                        in0=qk_slots[slot], scalar=1.0,
                        in1=vstg_h[:, vcg * 512:(vcg + 1) * 512],
                        op0=ALU.mult, op1=ALU.subtract), inc="v")
                    slot_free[slot] = em.cnt["v"]
                    slot_free_s[slot] = cks
                    ck_wb_free_t[bw] = ck_mm
                ckv = em.cnt["v"]
                cks = em.cnt["s"]
                em.wait("s", "v", ckv)
                em.dma("s", vscr_hi[c, st], vstg_h[:, 0:D], "dvsp0")
                ck_vstg_sp[0] = em.dma("s", vscr_lo[c, st], vstg_l[:, 0:D], "dvsp0")
                if debug:
                    em.dma("s", dbg["v_hi"][c, st], vstg_h[:, 0:D], "dvsp0")
                    ck_vstg_sp[0] = em.dma("s", dbg["v_lo"][c, st], vstg_l[:, 0:D], "dvsp0")

        rn_ck = [0, 0]
        load_xT(0)
        header(0)
        nb = [stream_w(0), stream_w(1)]
        for c in range(NCH):
            own = c < OC
            bx = c % 2
            if c + 1 < NCH:
                load_xT(c + 1)
            if own:
                qk_group(c, nb[0], 0, 0, ps_x[0:16, 0:CH], 0)
                b2 = stream_w(2)
                qk_group(c, nb[1], 4, 0, ps_x[0:16, 0:CH], 0)
                ck_ss_q = ck_psz_t
                b3 = stream_w(3)
                qk_group(c, b2, 0, 8, ps_wc[0:16, 512:1024], 1)
                qk_normalize(qscr_hi[c], qscr_lo[c],
                             dbg["q_hi"][c] if debug else None,
                             dbg["q_lo"][c] if debug else None, 0,
                             ps_x[0:16, 0:CH], 0, ck_ss_q, None, None)
                b4 = stream_w(4)
                qk_group(c, b3, 4, 8, ps_wc[0:16, 512:1024], 1)
                ck_ss_k = ck_psz_t
                b5 = stream_w(5)
                if c + 1 < NCH:
                    nb0 = stream_w(0 if (c + 1) < OC else 2)
                qk_normalize(kscr_hi[c], kscr_lo[c],
                             dbg["k_hi"][c] if debug else None, None, 8,
                             ps_wc[0:16, 512:1024], 1, ck_ss_k, None, None)
            else:
                qk_group(c, nb[0], 0, 8, ps_wc[0:16, 512:1024], 1)
                b4 = stream_w(4)
                qk_group(c, nb[1], 4, 8, ps_wc[0:16, 512:1024], 1)
                ck_ss_k = ck_psz_t
                b5 = stream_w(5)
                if c + 1 < NCH:
                    nb0 = stream_w(0 if (c + 1) < OC else 2)
                qk_normalize(kscr_hi[c], kscr_lo[c],
                             dbg["k_hi"][c] if debug else None, None, 8,
                             ps_wc[0:16, 512:1024], 1, ck_ss_k, None, None)
            if c + 1 < NCH:
                header(c + 1)
            v_groups(c, b4, b5)
            ck_xn_done_t[bx] = em.cnt["t"]
            if c + 1 < NCH:
                nb = [nb0, stream_w(1 if (c + 1) < OC else 3)]
            for gi in range(5 * c, min(5 * (c + 1), 40)):
                gamma_tile(gi)

        ck_A_stg = em.cnt["dstg"]
        ck_A_vsp = [em.cnt["dvsp0"], 0]
        ck_A_xnp = em.cnt["dxnp"]
        ck_gamma_s = gw_ck_s[0]
        em.wait("v", "s", ck_gamma_s)
        em.do("v", lambda e: e.tensor_reduce(out=gacc[:, 40:41], in_=gacc[:, 0:8],
                                             axis=AX.X, op=ALU.add), inc="v")
        em.do("v", lambda e: e.tensor_reduce(out=gacc[:, 41:42], in_=gacc[:, 8:40],
                                             axis=AX.X, op=ALU.add), inc="v")
        ck_g1 = em.cnt["v"]
        ck_g2 = ck_g1
        ps_x_free_s = 0

        def gamma_finish(ck_g, gfin, nelem, rc):
            nonlocal ps_x_free_v, ps_x_free_s
            em.wait("t", "v", ck_g)
            em.wait("t", "v", ps_x_free_v)
            em.do("t", lambda e: e.matmul(ps_x[0:1, 0:1], gacc[:, gfin:gfin + 1], onescol,
                                          start=True, stop=True), inc="t")
            ckt = em.cnt["t"]
            em.wait("v", "t", ckt)
            em.do("v", lambda e: e.tensor_scalar(out=grow[:], in0=ps_x[0:1, 0:1],
                                                 scalar1=1.0 / nelem, scalar2=EPS_Q,
                                                 op0=ALU.mult, op1=ALU.add), inc="v")
            em.selfw("v")
            em.do("v", lambda e: e.reciprocal(out=g2row[:], in_=grow[:]), inc="v")
            ckv = em.cnt["v"]
            em.wait("t", "v", ckv)
            em.do("t", lambda e: e.matmul(ps_x[:, 1:2], ones_r, g2row[:], start=True, stop=True), inc="t")
            ckt2 = em.cnt["t"]
            em.wait("v", "t", ckt2)
            em.do("v", lambda e, rc=rc: e.tensor_copy(out=rc[:], in_=ps_x[:, 1:2]), inc="v")
            ps_x_free_v = em.cnt["v"]
            return em.cnt["v"]

        ck_rc1 = gamma_finish(ck_g1, 40, FF * D, rcol1)
        ck_rc2 = gamma_finish(ck_g2, 41, FF * D, rcol2)

        q2_ld = [0]
        q2_sp = [0]
        q2_v = [0]

        def quant2_tile(i):
            if i < 8:
                src_ap, width, dst, rc, ck_rc, dkey, ti = W1T[i * 128:(i + 1) * 128, :], FF, wq1s, rcol1, ck_rc1, "wq1d", i
            else:
                src_ap, width, dst, rc, ck_rc, dkey, ti = W2T[(i - 8) * 128:(i - 7) * 128, :], D, wq2s, rcol2, ck_rc2, "wq2d", i - 8
            if i == 0:
                em.wait("y", "s", gw_ck_s[0])
            em.wait("y", "v", q2_v[0])
            em.wait("y", "dq", q2_sp[0])
            q2_ld[0] = em.dma("y", wt[:, 0:width], src_ap, "dq")
            em.wait("v", "dq", q2_ld[0])
            em.wait("v", "v", ck_rc)
            em.selfw("v")
            em.do("v", lambda e, rc=rc, width=width: e.tensor_scalar(
                out=wqt[:, 0:width], in0=wt[:, 0:width], scalar1=rc[:],
                scalar2=0.5, op0=ALU.mult, op1=ALU.is_gt), inc="v")
            em.selfw("v")
            em.do("v", lambda e, rc=rc, width=width: e.tensor_scalar(
                out=wt[:, 0:width], in0=wt[:, 0:width], scalar1=rc[:],
                scalar2=-0.5, op0=ALU.mult, op1=ALU.is_lt), inc="v")
            em.selfw("v")
            em.do("v", lambda e, width=width: e.tensor_sub(out=wqt[:, 0:width], in0=wqt[:, 0:width],
                                                           in1=wt[:, 0:width]), inc="v")
            q2_v[0] = em.cnt["v"]
            em.wait("y", "v", q2_v[0])
            q2_sp[0] = em.dma("y", dst[ti], wqt[:, 0:width], "dq")
            if debug:
                q2_sp[0] = em.dma("y", dbg[dkey][ti], wqt[:, 0:width], "dq")

        sB = em.cnt["s"]
        vB = em.cnt["v"]
        ck_kv_ld = [0, 0]
        ck_kv_free_t = [0, 0]
        ck_sgrp_s = [sB, sB, sB]
        ck_exp_s = [sB, sB, sB, sB]
        ck_sgrp_v = [vB, vB, vB, vB]
        ck_ab_t = [0, 0, 0, 0]
        ck_av_v = vB
        ck_dn_t = 0
        ck_dn_s = sB
        ck_rb_t = 0
        ck_rb_v = vB
        ck_fold_v = vB
        ck_fold_s = sB
        ck_at_s = 0
        ck_at_v = 0

        em.wait("s", "v", ck_scols)

        def load_head_y(h):
            bh = h % 2
            ct, hb = h // 2, h % 2
            em.wait("y", "t", ck_kv_free_t[bh])
            em.wait("y", "dstg", ck_A_stg)
            em.wait("y", "dvsp0", ck_A_vsp[0])
            em.wait("y", "dxnp", ck_A_xnp)
            sem = f"dkv{bh}"
            kin = r3(kscr_hi[:, ct, hb * 64:(hb + 1) * 64, :])
            em.dma("y", krh[bh][0:64, :].rearrange("p (a j) -> p a j", a=8), kin, sem)
            em.dma("y", krh[bh][64:128, :].rearrange("p (a j) -> p a j", a=8), kin, sem)
            ck = em.dma("y", qpk[bh][0:64, :].rearrange("p (a j) -> p a j", a=4),
                        r3(qscr_hi[:, ct, hb * 64:(hb + 1) * 64, :]), sem)
            return ck

        def load_head_s(h):
            bh = h % 2
            ct, hb = h // 2, h % 2
            em.wait("s", "t", ck_kv_free_t[bh])
            em.wait("s", "dstg", ck_A_stg)
            em.wait("s", "dvsp0", ck_A_vsp[0])
            em.wait("s", "dxnp", ck_A_xnp)
            sem2 = f"dkw{bh}"
            kil = r3(kscr_lo[:, ct, hb * 64:(hb + 1) * 64, :])
            em.dma("s", krl[bh][0:64, :].rearrange("p (a j) -> p a j", a=8), kil, sem2)
            em.dma("s", krl[bh][64:128, :].rearrange("p (a j) -> p a j", a=8), kil, sem2)
            em.dma("s", qpk[bh][64:128, :].rearrange("p (a j) -> p a j", a=4),
                   r3(qscr_lo[:, ct, hb * 64:(hb + 1) * 64, :]), sem2)
            vv = vpk[bh][:].rearrange("p (c st x d) -> p c st x d", c=8, st=4, x=2, d=64)
            em.dma("s", vv[:, :, :, 0, :],
                   vscr_hi[:, :, :, h * 64:(h + 1) * 64].transpose([2, 0, 1, 3]), sem2)
            ck_kw_ld[bh] = em.dma("s", vv[:, :, :, 1, :],
                                  vscr_lo[:, :, :, h * 64:(h + 1) * 64].transpose([2, 0, 1, 3]), sem2)

        pending_tail = [None]
        ck_rb_t = 0
        ck_kw_ld = [0, 0]
        ck_kv_ld[0] = load_head_y(0)
        load_head_s(0)
        for h in range(HH):
            bh = h % 2
            ct, hb = h // 2, h % 2
            if h + 1 < HH:
                ck_kv_ld[(h + 1) % 2] = load_head_y(h + 1)
            for qc in range(4):
                slot_i = h * 4 + qc
                if slot_i < 40:
                    quant2_tile(slot_i)

                def emit_av(g):
                    pr = g % 4
                    em.wait("t", "v", ck_sgrp_v[pr])
                    if g == 0:
                        em.wait("t", "v", ck_av_v)
                        em.wait("t", "s", ck_dn_s)
                    for kt in (2 * g, 2 * g + 1):
                        ah = (kt % 2)
                        em.do("t", lambda e, kt=kt, pr=pr, ah=ah, bh=bh: e.matmul(
                            ps_av[:], vpk[bh][:, kt * 128:(kt + 1) * 128],
                            ahi[pr][:, ah * 512:(ah + 1) * 512],
                            start=(kt == 0), stop=False), inc=None)
                        em.do("t", lambda e, kt=kt, pr=pr, ah=ah, bh=bh: e.matmul(
                            ps_av[:], vpk[bh][:, kt * 128:(kt + 1) * 128],
                            alo[pr][:, ah * 512:(ah + 1) * 512],
                            start=False, stop=(kt == 31)), inc=None)
                        em.do("t", lambda e, kt=kt, pr=pr, ah=ah: e.matmul(
                            ps_x[0:1, :], onesb[:], ahi[pr][:, ah * 512:(ah + 1) * 512],
                            start=(kt == 0), stop=False), inc=None)
                        em.do("t", lambda e, kt=kt, pr=pr, ah=ah: e.matmul(
                            ps_x[0:1, :], onesb[:], alo[pr][:, ah * 512:(ah + 1) * 512],
                            start=False, stop=(kt == 31)),
                            inc=("t" if kt == 2 * g + 1 else None))
                    ck_ab_t[pr] = em.cnt["t"]

                sseq = [0, 1, 0, 1, 2, 0, 1, 2, 0, 1, 2, 0, 1, 2, 0, 1]
                for g in range(16):
                    bk = sseq[g]
                    sg = [ps_wa, ps_wb, ps_wc][bk]
                    pr = g % 4
                    em.wait("t", f"dkv{bh}", ck_kv_ld[bh])
                    em.wait("t", f"dkw{bh}", ck_kw_ld[bh])
                    em.wait("t", "s", ck_sgrp_s[bk])
                    if g == 2 and pending_tail[0] is not None:
                        pending_tail[0]()
                        pending_tail[0] = None
                    if g == 2 and qc == 0 and h + 1 < HH:
                        load_head_s(h + 1)
                    if g == 4:
                        em.wait("t", "v", ck_fold_v)
                        em.wait("t", "v", ck_rb_v)
                    for kt in (2 * g, 2 * g + 1):
                        half = sg[:, (kt % 2) * 512:(kt % 2 + 1) * 512]
                        em.do("t", lambda e, kt=kt, half=half, bh=bh, qc=qc: e.matmul(
                            half, krh[bh][:, kt * 128:(kt + 1) * 128],
                            qpk[bh][:, qc * 512:(qc + 1) * 512], start=True, stop=False), inc=None)
                        em.do("t", lambda e, kt=kt, half=half, bh=bh, qc=qc: e.matmul(
                            half, krl[bh][:, kt * 128:(kt + 1) * 128],
                            qpk[bh][:, qc * 512:(qc + 1) * 512], start=False, stop=True),
                            inc=("t" if kt == 2 * g + 1 else None))
                    ck_smm = em.cnt["t"]
                    em.wait("s", "t", ck_smm)
                    em.wait("s", "t", ck_ab_t[pr])
                    em.wait("s", "v", ck_sgrp_v[pr])
                    em.do("s", lambda e, sg=sg, pr=pr, h=h: e.activation(
                        out=ahi[pr][:], in_=sg[:], func=AF.Exp,
                        scale=scols[:, h:h + 1], bias=nscols[:, h:h + 1]), inc="s")
                    em.do("s", lambda e, sg=sg, pr=pr, h=h: e.activation(
                        out=a32[pr][:], in_=sg[:], func=AF.Exp,
                        scale=scols[:, h:h + 1], bias=nscols[:, h:h + 1]), inc="s")
                    ck_sgrp_s[bk] = em.cnt["s"]
                    ck_exp_s[pr] = em.cnt["s"]
                    em.wait("v", "s", ck_exp_s[pr])
                    em.do("v", lambda e, pr=pr: e.scalar_tensor_tensor(
                        out=alo[pr][:], in0=a32[pr][:], scalar=1.0, in1=ahi[pr][:],
                        op0=ALU.mult, op1=ALU.subtract), inc="v")
                    ck_sgrp_v[pr] = em.cnt["v"]
                    if g >= 2:
                        emit_av(g - 2)
                emit_av(14)
                emit_av(15)
                ck_dn_t = em.cnt["t"]
                em.wait("v", "t", ck_dn_t)
                em.wait("v", "t", ck_rb_t)
                em.do("v", lambda e: e.tensor_copy(out=av32[:], in_=ps_av[:]), inc="v")
                ck_av_v = em.cnt["v"]
                em.wait("s", "t", ck_dn_t)
                em.do("s", lambda e: e.activation(out=lnrow[:], in_=ps_x[0:1, :], func=AF.Ln), inc="s")
                em.selfw("s")
                em.do("s", lambda e: e.activation(out=rdrow[:], in_=lnrow[:], func=AF.Exp, scale=-1.0), inc="s")
                ck_dn_s = em.cnt["s"]
                def make_tail(hb=hb, ct=ct, qc=qc):
                    def tail():
                        nonlocal ck_rb_t, ck_rb_v, ck_fold_v, ck_at_s, ck_fold_s, ck_at_v
                        em.wait("t", "v", ck_av_v)
                        em.wait("t", "v", ck_fold_v)
                        em.wait("t", "s", ck_fold_s)
                        em.do("t", lambda e: e.matmul(ps_wc[hb * 64:(hb + 1) * 64, 0:512], Fc, av32[:],
                                                      start=True, stop=True), inc="t")
                        em.wait("t", "s", ck_dn_s)
                        em.wait("t", "v", ck_rb_v)
                        em.do("t", lambda e: e.matmul(ps_wc[hb * 64:(hb + 1) * 64, 512:1024],
                                                      ones_r[0:1, 0:64], rdrow[:],
                                                      start=True, stop=True), inc="t")
                        ck_rb_t = em.cnt["t"]
                        em.wait("v", "t", ck_rb_t)
                        em.wait("v", "s", ck_at_s)
                        em.selfw("v")
                        em.do("v", lambda e: e.tensor_copy(out=rdbc[hb * 64:(hb + 1) * 64, :],
                                                           in_=ps_wc[hb * 64:(hb + 1) * 64, 512:1024]), inc="v")
                        em.selfw("v")
                        em.do("v", lambda e: e.tensor_mul(out=prod32[hb * 64:(hb + 1) * 64, :],
                                                          in0=ps_wc[hb * 64:(hb + 1) * 64, 0:512],
                                                          in1=rdbc[hb * 64:(hb + 1) * 64, :]), inc="v")
                        ck_rb_v = em.cnt["v"]
                        ck_fold_v = em.cnt["v"]
                        em.wait("s", "v", ck_rb_v)
                        off = ct * NQ + qc * 512
                        em.do("s", lambda e: e.activation(
                            out=attnT_hi[hb * 64:(hb + 1) * 64, off:off + 512],
                            in_=prod32[hb * 64:(hb + 1) * 64, :], func=AF.Copy), inc="s")
                        ck_at_s = em.cnt["s"]
                        ck_fold_s = em.cnt["s"]
                        em.wait("v", "s", ck_at_s)
                        em.do("v", lambda e: e.scalar_tensor_tensor(
                            out=attnT_lo[hb * 64:(hb + 1) * 64, off:off + 512],
                            in0=prod32[hb * 64:(hb + 1) * 64, :], scalar=1.0,
                            in1=attnT_hi[hb * 64:(hb + 1) * 64, off:off + 512],
                            op0=ALU.mult, op1=ALU.subtract), inc="v")
                        ck_at_v = em.cnt["v"]
                    return tail

                pending_tail[0] = make_tail()
            ck_kv_free_t[bh] = em.cnt["t"]

        if pending_tail[0] is not None:
            pending_tail[0]()
            pending_tail[0] = None
        ck_wq2 = em.cnt["dq"]
        ck_attn_v = em.cnt["v"]
        ck_attn_s = em.cnt["s"]
        if debug:
            em.wait("y", "v", ck_attn_v)
            em.wait("y", "s", ck_attn_s)
            em.dma("y", dbg["at_hi"], attnT_hi[:], "dstg")
            em.dma("y", dbg["at_lo"], attnT_lo[:], "dstg")

        tC = em.cnt["t"]
        em.wait("t", "v", ck_attn_v)
        em.wait("t", "s", ck_attn_s)
        em.wait("y", "t", tC)
        em.wait("y", "v", ck_attn_v)
        em.wait("y", "s", ck_attn_s)
        em.wait("s", "t", tC)
        em.wait("s", "v", ck_attn_v)
        em.wait("s", "dc", ck_c0)
        vC = em.cnt["v"]
        sC = em.cnt["s"]
        ck_cx_ld = [0, 0]
        ck_cx_free_t = [0, 0]
        ck_xo_ld = [0, 0]
        ck_xo_free_v = [0, 0]
        ck_w1_ld = [0, 0, 0, 0]
        ck_w1_free_t = [0, 0, 0, 0]
        ck_w2_ld = [0, 0, 0, 0]
        ck_w2_free_t = [0, 0, 0, 0]
        ck_h_ps = [vC, vC, vC, vC]
        ck_h_ps_s = [sC, sC, sC, sC]
        ck_out_ps_v = [vC, vC]
        outreg = [[ps_wc[:, 0:512], ps_wc[:, 512:1024]], [ps_av[:, 0:512], ps_x[:, 0:512]]]
        ck_out_st = [0, 0]
        ck_wob_ld = [0, 0]
        ck_wob_free_t = [0, 0]

        def load_cxn(cc):
            bc = cc % 2
            oc, sub = cc // 2, cc % 2
            em.wait("y", "t", ck_cx_free_t[bc])
            em.dma("y", cxnh[bc][:].rearrange("p (a j) -> p a j", a=8),
                   r3(xnscr_hi[oc][:, :, sub * CC:(sub + 1) * CC]), f"dcx{bc}")
            ck_cx_ld[bc] = em.dma("y", cxnl[bc][:].rearrange("p (a j) -> p a j", a=8),
                                  r3(xnscr_lo[oc][:, :, sub * CC:(sub + 1) * CC]), f"dcx{bc}")

        def load_xo(cc):
            bc = cc % 2
            em.wait("s", "v", ck_xo_free_v[bc])
            ck_xo_ld[bc] = em.dma("s", xob[bc][:].rearrange("p (a j) -> p a j", a=2),
                                  r3(x_own[cc * CC:(cc + 1) * CC, :].rearrange("(a p) j -> a p j", p=128)), f"dxo{bc}")

        def load_w1(ft, cc):
            bw = ft % 4
            em.wait("s", "t", ck_w1_free_t[bw])
            if cc == 0 and ft < 4:
                em.wait("s", "dq", ck_wq2)
            ck_w1_ld[bw] = em.dma("s", wq1b[bw][:].rearrange("p (a j) -> p a j", a=8),
                                  r3(wq1s[:, :, ft * 128:(ft + 1) * 128]), f"dw1{bw}")

        def load_w2(fc, cc):
            bw = fc % 4
            em.wait("s", "t", ck_w2_free_t[bw])
            if cc == 0 and fc < 4:
                em.wait("s", "dq", ck_wq2)
            ck_w2_ld[bw] = em.dma("s", wq2b[bw][:], wq2s[fc], f"dw2{bw}")

        def load_wob(ic, cc, mt):
            bo = ic % 2
            em.wait("s", "t", ck_wob_free_t[bo])
            if cc == 0 and mt == 0 and ic < 2:
                em.wait("s", "dwsp0", ck_wsp_tot[0])
                em.wait("s", "dwsp1", ck_wsp_tot[1])
            em.dma("s", wob_h[bo][:], woscr[0, ic], f"dwob{bo}")
            ck_wob_ld[bo] = em.dma("s", wob_l[bo][:], woscr[1, ic], f"dwob{bo}")

        load_cxn(0)
        load_xo(0)
        for cc in range(NCC):
            bc = cc % 2
            if cc + 1 < NCC:
                load_cxn(cc + 1)
                load_xo(cc + 1)
            for pre in range(3):
                load_w1(pre, cc)
            hslots = [ps_wa[:, 0:512], ps_wa[:, 512:1024], ps_wb[:, 0:512], ps_wb[:, 512:1024]]
            for ft in range(32):
                bw = ft % 4
                hp4 = ft % 4
                hps = hslots[hp4]
                em.wait("t", f"dw1{bw}", ck_w1_ld[bw])
                em.wait("t", f"dcx{bc}", ck_cx_ld[bc])
                em.wait("t", "v", ck_h_ps[hp4])
                em.wait("t", "s", ck_h_ps_s[hp4])
                seq = [(dc, xsb) for dc in range(8) for xsb in (cxnh[bc], cxnl[bc])]
                for j, (dc, xsb) in enumerate(seq):
                    last = j == len(seq) - 1
                    em.do("t", lambda e, dc=dc, xsb=xsb, bw=bw, hps=hps, j=j, last=last: e.matmul(
                        hps[:, 0:CC], wq1b[bw][:, dc * 128:(dc + 1) * 128],
                        xsb[:, dc * CC:(dc + 1) * CC], start=(j == 0), stop=last),
                        inc=("t" if last else None))
                ck_mm = em.cnt["t"]
                ck_w1_free_t[bw] = ck_mm
                if ft + 3 < 32:
                    load_w1(ft + 3, cc)
                em.wait("s", "t", ck_mm)
                em.wait("s", "v", ck_h_ps[hp4])
                em.do("s", lambda e, ft=ft, hps=hps: e.activation(
                    out=hh_sb[:, ft * CC:(ft + 1) * CC], in_=hps[:, 0:CC],
                    func=AF.Relu, bias=b1c[:, ft:ft + 1]), inc="s")
                em.do("s", lambda e, ft=ft, hps=hps, hp4=hp4: e.activation(
                    out=h32b[:, hp4 * CC:(hp4 + 1) * CC], in_=hps[:, 0:CC],
                    func=AF.Relu, bias=b1c[:, ft:ft + 1]), inc="s")
                ck_h_ps_s[hp4] = em.cnt["s"]
                em.wait("v", "s", ck_h_ps_s[hp4])
                em.do("v", lambda e, ft=ft, hp4=hp4: e.scalar_tensor_tensor(
                    out=hl_sb[:, ft * CC:(ft + 1) * CC], in0=h32b[:, hp4 * CC:(hp4 + 1) * CC],
                    scalar=1.0, in1=hh_sb[:, ft * CC:(ft + 1) * CC],
                    op0=ALU.mult, op1=ALU.subtract), inc="v")
                ck_h_ps[hp4] = em.cnt["v"]
            ck_h_done_v = em.cnt["v"]

            for mt in range(2):
                em.wait("t", "v", ck_out_ps_v[mt])
                load_wob(0, cc, mt)
                for ic in range(8):
                    bo = ic % 2
                    if ic + 1 < 8:
                        load_wob(ic + 1, cc, mt)
                    em.wait("t", f"dwob{bo}", ck_wob_ld[bo])
                    lhi = attnT_hi[:, ic * NQ + cc * CC + mt * 128: ic * NQ + cc * CC + (mt + 1) * 128]
                    llo = attnT_lo[:, ic * NQ + cc * CC + mt * 128: ic * NQ + cc * CC + (mt + 1) * 128]
                    for nh in range(2):
                        rhi = wob_h[bo][:, nh * 512:(nh + 1) * 512]
                        rlo = wob_l[bo][:, nh * 512:(nh + 1) * 512]
                        for (lt, rt) in ((lhi, rhi), (llo, rhi), (lhi, rlo)):
                            last = (nh == 1 and rt is rlo)
                            em.do("t", lambda e, lt=lt, rt=rt, mt=mt, nh=nh, ic=ic: e.matmul(
                                outreg[mt][nh], lt, rt,
                                start=(ic == 0 and lt is lhi and rt is rhi), stop=False),
                                inc=("t" if last else None))
                    ck_wob_free_t[bo] = em.cnt["t"]
                for nh in range(2):
                    em.do("t", lambda e, mt=mt, nh=nh: e.matmul(
                        outreg[mt][nh], ones_r,
                        btot[0:1, nh * 512:(nh + 1) * 512], start=False, stop=False), inc=None)
            em.wait("t", "v", ck_h_done_v)
            for pre in range(3):
                load_w2(pre, cc)
            for fc in range(32):
                bw = fc % 4
                em.wait("t", f"dw2{bw}", ck_w2_ld[bw])
                for mt in range(2):
                    for nh in range(2):
                        for hsb in (hh_sb, hl_sb):
                            stop = (fc == 31 and hsb is hl_sb)
                            fcl = (mt == 1 and nh == 1 and hsb is hl_sb)
                            em.do("t", lambda e, fc=fc, mt=mt, nh=nh, hsb=hsb, bw=bw, stop=stop: e.matmul(
                                outreg[mt][nh],
                                hsb[:, fc * CC + mt * 128: fc * CC + (mt + 1) * 128],
                                wq2b[bw][:, nh * 512:(nh + 1) * 512],
                                start=False, stop=stop),
                                inc=("t" if fcl else None))
                ck_w2_free_t[bw] = em.cnt["t"]
                if fc + 3 < 32:
                    load_w2(fc + 3, cc)
            ck_ffn2 = em.cnt["t"]
            em.wait("v", "t", ck_ffn2)
            em.wait("v", f"dxo{bc}", ck_xo_ld[bc])
            em.wait("v", f"dout{bc}", ck_out_st[bc])
            for mt in range(2):
                for nh in range(2):
                    em.do("v", lambda e, mt=mt, nh=nh, bc=bc: e.tensor_add(
                        out=outb[bc][:, mt * D + nh * 512: mt * D + (nh + 1) * 512],
                        in0=xob[bc][:, mt * D + nh * 512: mt * D + (nh + 1) * 512],
                        in1=outreg[mt][nh]), inc="v")
                ck_out_ps_v[mt] = em.cnt["v"]
            ck_outb_v = em.cnt["v"]
            ck_xo_free_v[bc] = ck_outb_v
            em.wait("s", "v", ck_outb_v)
            ck_out_st[bc] = em.dma("s", y_out[cc * CC:(cc + 1) * CC, :].rearrange("(a p) j -> a p j", p=128).transpose([1, 0, 2]),
                                   outb[bc][:].rearrange("p (a j) -> p a j", a=2), f"dout{bc}")
            ck_cx_free_t[bc] = em.cnt["t"]

        for snm in ["dout0", "dout1", "dg", "dq", "dstg", "dvsp0", "dxnp",
                    "dx0", "dx1", "dwb0", "dwb1", "dwb2", "dkv0", "dkv1", "dkw0", "dkw1", "dcx0", "dcx1",
                    "dxo0", "dxo1", "dw10", "dw11", "dw20", "dw21", "dwob0", "dwob1", "dc", "dwsp0", "dwsp1"]:
            em.wait("y", snm, em.cnt[snm])

    with nc.Block() as block:
        @block.sync
        def _(eng):
            walk(Em("y", eng, sems))

        @block.tensor
        def _(eng):
            walk(Em("t", eng, sems))

        @block.scalar
        def _(eng):
            walk(Em("s", eng, sems))

        @block.vector
        def _(eng):
            walk(Em("v", eng, sems))

        @block.gpsimd
        def _(eng):
            walk(Em("g", eng, sems))

    return nc


def _prep_inputs(L0_attn_x, L0_attn_w_qkv, L0_attn_w_out, L0_attn_b_out,
                 L0_attn_logit_scale, L0_ffn_W1, L0_ffn_b1, L0_ffn_W2, L0_ffn_b2):
    x = np.ascontiguousarray(np.asarray(L0_attn_x, dtype=np.float32))
    w_qkv = np.ascontiguousarray(np.asarray(L0_attn_w_qkv, dtype=np.float32))
    w_out = np.ascontiguousarray(np.asarray(L0_attn_w_out, dtype=np.float32))
    b_out = np.asarray(L0_attn_b_out, dtype=np.float32)
    lsc = np.asarray(L0_attn_logit_scale, dtype=np.float32).reshape(1, HH)
    W1 = np.asarray(L0_ffn_W1, dtype=np.float32)
    b1 = np.asarray(L0_ffn_b1, dtype=np.float32)
    W2 = np.asarray(L0_ffn_W2, dtype=np.float32)
    b2 = np.asarray(L0_ffn_b2, dtype=np.float32)
    W1T = np.ascontiguousarray(W1.T)
    W2T = np.ascontiguousarray(W2.T)
    b1c = np.ascontiguousarray(b1.reshape(32, 128).T)
    consts, c16 = _host_consts()
    in_maps = []
    for core in range(8):
        b, half = core // 2, core % 2
        own = x[b, half * NQ:(half + 1) * NQ]
        other = x[b, (1 - half) * NQ:(2 - half) * NQ]
        xTc = np.ascontiguousarray(np.concatenate([own, other], axis=0).T)
        in_maps.append({
            "xT": xTc,
            "x_own": np.ascontiguousarray(own),
            "w_qkv": w_qkv,
            "w_out": w_out,
            "W1T": W1T,
            "W2T": W2T,
            "b1c": b1c,
            "brow_out": b_out.reshape(1, D),
            "brow2": b2.reshape(1, D),
            "ls": lsc,
            "consts_in": consts,
            "consts16_in": c16,
        })
    return in_maps


def kernel(L0_attn_x, L0_attn_w_qkv, L0_attn_w_out, L0_attn_b_out,
           L0_attn_logit_scale, L0_ffn_W1, L0_ffn_b1, L0_ffn_W2, L0_ffn_b2):
    in_maps = _prep_inputs(L0_attn_x, L0_attn_w_qkv, L0_attn_w_out, L0_attn_b_out,
                           L0_attn_logit_scale, L0_ffn_W1, L0_ffn_b1, L0_ffn_W2, L0_ffn_b2)
    nc = build_nc(debug=False)
    res = run_bass_kernel_spmd(nc, in_maps, core_ids=list(range(8)))
    out = np.empty((B, N, D), dtype=np.float32)
    for core in range(8):
        b, half = core // 2, core % 2
        out[b, half * NQ:(half + 1) * NQ] = res.results[core]["y"]
    return out


# version 17
# speedup vs baseline: 1.0146x; 1.0146x over previous
import math
from contextlib import ExitStack

import numpy as np
import concourse.bass as bass
import concourse.mybir as mybir
from concourse.bass_utils import run_bass_kernel_spmd

F32 = mybir.dt.float32
BF16 = mybir.dt.bfloat16
AF = mybir.ActivationFunctionType
ALU = mybir.AluOpType
AX = mybir.AxisListType

B, N, D, HH, DH = 4, 4096, 1024, 16, 64
NQ = 2048
CH = 512
NCH = N // CH
OC = NQ // CH
CC = 256
NCC = NQ // CC
FF = 4096
LOG100 = float(np.float32(math.log(100.0)))
EPS_LN = 1e-5
EPS_Q = 1e-5
EPS_NRM = 1e-12


def _host_consts():
    c = np.zeros((128, 195), dtype=np.float32)
    c[:, 194] = EPS_LN
    c[:, 0] = 1.0
    c[:, 1] = 1.0 / 1024.0
    for m in range(64):
        c[m, 2 + m] = 1.0
        c[m + 64, 2 + m] = 1.0
    for t in range(8):
        c[0:64, 66 + 16 * t + 2 * t] = 1.0
        c[64:128, 66 + 16 * t + 2 * t + 1] = 1.0
    c16 = np.zeros((16, 1152), dtype=np.float32)
    for t in range(8):
        c16[2 * t, 128 * t + 0: 128 * t + 64] = 1.0
        c16[2 * t + 1, 128 * t + 64: 128 * t + 128] = 1.0
    c16[0, 1024:1152] = 1.0
    return c, c16


class Em:

    def __init__(self, active, eng, sems):
        self.active = active
        self.eng = eng
        self.sems = sems
        self.cnt = {k: 0 for k in sems}

    def do(self, eng, fn, inc=None, n=1):
        if eng == self.active:
            inst = fn(self.eng)
            if inc is not None:
                inst.then_inc(self.sems[inc], n)
        if inc is not None:
            self.cnt[inc] += n
        return self.cnt[inc] if inc is not None else None

    def dma(self, eng, out, in_, inc):
        return self.do(eng, lambda e: e.dma_start(out=out, in_=in_), inc=inc, n=16)

    def wait(self, eng, sem, val):
        if self.active == eng and val > 0:
            self.eng.wait_ge(self.sems[sem], val)

    def drain(self, eng):
        if self.active == eng:
            self.eng.drain()

    def selfw(self, eng):
        sem = {"t": "t", "s": "s", "v": "v", "g": "g"}[eng]
        self.wait(eng, sem, self.cnt[sem])


def build_nc(debug=False):
    nc = bass.Bass("TRN2", target_bir_lowering=False, debug=False)

    xT = nc.dram_tensor("xT", [D, N], F32, kind="ExternalInput").ap()
    x_own = nc.dram_tensor("x_own", [NQ, D], F32, kind="ExternalInput").ap()
    w_qkv = nc.dram_tensor("w_qkv", [D, 3 * D], F32, kind="ExternalInput").ap()
    w_out = nc.dram_tensor("w_out", [D, D], F32, kind="ExternalInput").ap()
    W1T = nc.dram_tensor("W1T", [D, FF], F32, kind="ExternalInput").ap()
    W2T = nc.dram_tensor("W2T", [FF, D], F32, kind="ExternalInput").ap()
    b1c_in = nc.dram_tensor("b1c", [128, 32], F32, kind="ExternalInput").ap()
    brow_out = nc.dram_tensor("brow_out", [1, D], F32, kind="ExternalInput").ap()
    brow2 = nc.dram_tensor("brow2", [1, D], F32, kind="ExternalInput").ap()
    ls_in = nc.dram_tensor("ls", [1, HH], F32, kind="ExternalInput").ap()
    consts_in = nc.dram_tensor("consts_in", [128, 195], F32, kind="ExternalInput").ap()
    c16_in = nc.dram_tensor("consts16_in", [16, 1152], F32, kind="ExternalInput").ap()
    y_out = nc.dram_tensor("y", [NQ, D], F32, kind="ExternalOutput").ap()

    wscr_hi = nc.dram_tensor("wscr_hi", [8, 128, 3 * D], BF16, kind="Internal").ap()
    wscr_lo = nc.dram_tensor("wscr_lo", [8, 128, 3 * D], BF16, kind="Internal").ap()
    woscr = nc.dram_tensor("woscr", [2, 8, 128, D], BF16, kind="Internal").ap()
    wq1s = nc.dram_tensor("wq1s", [8, 128, FF], BF16, kind="Internal").ap()
    wq2s = nc.dram_tensor("wq2s", [32, 128, D], BF16, kind="Internal").ap()
    xnscr_hi = nc.dram_tensor("xnscr_hi", [OC, 8, 128, CH], BF16, kind="Internal").ap()
    xnscr_lo = nc.dram_tensor("xnscr_lo", [OC, 8, 128, CH], BF16, kind="Internal").ap()
    kscr_hi = nc.dram_tensor("kscr_hi", [NCH, 8, 128, CH], BF16, kind="Internal").ap()
    kscr_lo = nc.dram_tensor("kscr_lo", [NCH, 8, 128, CH], BF16, kind="Internal").ap()
    qscr_hi = nc.dram_tensor("qscr_hi", [OC, 8, 128, CH], BF16, kind="Internal").ap()
    qscr_lo = nc.dram_tensor("qscr_lo", [OC, 8, 128, CH], BF16, kind="Internal").ap()
    vscr_hi = nc.dram_tensor("vscr_hi", [NCH, 4, 128, D], BF16, kind="Internal").ap()
    vscr_lo = nc.dram_tensor("vscr_lo", [NCH, 4, 128, D], BF16, kind="Internal").ap()

    dbg = {}
    if debug:
        for nm, shp, dt in [("xn_hi", [OC, 8, 128, CH], BF16), ("xn_lo", [OC, 8, 128, CH], BF16),
                            ("q_hi", [OC, 8, 128, CH], BF16), ("q_lo", [OC, 8, 128, CH], BF16),
                            ("k_hi", [NCH, 8, 128, CH], BF16),
                            ("v_hi", [NCH, 4, 128, D], BF16), ("v_lo", [NCH, 4, 128, D], BF16),
                            ("at_hi", [128, 8 * NQ], BF16), ("at_lo", [128, 8 * NQ], BF16),
                            ("wq1d", [8, 128, FF], BF16), ("wq2d", [32, 128, D], BF16)]:
            dbg[nm] = nc.dram_tensor("dbg_" + nm, shp, dt, kind="ExternalOutput").ap()

    SBTOP = 229312
    R0_SZ = 65536
    R1_SZ = 110592
    R0B = SBTOP - R0_SZ
    R1B = R0B - R1_SZ
    RQ_SZ = 24576
    RQB = R1B - RQ_SZ

    class Arena:
        def __init__(self, base, limit):
            self.off = base
            self.limit = limit

        def take(self, name, shape, dtype):
            nb = int(np.prod(shape[1:])) * mybir.dt.size(dtype)
            t = nc.alloc_sbuf_tensor_at(name, shape, dtype, offset=self.off)
            self.off += (nb + 63) // 64 * 64
            assert self.off <= self.limit, (name, self.off, self.limit)
            return t

    SB = nc.alloc_sbuf_tensor
    consts = SB("consts", [128, 195], F32)
    c16 = SB("c16", [16, 1152], F32)
    b1c = SB("b1cs", [128, 32], F32)
    btot = SB("btot", [1, D], F32)
    ls = SB("lss", [1, HH], F32)
    srow = SB("srow", [1, HH], F32)
    scols = SB("scols", [128, HH], F32)
    nscols = SB("nscols", [128, HH], F32)
    onesb = SB("onesb", [128, 1], BF16)
    rcol1 = SB("rcol1", [128, 1], F32)
    rcol2 = SB("rcol2", [128, 1], F32)
    grow = SB("grow", [1, 1], F32)
    g2row = SB("g2row", [1, 1], F32)
    gacc = SB("gacc", [128, 42], F32)
    wt = nc.alloc_sbuf_tensor_at("wt", [128, FF], F32, offset=RQB)
    wqt = nc.alloc_sbuf_tensor_at("wqt", [128, FF], BF16, offset=RQB + 16384)

    r0a = Arena(R0B, SBTOP)
    attnT_hi = r0a.take("attnT_hi", [128, 8 * NQ], BF16)
    attnT_lo = r0a.take("attnT_lo", [128, 8 * NQ], BF16)
    r0b = Arena(R0B, SBTOP)
    xt = [r0b.take(f"xt{i}", [128, 8 * CH], F32) for i in range(2)]
    wbh = [r0b.take(f"wbh{i}", [128, 8 * CH], BF16) for i in range(2)]
    wbl = [r0b.take(f"wbl{i}", [128, 8 * CH], BF16) for i in range(2)]

    ra = Arena(R1B, R0B)
    sqb = ra.take("sqb", [128, 5 * CH], F32)
    t1b2 = [ra.take(f"t1b{i}", [128, CH], F32) for i in range(2)]
    xn32b2 = [ra.take(f"xn32b{i}", [128, CH], F32) for i in range(2)]
    xnh2 = [ra.take(f"xnh{i}", [128, 8 * CH], BF16) for i in range(2)]
    xnl2 = [ra.take(f"xnl{i}", [128, 8 * CH], BF16) for i in range(2)]
    p32 = [ra.take(f"p32_{i}", [128, CH], F32) for i in range(12)]
    kstg_h = ra.take("kstg_h", [128, 2 * CH], BF16)
    kstg_l = ra.take("kstg_l", [128, 2 * CH], BF16)
    vstg_h = ra.take("vstg_h", [128, D], BF16)
    vstg_l = ra.take("vstg_l", [128, D], BF16)
    wbh3 = ra.take("wbh3", [128, 8 * CH], BF16)
    wbl3 = ra.take("wbl3", [128, 8 * CH], BF16)
    mrow = ra.take("mrow", [1, CH], F32)
    m2row = ra.take("m2row", [1, CH], F32)
    rstdrow = ra.take("rstdrow", [1, CH], F32)
    nrmall = ra.take("nrmall", [16, CH], F32)
    rnall = ra.take("rnall", [16, CH], F32)

    rb = Arena(R1B, R0B)
    krh = [rb.take(f"krh{i}", [128, N], BF16) for i in range(2)]
    krl = [rb.take(f"krl{i}", [128, N], BF16) for i in range(2)]
    vpk = [rb.take(f"vpk{i}", [128, 32 * 128], BF16) for i in range(2)]
    qpk = [rb.take(f"qpk{i}", [128, NQ], BF16) for i in range(2)]
    ahi = [rb.take(f"ahi{i}", [128, 1024], BF16) for i in range(4)]
    alo = [rb.take(f"alo{i}", [128, 1024], BF16) for i in range(4)]
    a32 = [rb.take(f"a32_{i}", [128, 1024], F32) for i in range(4)]
    av32 = rb.take("av32", [128, 512], F32)
    rdbc = rb.take("rdbc", [128, 512], F32)
    prod32 = rb.take("prod32", [128, 512], F32)
    lnrow = rb.take("lnrow", [1, 512], F32)
    rdrow = rb.take("rdrow", [1, 512], F32)

    rc_ = Arena(R1B, R0B)
    cxnh = [rc_.take(f"cxnh{i}", [128, 8 * CC], BF16) for i in range(2)]
    cxnl = [rc_.take(f"cxnl{i}", [128, 8 * CC], BF16) for i in range(2)]
    hh_sb = rc_.take("hh_sb", [128, 32 * CC], BF16)
    hl_sb = rc_.take("hl_sb", [128, 32 * CC], BF16)
    h32b = rc_.take("h32b", [128, 4 * CC], F32)
    wq1b = [rc_.take(f"wq1b{i}", [128, 8 * 128], BF16) for i in range(4)]
    wq2b = [rc_.take(f"wq2b{i}", [128, D], BF16) for i in range(4)]
    xob = [rc_.take(f"xob{i}", [128, 2 * D], F32) for i in range(2)]
    outb = [rc_.take(f"outb{i}", [128, 2 * D], F32) for i in range(2)]
    wob_h = [rc_.take(f"wob_h{i}", [128, D], BF16) for i in range(2)]
    wob_l = [rc_.take(f"wob_l{i}", [128, D], BF16) for i in range(2)]

    PS = nc.alloc_psum_tensor
    ps_wa = PS("ps_wa", [128, 1024], F32)
    ps_wb = PS("ps_wb", [128, 1024], F32)
    ps_wc = PS("ps_wc", [128, 1024], F32)
    ps_av = PS("ps_av", [128, 512], F32)
    ps_x = PS("ps_x", [128, 512], F32)

    est = ExitStack()
    sems = {name: est.enter_context(nc.semaphore(name))
            for name in ["dc", "dx0", "dx1", "dwsp0", "dwsp1", "dwb0", "dwb1", "dwb2", "dstg", "dvsp0",
                         "dxnp", "dkv0", "dkv1", "dkw0", "dkw1", "dcx0", "dcx1", "dxo0", "dxo1",
                         "dw10", "dw11", "dw12", "dw13", "dw20", "dw21", "dw22", "dw23", "dwob0", "dwob1", "dout0", "dout1",
                         "dg", "dq", "t", "s", "v", "g"]}

    onescol = consts[:, 0:1]
    cmean = consts[:, 1:2]
    Fc = consts[:, 2:66]

    def Et(t):
        return consts[:, 66 + 16 * t: 82 + 16 * t]

    def Ft(t):
        return c16[:, 128 * t: 128 * (t + 1)]

    ones_r = c16[0:1, 1024:1152]

    def r3(ap3):
        return ap3.transpose([1, 0, 2])

    def walk(em: Em):
        qk_slots = [ps_wa[:, 0:512], ps_wa[:, 512:1024], ps_wb[:, 0:512],
                    ps_wb[:, 512:1024], ps_wc[:, 0:512]]
        slot_free = [0] * 5
        slot_free_s = [0] * 5

        for (sb, ap) in [(consts, consts_in), (c16, c16_in), (b1c, b1c_in),
                         (ls, ls_in)]:
            em.dma("y", sb[:], ap, "dc")
        ck_c0 = em.cnt["dc"]

        em.dma("y", xt[0][0:1, 0:D], brow_out, "dx0")
        ck_br = em.dma("y", xt[0][0:1, D:2 * D], brow2, "dx0")
        em.wait("v", "dx0", ck_br)
        em.do("v", lambda e: e.tensor_add(out=btot[:], in0=xt[0][0:1, 0:D], in1=xt[0][0:1, D:2 * D]), inc="v")
        ck_btot = em.cnt["v"]

        ck_w_ld = [0, 0]
        ck_w_v = [ck_btot, 0]
        ck_wspill = [0, 0]
        for dt in range(8):
            b = dt % 2
            em.wait("y", "v", ck_w_v[b])
            ck_w_ld[b] = em.dma("y", xt[b][:, 0:3 * D], w_qkv[dt * 128:(dt + 1) * 128, :], f"dx{b}")
            em.wait("v", f"dx{b}", ck_w_ld[b])
            em.wait("v", f"dwsp{b}", ck_wspill[b])
            em.do("v", lambda e, b=b: e.tensor_copy(out=wbh[b][:, 0:3 * D], in_=xt[b][:, 0:3 * D]), inc="v")
            em.selfw("v")
            em.do("v", lambda e, b=b: e.scalar_tensor_tensor(
                out=wbl[b][:, 0:3 * D], in0=xt[b][:, 0:3 * D], scalar=1.0,
                in1=wbh[b][:, 0:3 * D], op0=ALU.mult, op1=ALU.subtract), inc="v")
            ck_w_v[b] = em.cnt["v"]
            em.wait("y", "v", ck_w_v[b])
            em.dma("y", wscr_hi[dt], wbh[b][:, 0:3 * D], f"dwsp{b}")
            ck_wspill[b] = em.dma("y", wscr_lo[dt], wbl[b][:, 0:3 * D], f"dwsp{b}")

        for dt in range(8):
            b = dt % 2
            em.wait("y", "v", ck_w_v[b])
            ck_w_ld[b] = em.dma("y", xt[b][:, 0:D], w_out[dt * 128:(dt + 1) * 128, :], f"dx{b}")
            em.wait("v", f"dx{b}", ck_w_ld[b])
            em.wait("v", f"dwsp{b}", ck_wspill[b])
            em.do("v", lambda e, b=b: e.tensor_copy(out=wbh[b][:, 0:D], in_=xt[b][:, 0:D]), inc="v")
            em.selfw("v")
            em.do("v", lambda e, b=b: e.scalar_tensor_tensor(
                out=wbl[b][:, 0:D], in0=xt[b][:, 0:D], scalar=1.0,
                in1=wbh[b][:, 0:D], op0=ALU.mult, op1=ALU.subtract), inc="v")
            ck_w_v[b] = em.cnt["v"]
            em.wait("y", "v", ck_w_v[b])
            em.dma("y", woscr[0, dt], wbh[b][:, 0:D], f"dwsp{b}")
            ck_wspill[b] = em.dma("y", woscr[1, dt], wbl[b][:, 0:D], f"dwsp{b}")
        ck_wsp_tot = [em.cnt["dwsp0"], em.cnt["dwsp1"]]

        em.wait("v", "dc", ck_c0)
        em.do("v", lambda e: e.tensor_scalar_min(out=srow[:], in0=ls[:], scalar1=LOG100), inc="v")
        ck_ls = em.cnt["v"]
        em.wait("s", "v", ck_ls)
        em.do("s", lambda e: e.activation(out=srow[:], in_=srow[:], func=AF.Exp), inc="s")
        ck_s0 = em.cnt["s"]
        em.wait("t", "s", ck_s0)
        em.wait("t", "dc", ck_c0)
        for h in range(HH):
            em.do("t", lambda e, h=h: e.matmul(ps_x[:, h:h + 1], ones_r, srow[0:1, h:h + 1],
                                               start=True, stop=True),
                  inc=("t" if h == HH - 1 else None))
        ck_t0 = em.cnt["t"]
        em.wait("v", "t", ck_t0)
        em.do("v", lambda e: e.tensor_copy(out=onesb[:], in_=consts[:, 0:1]), inc="v")
        em.do("v", lambda e: e.tensor_copy(out=scols[:], in_=ps_x[:, 0:HH]), inc="v")
        em.selfw("v")
        em.do("v", lambda e: e.tensor_scalar_mul(out=nscols[:], in0=scols[:], scalar1=-1.0), inc="v")
        ck_scols = em.cnt["v"]
        ps_x_free_v = ck_scols

        gw_ck_s = [0]
        gw_ck_dg = [0]

        def gamma_tile(i):
            src_ap = W1T[i * 128:(i + 1) * 128, :] if i < 8 else W2T[(i - 8) * 128:(i - 7) * 128, :]
            width = FF if i < 8 else D
            em.wait("g", "s", gw_ck_s[0])
            gw_ck_dg[0] = em.dma("g", wt[:, 0:width], src_ap, "dg")
            em.wait("s", "dg", gw_ck_dg[0])
            em.do("s", lambda e, i=i, width=width: e.activation(
                out=wqt[:, 0:width], in_=wt[:, 0:width], func=AF.Abs,
                accum_out=gacc[:, i:i + 1]), inc="s")
            gw_ck_s[0] = em.cnt["s"]

        ck_xt_free = [em.cnt["v"], em.cnt["v"]]
        ck_xt = [0, 0]
        wb_i = [0]
        ck_wb_ld = [0, 0, 0]
        ck_wb_free_t = [0, 0, 0]
        wbhs = [wbh[0], wbh[1], wbh3]
        wbls = [wbl[0], wbl[1], wbl3]
        ck_kstg_sp = 0
        ck_vstg_sp = [0, 0]
        ck_xn_sp = 0
        ck_sqb_t = [0]
        ck_xn_v = [0, 0]
        ck_xn_done_t = [0, 0]
        ck_psz_t = 0
        ck_ss_free_s = [0, 0]
        ck_psav_v = 0
        ck_rnbcq_v = 0
        si = [0]

        def load_xT(c):
            bx = c % 2
            em.wait("y", "v", ck_xt_free[bx])
            ck_xt[bx] = em.dma("y", xt[bx][:].rearrange("p (a j) -> p a j", a=8),
                               r3(xT.rearrange("(a p) n -> a p n", p=128)[:, :, c * CH:(c + 1) * CH]),
                               f"dx{bx}")

        def header(c):
            nonlocal ck_xn_sp
            bx = c % 2
            xnh, xnl = xnh2[bx], xnl2[bx]
            em.wait("v", f"dx{bx}", ck_xt[bx])
            em.wait("v", "t", max(ck_sqb_t[0], ck_psz_t))
            em.selfw("v")
            for dc in range(8):
                dst = (sqb[:, dc * CH:(dc + 1) * CH] if dc < 5 else
                       [t1b2[0], t1b2[1], xn32b2[0]][dc - 5][:])
                em.do("v", lambda e, dc=dc, bx=bx, dst=dst: e.tensor_mul(
                    out=dst, in0=xt[bx][:, dc * CH:(dc + 1) * CH],
                    in1=xt[bx][:, dc * CH:(dc + 1) * CH]), inc=("v" if dc == 7 else None))
            sq_ck = em.cnt["v"]
            em.wait("t", f"dx{bx}", ck_xt[bx])
            em.wait("t", "v", ck_rnbcq_v)
            em.wait("t", "s", ck_ss_free_s[0])
            for dc in range(8):
                em.do("t", lambda e, dc=dc, bx=bx: e.matmul(
                    ps_x[0:1, 0:CH], cmean, xt[bx][:, dc * CH:(dc + 1) * CH],
                    start=(dc == 0), stop=(dc == 7)), inc=("t" if dc == 7 else None))
            em.wait("t", "v", sq_ck)
            em.wait("t", "v", ck_psav_v)
            for dc in range(8):
                s_ap = (sqb[:, dc * CH:(dc + 1) * CH] if dc < 5 else
                        [t1b2[0], t1b2[1], xn32b2[0]][dc - 5][:])
                em.do("t", lambda e, dc=dc, s_ap=s_ap: e.matmul(
                    ps_av[0:1, 0:CH], cmean, s_ap,
                    start=(dc == 0), stop=(dc == 7)), inc=("t" if dc == 7 else None))
            ck_sqb_t[0] = em.cnt["t"]
            ck_stats_t = em.cnt["t"]
            em.wait("v", "t", ck_stats_t)
            em.do("v", lambda e: e.tensor_copy(out=mrow[:], in_=ps_x[0:1, 0:CH]), inc="v")
            em.selfw("v")
            em.do("v", lambda e: e.tensor_mul(out=m2row[:], in0=mrow[:], in1=mrow[:]), inc="v")
            em.selfw("v")
            em.do("v", lambda e: e.scalar_tensor_tensor(out=rstdrow[:], in0=ps_av[0:1, 0:CH], scalar=1.0,
                                                        in1=m2row[:], op0=ALU.mult, op1=ALU.subtract), inc="v")
            ck_var = em.cnt["v"]
            em.wait("s", "v", ck_var)
            em.do("s", lambda e: e.activation(out=rstdrow[:], in_=rstdrow[:], func=AF.Sqrt, bias=consts[0:1, 194:195]), inc="s")
            ck_sqt = em.cnt["s"]
            em.wait("v", "s", ck_sqt)
            em.do("v", lambda e: e.reciprocal(out=m2row[:], in_=rstdrow[:]), inc="v")
            ck_rstd = em.cnt["v"]
            em.wait("t", "v", ck_rstd)
            em.do("t", lambda e: e.matmul(ps_x[:, 0:CH], ones_r, mrow[:], start=True, stop=True), inc="t")
            em.do("t", lambda e: e.matmul(ps_av[:, 0:CH], ones_r, m2row[:], start=True, stop=True), inc="t")
            ck_bc = em.cnt["t"]
            em.wait("v", "t", ck_bc)
            em.wait("v", "dxnp", ck_xn_sp)
            em.wait("v", "t", ck_xn_done_t[bx])
            for dc in range(8):
                pb = dc % 2
                em.selfw("v")
                em.do("v", lambda e, dc=dc, bx=bx, pb=pb: e.scalar_tensor_tensor(
                    out=t1b2[pb][:], in0=xt[bx][:, dc * CH:(dc + 1) * CH], scalar=1.0,
                    in1=ps_x[:, 0:CH], op0=ALU.mult, op1=ALU.subtract), inc="v")
                em.selfw("v")
                em.do("v", lambda e, pb=pb: e.tensor_mul(out=xn32b2[pb][:], in0=t1b2[pb][:],
                                                         in1=ps_av[:, 0:CH]), inc="v")
                ckv = em.cnt["v"]
                em.wait("s", "v", ckv)
                em.wait("s", "dxnp", ck_xn_sp)
                em.do("s", lambda e, dc=dc, pb=pb, xnh=xnh: e.activation(out=xnh[:, dc * CH:(dc + 1) * CH],
                                                                         in_=xn32b2[pb][:], func=AF.Copy), inc="s")
                cks = em.cnt["s"]
                em.wait("v", "s", cks)
                em.do("v", lambda e, dc=dc, pb=pb, xnh=xnh, xnl=xnl: e.scalar_tensor_tensor(
                    out=xnl[:, dc * CH:(dc + 1) * CH], in0=xn32b2[pb][:], scalar=1.0,
                    in1=xnh[:, dc * CH:(dc + 1) * CH], op0=ALU.mult, op1=ALU.subtract), inc="v")
            ck_xn_v[bx] = em.cnt["v"]
            ck_xt_free[bx] = em.cnt["v"]
            if c < OC:
                em.wait("s", "v", ck_xn_v[bx])
                em.dma("s", r3(xnscr_hi[c]), xnh[:].rearrange("p (a j) -> p a j", a=8), "dxnp")
                ck_xn_sp = em.dma("s", r3(xnscr_lo[c]), xnl[:].rearrange("p (a j) -> p a j", a=8), "dxnp")
                if debug:
                    em.dma("s", r3(dbg["xn_hi"][c]), xnh[:].rearrange("p (a j) -> p a j", a=8), "dxnp")
                    ck_xn_sp = em.dma("s", r3(dbg["xn_lo"][c]), xnl[:].rearrange("p (a j) -> p a j", a=8), "dxnp")

        def stream_w(cg):
            b = wb_i[0] % 3
            wb_i[0] += 1
            em.wait("y", "t", ck_wb_free_t[b])
            em.wait("y", "dwsp0", ck_wsp_tot[0])
            em.wait("y", "dwsp1", ck_wsp_tot[1])
            em.dma("y", wbhs[b][:].rearrange("p (a j) -> p a j", a=8),
                   r3(wscr_hi[:, :, cg * 512:(cg + 1) * 512]), f"dwb{b}")
            ck_wb_ld[b] = em.dma("y", wbls[b][:].rearrange("p (a j) -> p a j", a=8),
                                 r3(wscr_lo[:, :, cg * 512:(cg + 1) * 512]), f"dwb{b}")
            return b

        pending_ss = [None]
        ck_ss_done = [0, 0]

        def flush_ss():
            nonlocal ck_psz_t
            if pending_ss[0] is None:
                return
            t, ss_ap, kind, ckv, par = pending_ss[0]
            pending_ss[0] = None
            em.wait("t", "v", ckv)
            if t == 0:
                em.wait("t", "s", ck_ss_free_s[kind])
            em.do("t", lambda e, t=t, ss_ap=ss_ap, par=par: e.matmul(
                ss_ap, Et(t), sqb[:, par * CH:(par + 1) * CH],
                start=(t == 0), stop=(t == 7)), inc="t")
            ck_psz_t = em.cnt["t"]
            if t == 7:
                ck_ss_done[kind] = ck_psz_t

        def qk_group(c, bw, base_t, pbase, ss_ap, kind):
            nonlocal ck_psz_t
            bx = c % 2
            xnh, xnl = xnh2[bx], xnl2[bx]
            for i in range(4):
                t = base_t + i
                pi = (pbase + t) % 12
                slot = si[0] % 5
                si[0] += 1
                em.wait("t", f"dwb{bw}", ck_wb_ld[bw])
                em.wait("t", "v", max(slot_free[slot], ck_xn_v[bx]))
                em.wait("t", "s", slot_free_s[slot])
                seq = [(dc, wsb, xsb) for dc in range(8)
                       for (wsb, xsb) in ((wbhs[bw], xnh), (wbhs[bw], xnl), (wbls[bw], xnh))]
                for j, (dc, wsb, xsb) in enumerate(seq):
                    last = j == len(seq) - 1
                    em.do("t", lambda e, wsb=wsb, xsb=xsb, dc=dc, i=i, slot=slot, j=j, last=last: e.matmul(
                        qk_slots[slot], wsb[:, dc * 512 + i * 128: dc * 512 + (i + 1) * 128],
                        xsb[:, dc * CH:(dc + 1) * CH], start=(j == 0), stop=last),
                        inc=("t" if last else None))
                ck_mm = em.cnt["t"]
                flush_ss()
                em.wait("v", "t", ck_mm)
                em.do("v", lambda e, pi=pi, slot=slot: e.tensor_copy(out=p32[pi][:], in_=qk_slots[slot]), inc="v")
                em.selfw("v")
                par = t % 2
                em.do("v", lambda e, pi=pi, par=par: e.tensor_mul(
                    out=sqb[:, par * CH:(par + 1) * CH], in0=p32[pi][:], in1=p32[pi][:]), inc="v")
                ckv = em.cnt["v"]
                slot_free[slot] = ckv
                pending_ss[0] = (t, ss_ap, kind, ckv, par)
            ck_wb_free_t[bw] = ck_mm

        def qk_normalize(scr_hi, scr_lo, dhi, dlo, pbase, ss_ap, kind, ck_sst, rn_ps, ck_rnps):
            nonlocal ck_psav_v, ck_rnbcq_v, ck_kstg_sp
            if pending_ss[0] is not None and pending_ss[0][2] == kind:
                flush_ss()
            em.wait("s", "t", ck_ss_done[kind])
            em.do("s", lambda e, ss_ap=ss_ap: e.activation(out=nrmall[:, 0:CH],
                                                           in_=ss_ap, func=AF.Sqrt), inc="s")
            ck_ss_free_s[kind] = em.cnt["s"]
            em.wait("v", "s", ck_ss_free_s[kind])
            em.do("v", lambda e: e.tensor_scalar_max(out=rnall[:, 0:CH], in0=nrmall[:, 0:CH],
                                                     scalar1=EPS_NRM), inc="v")
            em.selfw("v")
            em.do("v", lambda e: e.reciprocal(out=nrmall[:, 0:CH], in_=rnall[:, 0:CH]), inc="v")
            ck_rn = em.cnt["v"]
            rn_b = [ps_x[:, 0:CH], ps_av[:, 0:CH]]
            for t in range(8):
                pb = t % 2
                em.wait("t", "v", max(ck_rn, rn_ck[pb]))
                em.do("t", lambda e, t=t, pb=pb: e.matmul(rn_b[pb], Ft(t), nrmall[:, 0:CH],
                                                          start=True, stop=True), inc="t")
                ckt = em.cnt["t"]
                em.wait("v", "t", ckt)
                em.wait("v", "dstg", ck_kstg_sp)
                em.selfw("v")
                em.do("v", lambda e, t=t, pb=pb, pbase=pbase: e.tensor_mul(
                    out=t1b2[pb][:], in0=p32[(pbase + t) % 12][:],
                    in1=rn_b[pb]), inc="v")
                ckv = em.cnt["v"]
                rn_ck[pb] = ckv
                em.wait("s", "v", ckv)
                em.wait("s", "dstg", ck_kstg_sp)
                em.do("s", lambda e, t=t, pb=pb: e.activation(out=kstg_h[:, (t % 2) * CH:(t % 2 + 1) * CH],
                                                              in_=t1b2[pb][:], func=AF.Copy), inc="s")
                cks = em.cnt["s"]
                em.wait("v", "s", cks)
                em.do("v", lambda e, t=t, pb=pb: e.scalar_tensor_tensor(
                    out=kstg_l[:, (t % 2) * CH:(t % 2 + 1) * CH], in0=t1b2[pb][:], scalar=1.0,
                    in1=kstg_h[:, (t % 2) * CH:(t % 2 + 1) * CH], op0=ALU.mult, op1=ALU.subtract), inc="v")
                if t % 2 == 1:
                    half = t - 1
                    ck_stg = em.cnt["v"]
                    em.wait("s", "v", ck_stg)
                    em.dma("s", r3(scr_hi[half:half + 2]), kstg_h[:].rearrange("p (a j) -> p a j", a=2), "dstg")
                    ck_kstg_sp = em.dma("s", r3(scr_lo[half:half + 2]), kstg_l[:].rearrange("p (a j) -> p a j", a=2), "dstg")
                    if debug and dhi is not None:
                        ck_kstg_sp = em.dma("s", r3(dhi[half:half + 2]), kstg_h[:].rearrange("p (a j) -> p a j", a=2), "dstg")
                    if debug and dlo is not None:
                        ck_kstg_sp = em.dma("s", r3(dlo[half:half + 2]), kstg_l[:].rearrange("p (a j) -> p a j", a=2), "dstg")
            ck_rnbcq_v = max(rn_ck)
            ck_psav_v = max(rn_ck)

        def v_groups(c, b4, b5):
            bx = c % 2
            xnh, xnl = xnh2[bx], xnl2[bx]
            for st in range(4):
                sl = 0
                for cg in (4, 5):
                    bw = b4 if cg == 4 else b5
                    vcg = cg - 4
                    slot = si[0] % 5
                    si[0] += 1
                    em.wait("t", f"dwb{bw}", ck_wb_ld[bw])
                    em.wait("t", "v", max(slot_free[slot], ck_xn_v[bx]))
                    em.wait("t", "s", slot_free_s[slot])
                    seq = [(dc, xsb, wsb) for dc in range(8)
                           for (xsb, wsb) in ((xnh, wbhs[bw]), (xnl, wbhs[bw]), (xnh, wbls[bw]))]
                    for j, (dc, xsb, wsb) in enumerate(seq):
                        last = j == len(seq) - 1
                        em.do("t", lambda e, xsb=xsb, wsb=wsb, dc=dc, st=st, slot=slot, j=j, last=last: e.matmul(
                            qk_slots[slot], xsb[:, dc * CH + st * 128: dc * CH + (st + 1) * 128],
                            wsb[:, dc * 512:(dc + 1) * 512], start=(j == 0), stop=last),
                            inc=("t" if last else None))
                    ck_mm = em.cnt["t"]
                    em.wait("s", "t", ck_mm)
                    em.wait("s", "dvsp0", ck_vstg_sp[0])
                    em.do("s", lambda e, vcg=vcg, slot=slot: e.activation(
                        out=vstg_h[:, vcg * 512:(vcg + 1) * 512],
                        in_=qk_slots[slot], func=AF.Copy), inc="s")
                    cks = em.cnt["s"]
                    em.wait("v", "s", cks)
                    em.wait("v", "dvsp0", ck_vstg_sp[0])
                    em.do("v", lambda e, vcg=vcg, slot=slot: e.scalar_tensor_tensor(
                        out=vstg_l[:, vcg * 512:(vcg + 1) * 512],
                        in0=qk_slots[slot], scalar=1.0,
                        in1=vstg_h[:, vcg * 512:(vcg + 1) * 512],
                        op0=ALU.mult, op1=ALU.subtract), inc="v")
                    slot_free[slot] = em.cnt["v"]
                    slot_free_s[slot] = cks
                    ck_wb_free_t[bw] = ck_mm
                ckv = em.cnt["v"]
                cks = em.cnt["s"]
                em.wait("s", "v", ckv)
                em.dma("s", vscr_hi[c, st], vstg_h[:, 0:D], "dvsp0")
                ck_vstg_sp[0] = em.dma("s", vscr_lo[c, st], vstg_l[:, 0:D], "dvsp0")
                if debug:
                    em.dma("s", dbg["v_hi"][c, st], vstg_h[:, 0:D], "dvsp0")
                    ck_vstg_sp[0] = em.dma("s", dbg["v_lo"][c, st], vstg_l[:, 0:D], "dvsp0")

        rn_ck = [0, 0]
        load_xT(0)
        header(0)
        nb = [stream_w(0), stream_w(1)]
        for c in range(NCH):
            own = c < OC
            bx = c % 2
            if c + 1 < NCH:
                load_xT(c + 1)
            if own:
                qk_group(c, nb[0], 0, 0, ps_x[0:16, 0:CH], 0)
                b2 = stream_w(2)
                qk_group(c, nb[1], 4, 0, ps_x[0:16, 0:CH], 0)
                ck_ss_q = ck_psz_t
                b3 = stream_w(3)
                qk_group(c, b2, 0, 8, ps_wc[0:16, 512:1024], 1)
                qk_normalize(qscr_hi[c], qscr_lo[c],
                             dbg["q_hi"][c] if debug else None,
                             dbg["q_lo"][c] if debug else None, 0,
                             ps_x[0:16, 0:CH], 0, ck_ss_q, None, None)
                b4 = stream_w(4)
                qk_group(c, b3, 4, 8, ps_wc[0:16, 512:1024], 1)
                ck_ss_k = ck_psz_t
                b5 = stream_w(5)
                if c + 1 < NCH:
                    nb0 = stream_w(0 if (c + 1) < OC else 2)
                qk_normalize(kscr_hi[c], kscr_lo[c],
                             dbg["k_hi"][c] if debug else None, None, 8,
                             ps_wc[0:16, 512:1024], 1, ck_ss_k, None, None)
            else:
                qk_group(c, nb[0], 0, 8, ps_wc[0:16, 512:1024], 1)
                b4 = stream_w(4)
                qk_group(c, nb[1], 4, 8, ps_wc[0:16, 512:1024], 1)
                ck_ss_k = ck_psz_t
                b5 = stream_w(5)
                if c + 1 < NCH:
                    nb0 = stream_w(0 if (c + 1) < OC else 2)
                qk_normalize(kscr_hi[c], kscr_lo[c],
                             dbg["k_hi"][c] if debug else None, None, 8,
                             ps_wc[0:16, 512:1024], 1, ck_ss_k, None, None)
            if c + 1 < NCH:
                header(c + 1)
            v_groups(c, b4, b5)
            ck_xn_done_t[bx] = em.cnt["t"]
            if c + 1 < NCH:
                nb = [nb0, stream_w(1 if (c + 1) < OC else 3)]
            for gi in range(5 * c, min(5 * (c + 1), 40)):
                gamma_tile(gi)

        ck_A_stg = em.cnt["dstg"]
        ck_A_vsp = [em.cnt["dvsp0"], 0]
        ck_A_xnp = em.cnt["dxnp"]
        ck_gamma_s = gw_ck_s[0]
        em.wait("v", "s", ck_gamma_s)
        em.do("v", lambda e: e.tensor_reduce(out=gacc[:, 40:41], in_=gacc[:, 0:8],
                                             axis=AX.X, op=ALU.add), inc="v")
        em.do("v", lambda e: e.tensor_reduce(out=gacc[:, 41:42], in_=gacc[:, 8:40],
                                             axis=AX.X, op=ALU.add), inc="v")
        ck_g1 = em.cnt["v"]
        ck_g2 = ck_g1
        ps_x_free_s = 0

        def gamma_finish(ck_g, gfin, nelem, rc):
            nonlocal ps_x_free_v, ps_x_free_s
            em.wait("t", "v", ck_g)
            em.wait("t", "v", ps_x_free_v)
            em.do("t", lambda e: e.matmul(ps_x[0:1, 0:1], gacc[:, gfin:gfin + 1], onescol,
                                          start=True, stop=True), inc="t")
            ckt = em.cnt["t"]
            em.wait("v", "t", ckt)
            em.do("v", lambda e: e.tensor_scalar(out=grow[:], in0=ps_x[0:1, 0:1],
                                                 scalar1=1.0 / nelem, scalar2=EPS_Q,
                                                 op0=ALU.mult, op1=ALU.add), inc="v")
            em.selfw("v")
            em.do("v", lambda e: e.reciprocal(out=g2row[:], in_=grow[:]), inc="v")
            ckv = em.cnt["v"]
            em.wait("t", "v", ckv)
            em.do("t", lambda e: e.matmul(ps_x[:, 1:2], ones_r, g2row[:], start=True, stop=True), inc="t")
            ckt2 = em.cnt["t"]
            em.wait("v", "t", ckt2)
            em.do("v", lambda e, rc=rc: e.tensor_copy(out=rc[:], in_=ps_x[:, 1:2]), inc="v")
            ps_x_free_v = em.cnt["v"]
            return em.cnt["v"]

        ck_rc1 = gamma_finish(ck_g1, 40, FF * D, rcol1)
        ck_rc2 = gamma_finish(ck_g2, 41, FF * D, rcol2)

        q2_ld = [0]
        q2_sp = [0]
        q2_v = [0]

        def quant2_tile(i):
            if i < 8:
                src_ap, width, dst, rc, ck_rc, dkey, ti = W1T[i * 128:(i + 1) * 128, :], FF, wq1s, rcol1, ck_rc1, "wq1d", i
            else:
                src_ap, width, dst, rc, ck_rc, dkey, ti = W2T[(i - 8) * 128:(i - 7) * 128, :], D, wq2s, rcol2, ck_rc2, "wq2d", i - 8
            if i == 0:
                em.wait("y", "s", gw_ck_s[0])
            em.wait("y", "v", q2_v[0])
            em.wait("y", "dq", q2_sp[0])
            q2_ld[0] = em.dma("y", wt[:, 0:width], src_ap, "dq")
            em.wait("v", "dq", q2_ld[0])
            em.wait("v", "v", ck_rc)
            em.selfw("v")
            em.do("v", lambda e, rc=rc, width=width: e.tensor_scalar(
                out=wqt[:, 0:width], in0=wt[:, 0:width], scalar1=rc[:],
                scalar2=0.5, op0=ALU.mult, op1=ALU.is_gt), inc="v")
            em.selfw("v")
            em.do("v", lambda e, rc=rc, width=width: e.tensor_scalar(
                out=wt[:, 0:width], in0=wt[:, 0:width], scalar1=rc[:],
                scalar2=-0.5, op0=ALU.mult, op1=ALU.is_lt), inc="v")
            em.selfw("v")
            em.do("v", lambda e, width=width: e.tensor_sub(out=wqt[:, 0:width], in0=wqt[:, 0:width],
                                                           in1=wt[:, 0:width]), inc="v")
            q2_v[0] = em.cnt["v"]
            em.wait("y", "v", q2_v[0])
            q2_sp[0] = em.dma("y", dst[ti], wqt[:, 0:width], "dq")
            if debug:
                q2_sp[0] = em.dma("y", dbg[dkey][ti], wqt[:, 0:width], "dq")

        sB = em.cnt["s"]
        vB = em.cnt["v"]
        ck_kv_ld = [0, 0]
        ck_kv_free_t = [0, 0]
        ck_sgrp_s = [sB, sB, sB]
        ck_exp_s = [sB, sB, sB, sB]
        ck_sgrp_v = [vB, vB, vB, vB]
        ck_ab_t = [0, 0, 0, 0]
        ck_av_v = vB
        ck_dn_t = 0
        ck_dn_s = sB
        ck_rb_t = 0
        ck_rb_v = vB
        ck_fold_v = vB
        ck_fold_s = sB
        ck_at_s = 0
        ck_at_v = 0

        em.wait("s", "v", ck_scols)

        def load_head_y(h):
            bh = h % 2
            ct, hb = h // 2, h % 2
            em.wait("y", "t", ck_kv_free_t[bh])
            em.wait("y", "dstg", ck_A_stg)
            em.wait("y", "dvsp0", ck_A_vsp[0])
            em.wait("y", "dxnp", ck_A_xnp)
            sem = f"dkv{bh}"
            kin = r3(kscr_hi[:, ct, hb * 64:(hb + 1) * 64, :])
            em.dma("y", krh[bh][0:64, :].rearrange("p (a j) -> p a j", a=8), kin, sem)
            em.dma("y", krh[bh][64:128, :].rearrange("p (a j) -> p a j", a=8), kin, sem)
            ck = em.dma("y", qpk[bh][0:64, :].rearrange("p (a j) -> p a j", a=4),
                        r3(qscr_hi[:, ct, hb * 64:(hb + 1) * 64, :]), sem)
            return ck

        def load_head_s(h):
            bh = h % 2
            ct, hb = h // 2, h % 2
            em.wait("s", "t", ck_kv_free_t[bh])
            em.wait("s", "dstg", ck_A_stg)
            em.wait("s", "dvsp0", ck_A_vsp[0])
            em.wait("s", "dxnp", ck_A_xnp)
            sem2 = f"dkw{bh}"
            kil = r3(kscr_lo[:, ct, hb * 64:(hb + 1) * 64, :])
            em.dma("s", krl[bh][0:64, :].rearrange("p (a j) -> p a j", a=8), kil, sem2)
            em.dma("s", krl[bh][64:128, :].rearrange("p (a j) -> p a j", a=8), kil, sem2)
            ck_kw_ld[bh] = em.dma("s", qpk[bh][64:128, :].rearrange("p (a j) -> p a j", a=4),
                                  r3(qscr_lo[:, ct, hb * 64:(hb + 1) * 64, :]), sem2)
            em.wait("g", "t", ck_kv_free_t[bh])
            em.wait("g", "dstg", ck_A_stg)
            em.wait("g", "dvsp0", ck_A_vsp[0])
            em.wait("g", "dxnp", ck_A_xnp)
            vv = vpk[bh][:].rearrange("p (c st x d) -> p c st x d", c=8, st=4, x=2, d=64)
            em.wait("g", "dg", em.cnt["dg"])
            em.dma("g", vv[:, :, :, 0, :],
                   vscr_hi[:, :, :, h * 64:(h + 1) * 64].transpose([2, 0, 1, 3]), "dg")
            em.wait("g", "dg", em.cnt["dg"])
            ck_vpl_ld[bh] = em.dma("g", vv[:, :, :, 1, :],
                                   vscr_lo[:, :, :, h * 64:(h + 1) * 64].transpose([2, 0, 1, 3]), "dg")

        pending_tail = [None]
        ck_rb_t = 0
        ck_kw_ld = [0, 0]
        ck_vpl_ld = [0, 0]
        ck_kv_ld[0] = load_head_y(0)
        load_head_s(0)
        for h in range(HH):
            bh = h % 2
            ct, hb = h // 2, h % 2
            if h + 1 < HH:
                ck_kv_ld[(h + 1) % 2] = load_head_y(h + 1)
            for qc in range(4):
                slot_i = h * 4 + qc
                if slot_i < 40:
                    quant2_tile(slot_i)

                def emit_av(g):
                    pr = g % 4
                    em.wait("t", "v", ck_sgrp_v[pr])
                    if g == 0:
                        em.wait("t", "v", ck_av_v)
                        em.wait("t", "s", ck_dn_s)
                    for kt in (2 * g, 2 * g + 1):
                        ah = (kt % 2)
                        em.do("t", lambda e, kt=kt, pr=pr, ah=ah, bh=bh: e.matmul(
                            ps_av[:], vpk[bh][:, kt * 128:(kt + 1) * 128],
                            ahi[pr][:, ah * 512:(ah + 1) * 512],
                            start=(kt == 0), stop=False), inc=None)
                        em.do("t", lambda e, kt=kt, pr=pr, ah=ah, bh=bh: e.matmul(
                            ps_av[:], vpk[bh][:, kt * 128:(kt + 1) * 128],
                            alo[pr][:, ah * 512:(ah + 1) * 512],
                            start=False, stop=(kt == 31)), inc=None)
                        em.do("t", lambda e, kt=kt, pr=pr, ah=ah: e.matmul(
                            ps_x[0:1, :], onesb[:], ahi[pr][:, ah * 512:(ah + 1) * 512],
                            start=(kt == 0), stop=False), inc=None)
                        em.do("t", lambda e, kt=kt, pr=pr, ah=ah: e.matmul(
                            ps_x[0:1, :], onesb[:], alo[pr][:, ah * 512:(ah + 1) * 512],
                            start=False, stop=(kt == 31)),
                            inc=("t" if kt == 2 * g + 1 else None))
                    ck_ab_t[pr] = em.cnt["t"]

                sseq = [0, 1, 0, 1, 2, 0, 1, 2, 0, 1, 2, 0, 1, 2, 0, 1]
                for g in range(16):
                    bk = sseq[g]
                    sg = [ps_wa, ps_wb, ps_wc][bk]
                    pr = g % 4
                    em.wait("t", f"dkv{bh}", ck_kv_ld[bh])
                    em.wait("t", f"dkw{bh}", ck_kw_ld[bh])
                    em.wait("t", "dg", ck_vpl_ld[bh])
                    em.wait("t", "s", ck_sgrp_s[bk])
                    if g == 2 and pending_tail[0] is not None:
                        pending_tail[0]()
                        pending_tail[0] = None
                    if g == 2 and qc == 0 and h + 1 < HH:
                        load_head_s(h + 1)
                    if g == 4:
                        em.wait("t", "v", ck_fold_v)
                        em.wait("t", "v", ck_rb_v)
                    for kt in (2 * g, 2 * g + 1):
                        half = sg[:, (kt % 2) * 512:(kt % 2 + 1) * 512]
                        em.do("t", lambda e, kt=kt, half=half, bh=bh, qc=qc: e.matmul(
                            half, krh[bh][:, kt * 128:(kt + 1) * 128],
                            qpk[bh][:, qc * 512:(qc + 1) * 512], start=True, stop=False), inc=None)
                        em.do("t", lambda e, kt=kt, half=half, bh=bh, qc=qc: e.matmul(
                            half, krl[bh][:, kt * 128:(kt + 1) * 128],
                            qpk[bh][:, qc * 512:(qc + 1) * 512], start=False, stop=True),
                            inc=("t" if kt == 2 * g + 1 else None))
                    ck_smm = em.cnt["t"]
                    em.wait("s", "t", ck_smm)
                    em.wait("s", "t", ck_ab_t[pr])
                    em.wait("s", "v", ck_sgrp_v[pr])
                    em.do("s", lambda e, sg=sg, pr=pr, h=h: e.activation(
                        out=ahi[pr][:], in_=sg[:], func=AF.Exp,
                        scale=scols[:, h:h + 1], bias=nscols[:, h:h + 1]), inc="s")
                    em.do("s", lambda e, sg=sg, pr=pr, h=h: e.activation(
                        out=a32[pr][:], in_=sg[:], func=AF.Exp,
                        scale=scols[:, h:h + 1], bias=nscols[:, h:h + 1]), inc="s")
                    ck_sgrp_s[bk] = em.cnt["s"]
                    ck_exp_s[pr] = em.cnt["s"]
                    em.wait("v", "s", ck_exp_s[pr])
                    em.do("v", lambda e, pr=pr: e.scalar_tensor_tensor(
                        out=alo[pr][:], in0=a32[pr][:], scalar=1.0, in1=ahi[pr][:],
                        op0=ALU.mult, op1=ALU.subtract), inc="v")
                    ck_sgrp_v[pr] = em.cnt["v"]
                    if g >= 2:
                        emit_av(g - 2)
                emit_av(14)
                emit_av(15)
                ck_dn_t = em.cnt["t"]
                em.wait("v", "t", ck_dn_t)
                em.wait("v", "t", ck_rb_t)
                em.do("v", lambda e: e.tensor_copy(out=av32[:], in_=ps_av[:]), inc="v")
                ck_av_v = em.cnt["v"]
                em.wait("s", "t", ck_dn_t)
                em.do("s", lambda e: e.activation(out=lnrow[:], in_=ps_x[0:1, :], func=AF.Ln), inc="s")
                em.selfw("s")
                em.do("s", lambda e: e.activation(out=rdrow[:], in_=lnrow[:], func=AF.Exp, scale=-1.0), inc="s")
                ck_dn_s = em.cnt["s"]
                def make_tail(hb=hb, ct=ct, qc=qc):
                    def tail():
                        nonlocal ck_rb_t, ck_rb_v, ck_fold_v, ck_at_s, ck_fold_s, ck_at_v
                        em.wait("t", "v", ck_av_v)
                        em.wait("t", "v", ck_fold_v)
                        em.wait("t", "s", ck_fold_s)
                        em.do("t", lambda e: e.matmul(ps_wc[hb * 64:(hb + 1) * 64, 0:512], Fc, av32[:],
                                                      start=True, stop=True), inc="t")
                        em.wait("t", "s", ck_dn_s)
                        em.wait("t", "v", ck_rb_v)
                        em.do("t", lambda e: e.matmul(ps_wc[hb * 64:(hb + 1) * 64, 512:1024],
                                                      ones_r[0:1, 0:64], rdrow[:],
                                                      start=True, stop=True), inc="t")
                        ck_rb_t = em.cnt["t"]
                        em.wait("v", "t", ck_rb_t)
                        em.wait("v", "s", ck_at_s)
                        em.selfw("v")
                        em.do("v", lambda e: e.tensor_copy(out=rdbc[hb * 64:(hb + 1) * 64, :],
                                                           in_=ps_wc[hb * 64:(hb + 1) * 64, 512:1024]), inc="v")
                        em.selfw("v")
                        em.do("v", lambda e: e.tensor_mul(out=prod32[hb * 64:(hb + 1) * 64, :],
                                                          in0=ps_wc[hb * 64:(hb + 1) * 64, 0:512],
                                                          in1=rdbc[hb * 64:(hb + 1) * 64, :]), inc="v")
                        ck_rb_v = em.cnt["v"]
                        ck_fold_v = em.cnt["v"]
                        em.wait("s", "v", ck_rb_v)
                        off = ct * NQ + qc * 512
                        em.do("s", lambda e: e.activation(
                            out=attnT_hi[hb * 64:(hb + 1) * 64, off:off + 512],
                            in_=prod32[hb * 64:(hb + 1) * 64, :], func=AF.Copy), inc="s")
                        ck_at_s = em.cnt["s"]
                        ck_fold_s = em.cnt["s"]
                        em.wait("v", "s", ck_at_s)
                        em.do("v", lambda e: e.scalar_tensor_tensor(
                            out=attnT_lo[hb * 64:(hb + 1) * 64, off:off + 512],
                            in0=prod32[hb * 64:(hb + 1) * 64, :], scalar=1.0,
                            in1=attnT_hi[hb * 64:(hb + 1) * 64, off:off + 512],
                            op0=ALU.mult, op1=ALU.subtract), inc="v")
                        ck_at_v = em.cnt["v"]
                    return tail

                pending_tail[0] = make_tail()
            ck_kv_free_t[bh] = em.cnt["t"]

        if pending_tail[0] is not None:
            pending_tail[0]()
            pending_tail[0] = None
        ck_wq2 = em.cnt["dq"]
        ck_attn_v = em.cnt["v"]
        ck_attn_s = em.cnt["s"]
        if debug:
            em.wait("y", "v", ck_attn_v)
            em.wait("y", "s", ck_attn_s)
            em.dma("y", dbg["at_hi"], attnT_hi[:], "dstg")
            em.dma("y", dbg["at_lo"], attnT_lo[:], "dstg")

        tC = em.cnt["t"]
        em.wait("t", "v", ck_attn_v)
        em.wait("t", "s", ck_attn_s)
        em.wait("y", "t", tC)
        em.wait("y", "v", ck_attn_v)
        em.wait("y", "s", ck_attn_s)
        em.wait("s", "t", tC)
        em.wait("s", "v", ck_attn_v)
        em.wait("s", "dc", ck_c0)
        vC = em.cnt["v"]
        sC = em.cnt["s"]
        ck_cx_ld = [0, 0]
        ck_cx_free_t = [0, 0]
        ck_xo_ld = [0, 0]
        ck_xo_free_v = [0, 0]
        ck_w1_ld = [0, 0, 0, 0]
        ck_w1_free_t = [0, 0, 0, 0]
        ck_w2_ld = [0, 0, 0, 0]
        ck_w2_free_t = [0, 0, 0, 0]
        ck_h_ps = [vC, vC, vC, vC]
        ck_h_ps_s = [sC, sC, sC, sC]
        ck_out_ps_v = [vC, vC]
        outreg = [[ps_wc[:, 0:512], ps_wc[:, 512:1024]], [ps_av[:, 0:512], ps_x[:, 0:512]]]
        ck_out_st = [0, 0]
        ck_wob_ld = [0, 0]
        ck_wob_free_t = [0, 0]

        def load_cxn(cc):
            bc = cc % 2
            oc, sub = cc // 2, cc % 2
            em.wait("y", "t", ck_cx_free_t[bc])
            em.dma("y", cxnh[bc][:].rearrange("p (a j) -> p a j", a=8),
                   r3(xnscr_hi[oc][:, :, sub * CC:(sub + 1) * CC]), f"dcx{bc}")
            ck_cx_ld[bc] = em.dma("y", cxnl[bc][:].rearrange("p (a j) -> p a j", a=8),
                                  r3(xnscr_lo[oc][:, :, sub * CC:(sub + 1) * CC]), f"dcx{bc}")

        def load_xo(cc):
            bc = cc % 2
            em.wait("s", "v", ck_xo_free_v[bc])
            ck_xo_ld[bc] = em.dma("s", xob[bc][:].rearrange("p (a j) -> p a j", a=2),
                                  r3(x_own[cc * CC:(cc + 1) * CC, :].rearrange("(a p) j -> a p j", p=128)), f"dxo{bc}")

        def load_w1(ft, cc):
            bw = ft % 4
            em.wait("s", "t", ck_w1_free_t[bw])
            if cc == 0 and ft < 4:
                em.wait("s", "dq", ck_wq2)
            ck_w1_ld[bw] = em.dma("s", wq1b[bw][:].rearrange("p (a j) -> p a j", a=8),
                                  r3(wq1s[:, :, ft * 128:(ft + 1) * 128]), f"dw1{bw}")

        def load_w2(fc, cc):
            bw = fc % 4
            em.wait("s", "t", ck_w2_free_t[bw])
            if cc == 0 and fc < 4:
                em.wait("s", "dq", ck_wq2)
            ck_w2_ld[bw] = em.dma("s", wq2b[bw][:], wq2s[fc], f"dw2{bw}")

        def load_wob(ic, cc, mt):
            bo = ic % 2
            em.wait("s", "t", ck_wob_free_t[bo])
            if cc == 0 and mt == 0 and ic < 2:
                em.wait("s", "dwsp0", ck_wsp_tot[0])
                em.wait("s", "dwsp1", ck_wsp_tot[1])
            em.dma("s", wob_h[bo][:], woscr[0, ic], f"dwob{bo}")
            ck_wob_ld[bo] = em.dma("s", wob_l[bo][:], woscr[1, ic], f"dwob{bo}")

        load_cxn(0)
        load_xo(0)
        for cc in range(NCC):
            bc = cc % 2
            if cc + 1 < NCC:
                load_cxn(cc + 1)
                load_xo(cc + 1)
            for pre in range(3):
                load_w1(pre, cc)
            hslots = [ps_wa[:, 0:512], ps_wa[:, 512:1024], ps_wb[:, 0:512], ps_wb[:, 512:1024]]
            for ft in range(32):
                bw = ft % 4
                hp4 = ft % 4
                hps = hslots[hp4]
                em.wait("t", f"dw1{bw}", ck_w1_ld[bw])
                em.wait("t", f"dcx{bc}", ck_cx_ld[bc])
                em.wait("t", "v", ck_h_ps[hp4])
                em.wait("t", "s", ck_h_ps_s[hp4])
                seq = [(dc, xsb) for dc in range(8) for xsb in (cxnh[bc], cxnl[bc])]
                for j, (dc, xsb) in enumerate(seq):
                    last = j == len(seq) - 1
                    em.do("t", lambda e, dc=dc, xsb=xsb, bw=bw, hps=hps, j=j, last=last: e.matmul(
                        hps[:, 0:CC], wq1b[bw][:, dc * 128:(dc + 1) * 128],
                        xsb[:, dc * CC:(dc + 1) * CC], start=(j == 0), stop=last),
                        inc=("t" if last else None))
                ck_mm = em.cnt["t"]
                ck_w1_free_t[bw] = ck_mm
                if ft + 3 < 32:
                    load_w1(ft + 3, cc)
                em.wait("s", "t", ck_mm)
                em.wait("s", "v", ck_h_ps[hp4])
                em.do("s", lambda e, ft=ft, hps=hps: e.activation(
                    out=hh_sb[:, ft * CC:(ft + 1) * CC], in_=hps[:, 0:CC],
                    func=AF.Relu, bias=b1c[:, ft:ft + 1]), inc="s")
                em.do("s", lambda e, ft=ft, hps=hps, hp4=hp4: e.activation(
                    out=h32b[:, hp4 * CC:(hp4 + 1) * CC], in_=hps[:, 0:CC],
                    func=AF.Relu, bias=b1c[:, ft:ft + 1]), inc="s")
                ck_h_ps_s[hp4] = em.cnt["s"]
                em.wait("v", "s", ck_h_ps_s[hp4])
                em.do("v", lambda e, ft=ft, hp4=hp4: e.scalar_tensor_tensor(
                    out=hl_sb[:, ft * CC:(ft + 1) * CC], in0=h32b[:, hp4 * CC:(hp4 + 1) * CC],
                    scalar=1.0, in1=hh_sb[:, ft * CC:(ft + 1) * CC],
                    op0=ALU.mult, op1=ALU.subtract), inc="v")
                ck_h_ps[hp4] = em.cnt["v"]
            ck_h_done_v = em.cnt["v"]

            for mt in range(2):
                em.wait("t", "v", ck_out_ps_v[mt])
                load_wob(0, cc, mt)
                for ic in range(8):
                    bo = ic % 2
                    if ic + 1 < 8:
                        load_wob(ic + 1, cc, mt)
                    em.wait("t", f"dwob{bo}", ck_wob_ld[bo])
                    lhi = attnT_hi[:, ic * NQ + cc * CC + mt * 128: ic * NQ + cc * CC + (mt + 1) * 128]
                    llo = attnT_lo[:, ic * NQ + cc * CC + mt * 128: ic * NQ + cc * CC + (mt + 1) * 128]
                    for nh in range(2):
                        rhi = wob_h[bo][:, nh * 512:(nh + 1) * 512]
                        rlo = wob_l[bo][:, nh * 512:(nh + 1) * 512]
                        for (lt, rt) in ((lhi, rhi), (llo, rhi), (lhi, rlo)):
                            last = (nh == 1 and rt is rlo)
                            em.do("t", lambda e, lt=lt, rt=rt, mt=mt, nh=nh, ic=ic: e.matmul(
                                outreg[mt][nh], lt, rt,
                                start=(ic == 0 and lt is lhi and rt is rhi), stop=False),
                                inc=("t" if last else None))
                    ck_wob_free_t[bo] = em.cnt["t"]
                for nh in range(2):
                    em.do("t", lambda e, mt=mt, nh=nh: e.matmul(
                        outreg[mt][nh], ones_r,
                        btot[0:1, nh * 512:(nh + 1) * 512], start=False, stop=False), inc=None)
            em.wait("t", "v", ck_h_done_v)
            for pre in range(3):
                load_w2(pre, cc)
            for fc in range(32):
                bw = fc % 4
                em.wait("t", f"dw2{bw}", ck_w2_ld[bw])
                for mt in range(2):
                    for nh in range(2):
                        for hsb in (hh_sb, hl_sb):
                            stop = (fc == 31 and hsb is hl_sb)
                            fcl = (mt == 1 and nh == 1 and hsb is hl_sb)
                            em.do("t", lambda e, fc=fc, mt=mt, nh=nh, hsb=hsb, bw=bw, stop=stop: e.matmul(
                                outreg[mt][nh],
                                hsb[:, fc * CC + mt * 128: fc * CC + (mt + 1) * 128],
                                wq2b[bw][:, nh * 512:(nh + 1) * 512],
                                start=False, stop=stop),
                                inc=("t" if fcl else None))
                ck_w2_free_t[bw] = em.cnt["t"]
                if fc + 3 < 32:
                    load_w2(fc + 3, cc)
            ck_ffn2 = em.cnt["t"]
            em.wait("v", "t", ck_ffn2)
            em.wait("v", f"dxo{bc}", ck_xo_ld[bc])
            em.wait("v", f"dout{bc}", ck_out_st[bc])
            for mt in range(2):
                for nh in range(2):
                    em.do("v", lambda e, mt=mt, nh=nh, bc=bc: e.tensor_add(
                        out=outb[bc][:, mt * D + nh * 512: mt * D + (nh + 1) * 512],
                        in0=xob[bc][:, mt * D + nh * 512: mt * D + (nh + 1) * 512],
                        in1=outreg[mt][nh]), inc="v")
                ck_out_ps_v[mt] = em.cnt["v"]
            ck_outb_v = em.cnt["v"]
            ck_xo_free_v[bc] = ck_outb_v
            em.wait("y", "v", ck_outb_v)
            ck_out_st[bc] = em.dma("y", y_out[cc * CC:(cc + 1) * CC, :].rearrange("(a p) j -> a p j", p=128).transpose([1, 0, 2]),
                                   outb[bc][:].rearrange("p (a j) -> p a j", a=2), f"dout{bc}")
            ck_cx_free_t[bc] = em.cnt["t"]

        for snm in ["dout0", "dout1", "dg", "dq", "dstg", "dvsp0", "dxnp",
                    "dx0", "dx1", "dwb0", "dwb1", "dwb2", "dkv0", "dkv1", "dkw0", "dkw1", "dcx0", "dcx1",
                    "dxo0", "dxo1", "dw10", "dw11", "dw20", "dw21", "dwob0", "dwob1", "dc", "dwsp0", "dwsp1"]:
            em.wait("y", snm, em.cnt[snm])

    with nc.Block() as block:
        @block.sync
        def _(eng):
            walk(Em("y", eng, sems))

        @block.tensor
        def _(eng):
            walk(Em("t", eng, sems))

        @block.scalar
        def _(eng):
            walk(Em("s", eng, sems))

        @block.vector
        def _(eng):
            walk(Em("v", eng, sems))

        @block.gpsimd
        def _(eng):
            walk(Em("g", eng, sems))

    return nc


def _prep_inputs(L0_attn_x, L0_attn_w_qkv, L0_attn_w_out, L0_attn_b_out,
                 L0_attn_logit_scale, L0_ffn_W1, L0_ffn_b1, L0_ffn_W2, L0_ffn_b2):
    x = np.ascontiguousarray(np.asarray(L0_attn_x, dtype=np.float32))
    w_qkv = np.ascontiguousarray(np.asarray(L0_attn_w_qkv, dtype=np.float32))
    w_out = np.ascontiguousarray(np.asarray(L0_attn_w_out, dtype=np.float32))
    b_out = np.asarray(L0_attn_b_out, dtype=np.float32)
    lsc = np.asarray(L0_attn_logit_scale, dtype=np.float32).reshape(1, HH)
    W1 = np.asarray(L0_ffn_W1, dtype=np.float32)
    b1 = np.asarray(L0_ffn_b1, dtype=np.float32)
    W2 = np.asarray(L0_ffn_W2, dtype=np.float32)
    b2 = np.asarray(L0_ffn_b2, dtype=np.float32)
    W1T = np.ascontiguousarray(W1.T)
    W2T = np.ascontiguousarray(W2.T)
    b1c = np.ascontiguousarray(b1.reshape(32, 128).T)
    consts, c16 = _host_consts()
    in_maps = []
    for core in range(8):
        b, half = core // 2, core % 2
        own = x[b, half * NQ:(half + 1) * NQ]
        other = x[b, (1 - half) * NQ:(2 - half) * NQ]
        xTc = np.ascontiguousarray(np.concatenate([own, other], axis=0).T)
        in_maps.append({
            "xT": xTc,
            "x_own": np.ascontiguousarray(own),
            "w_qkv": w_qkv,
            "w_out": w_out,
            "W1T": W1T,
            "W2T": W2T,
            "b1c": b1c,
            "brow_out": b_out.reshape(1, D),
            "brow2": b2.reshape(1, D),
            "ls": lsc,
            "consts_in": consts,
            "consts16_in": c16,
        })
    return in_maps


def kernel(L0_attn_x, L0_attn_w_qkv, L0_attn_w_out, L0_attn_b_out,
           L0_attn_logit_scale, L0_ffn_W1, L0_ffn_b1, L0_ffn_W2, L0_ffn_b2):
    in_maps = _prep_inputs(L0_attn_x, L0_attn_w_qkv, L0_attn_w_out, L0_attn_b_out,
                           L0_attn_logit_scale, L0_ffn_W1, L0_ffn_b1, L0_ffn_W2, L0_ffn_b2)
    nc = build_nc(debug=False)
    res = run_bass_kernel_spmd(nc, in_maps, core_ids=list(range(8)))
    out = np.empty((B, N, D), dtype=np.float32)
    for core in range(8):
        b, half = core // 2, core % 2
        out[b, half * NQ:(half + 1) * NQ] = res.results[core]["y"]
    return out


# version 18
# speedup vs baseline: 1.0649x; 1.0496x over previous
import math
from contextlib import ExitStack

import numpy as np
import concourse.bass as bass
import concourse.mybir as mybir
from concourse.bass_utils import run_bass_kernel_spmd

F32 = mybir.dt.float32
BF16 = mybir.dt.bfloat16
AF = mybir.ActivationFunctionType
ALU = mybir.AluOpType
AX = mybir.AxisListType

B, N, D, HH, DH = 4, 4096, 1024, 16, 64
NQ = 2048
CH = 512
NCH = N // CH
OC = NQ // CH
CC = 256
NCC = NQ // CC
FF = 4096
LOG100 = float(np.float32(math.log(100.0)))
EPS_LN = 1e-5
EPS_Q = 1e-5
EPS_NRM = 1e-12


def _host_consts():
    c = np.zeros((128, 195), dtype=np.float32)
    c[:, 194] = EPS_LN
    c[:, 0] = 1.0
    c[:, 1] = 1.0 / 1024.0
    for m in range(64):
        c[m, 2 + m] = 1.0
        c[m + 64, 2 + m] = 1.0
    for t in range(8):
        c[0:64, 66 + 16 * t + 2 * t] = 1.0
        c[64:128, 66 + 16 * t + 2 * t + 1] = 1.0
    c16 = np.zeros((16, 1152), dtype=np.float32)
    for t in range(8):
        c16[2 * t, 128 * t + 0: 128 * t + 64] = 1.0
        c16[2 * t + 1, 128 * t + 64: 128 * t + 128] = 1.0
    c16[0, 1024:1152] = 1.0
    return c, c16


class Em:

    def __init__(self, active, eng, sems):
        self.active = active
        self.eng = eng
        self.sems = sems
        self.cnt = {k: 0 for k in sems}

    def do(self, eng, fn, inc=None, n=1):
        if eng == self.active:
            inst = fn(self.eng)
            if inc is not None:
                inst.then_inc(self.sems[inc], n)
        if inc is not None:
            self.cnt[inc] += n
        return self.cnt[inc] if inc is not None else None

    def dma(self, eng, out, in_, inc):
        return self.do(eng, lambda e: e.dma_start(out=out, in_=in_), inc=inc, n=16)

    def wait(self, eng, sem, val):
        if self.active == eng and val > 0:
            self.eng.wait_ge(self.sems[sem], val)

    def drain(self, eng):
        if self.active == eng:
            self.eng.drain()

    def selfw(self, eng):
        sem = {"t": "t", "s": "s", "v": "v", "g": "g"}[eng]
        self.wait(eng, sem, self.cnt[sem])


def build_nc(debug=False):
    nc = bass.Bass("TRN2", target_bir_lowering=False, debug=False)

    xT = nc.dram_tensor("xT", [D, N], F32, kind="ExternalInput").ap()
    x_own = nc.dram_tensor("x_own", [NQ, D], F32, kind="ExternalInput").ap()
    w_qkv = nc.dram_tensor("w_qkv", [D, 3 * D], F32, kind="ExternalInput").ap()
    w_out = nc.dram_tensor("w_out", [D, D], F32, kind="ExternalInput").ap()
    W1T = nc.dram_tensor("W1T", [D, FF], F32, kind="ExternalInput").ap()
    W2T = nc.dram_tensor("W2T", [FF, D], F32, kind="ExternalInput").ap()
    b1c_in = nc.dram_tensor("b1c", [128, 32], F32, kind="ExternalInput").ap()
    brow_out = nc.dram_tensor("brow_out", [1, D], F32, kind="ExternalInput").ap()
    brow2 = nc.dram_tensor("brow2", [1, D], F32, kind="ExternalInput").ap()
    ls_in = nc.dram_tensor("ls", [1, HH], F32, kind="ExternalInput").ap()
    consts_in = nc.dram_tensor("consts_in", [128, 195], F32, kind="ExternalInput").ap()
    c16_in = nc.dram_tensor("consts16_in", [16, 1152], F32, kind="ExternalInput").ap()
    y_out = nc.dram_tensor("y", [NQ, D], F32, kind="ExternalOutput").ap()

    wscr_hi = nc.dram_tensor("wscr_hi", [8, 128, 3 * D], BF16, kind="Internal").ap()
    wscr_lo = nc.dram_tensor("wscr_lo", [8, 128, 3 * D], BF16, kind="Internal").ap()
    woscr = nc.dram_tensor("woscr", [2, 8, 128, D], BF16, kind="Internal").ap()
    wq1s = nc.dram_tensor("wq1s", [8, 128, FF], BF16, kind="Internal").ap()
    wq2s = nc.dram_tensor("wq2s", [32, 128, D], BF16, kind="Internal").ap()
    xnscr_hi = nc.dram_tensor("xnscr_hi", [OC, 8, 128, CH], BF16, kind="Internal").ap()
    xnscr_lo = nc.dram_tensor("xnscr_lo", [OC, 8, 128, CH], BF16, kind="Internal").ap()
    kscr_hi = nc.dram_tensor("kscr_hi", [NCH, 8, 128, CH], BF16, kind="Internal").ap()
    kscr_lo = nc.dram_tensor("kscr_lo", [NCH, 8, 128, CH], BF16, kind="Internal").ap()
    qscr_hi = nc.dram_tensor("qscr_hi", [OC, 8, 128, CH], BF16, kind="Internal").ap()
    qscr_lo = nc.dram_tensor("qscr_lo", [OC, 8, 128, CH], BF16, kind="Internal").ap()
    vscr_hi = nc.dram_tensor("vscr_hi", [NCH, 4, 128, D], BF16, kind="Internal").ap()
    vscr_lo = nc.dram_tensor("vscr_lo", [NCH, 4, 128, D], BF16, kind="Internal").ap()

    dbg = {}
    if debug:
        for nm, shp, dt in [("xn_hi", [OC, 8, 128, CH], BF16), ("xn_lo", [OC, 8, 128, CH], BF16),
                            ("q_hi", [OC, 8, 128, CH], BF16), ("q_lo", [OC, 8, 128, CH], BF16),
                            ("k_hi", [NCH, 8, 128, CH], BF16),
                            ("v_hi", [NCH, 4, 128, D], BF16), ("v_lo", [NCH, 4, 128, D], BF16),
                            ("at_hi", [128, 8 * NQ], BF16), ("at_lo", [128, 8 * NQ], BF16),
                            ("wq1d", [8, 128, FF], BF16), ("wq2d", [32, 128, D], BF16)]:
            dbg[nm] = nc.dram_tensor("dbg_" + nm, shp, dt, kind="ExternalOutput").ap()

    SBTOP = 229312
    R0_SZ = 65536
    R1_SZ = 110592
    R0B = SBTOP - R0_SZ
    R1B = R0B - R1_SZ
    RQ_SZ = 24576
    RQB = R1B - RQ_SZ

    class Arena:
        def __init__(self, base, limit):
            self.off = base
            self.limit = limit

        def take(self, name, shape, dtype):
            nb = int(np.prod(shape[1:])) * mybir.dt.size(dtype)
            t = nc.alloc_sbuf_tensor_at(name, shape, dtype, offset=self.off)
            self.off += (nb + 63) // 64 * 64
            assert self.off <= self.limit, (name, self.off, self.limit)
            return t

    SB = nc.alloc_sbuf_tensor
    consts = SB("consts", [128, 195], F32)
    c16 = SB("c16", [16, 1152], F32)
    b1c = SB("b1cs", [128, 32], F32)
    btot = SB("btot", [1, D], F32)
    ls = SB("lss", [1, HH], F32)
    srow = SB("srow", [1, HH], F32)
    scols = SB("scols", [128, HH], F32)
    nscols = SB("nscols", [128, HH], F32)
    onesb = SB("onesb", [128, 1], BF16)
    rcol1 = SB("rcol1", [128, 1], F32)
    rcol2 = SB("rcol2", [128, 1], F32)
    grow = SB("grow", [1, 1], F32)
    g2row = SB("g2row", [1, 1], F32)
    gacc = SB("gacc", [128, 42], F32)
    wt = nc.alloc_sbuf_tensor_at("wt", [128, FF], F32, offset=RQB)
    wqt = nc.alloc_sbuf_tensor_at("wqt", [128, FF], BF16, offset=RQB + 16384)

    r0a = Arena(R0B, SBTOP)
    attnT_hi = r0a.take("attnT_hi", [128, 8 * NQ], BF16)
    attnT_lo = r0a.take("attnT_lo", [128, 8 * NQ], BF16)
    r0b = Arena(R0B, SBTOP)
    xt = [r0b.take(f"xt{i}", [128, 8 * CH], F32) for i in range(2)]
    wbh = [r0b.take(f"wbh{i}", [128, 8 * CH], BF16) for i in range(2)]
    wbl = [r0b.take(f"wbl{i}", [128, 8 * CH], BF16) for i in range(2)]

    ra = Arena(R1B, R0B)
    sqb = ra.take("sqb", [128, 5 * CH], F32)
    t1b2 = [ra.take(f"t1b{i}", [128, CH], F32) for i in range(2)]
    xn32b2 = [ra.take(f"xn32b{i}", [128, CH], F32) for i in range(2)]
    xnh2 = [ra.take(f"xnh{i}", [128, 8 * CH], BF16) for i in range(2)]
    xnl2 = [ra.take(f"xnl{i}", [128, 8 * CH], BF16) for i in range(2)]
    p32 = [ra.take(f"p32_{i}", [128, CH], F32) for i in range(12)]
    kstg_h = ra.take("kstg_h", [128, 2 * CH], BF16)
    kstg_l = ra.take("kstg_l", [128, 2 * CH], BF16)
    vstg_h = ra.take("vstg_h", [128, D], BF16)
    vstg_l = ra.take("vstg_l", [128, D], BF16)
    wbh3 = ra.take("wbh3", [128, 8 * CH], BF16)
    wbl3 = ra.take("wbl3", [128, 8 * CH], BF16)
    mrow = ra.take("mrow", [1, CH], F32)
    m2row = ra.take("m2row", [1, CH], F32)
    rstdrow = ra.take("rstdrow", [1, CH], F32)
    nrmall = ra.take("nrmall", [16, CH], F32)
    rnall = ra.take("rnall", [16, CH], F32)

    rb = Arena(R1B, R0B)
    krh = [rb.take(f"krh{i}", [128, N], BF16) for i in range(2)]
    krl = [rb.take(f"krl{i}", [128, N], BF16) for i in range(2)]
    vpk = [rb.take(f"vpk{i}", [128, 32 * 128], BF16) for i in range(2)]
    qpk = [rb.take(f"qpk{i}", [128, NQ], BF16) for i in range(2)]
    ahi = [rb.take(f"ahi{i}", [128, 1024], BF16) for i in range(4)]
    alo = [rb.take(f"alo{i}", [128, 1024], BF16) for i in range(4)]
    a32 = [rb.take(f"a32_{i}", [128, 1024], F32) for i in range(4)]
    av32 = rb.take("av32", [128, 512], F32)
    rdbc = rb.take("rdbc", [128, 512], F32)
    prod32 = rb.take("prod32", [128, 512], F32)
    lnrow = rb.take("lnrow", [1, 512], F32)
    rdrow = rb.take("rdrow", [1, 512], F32)

    rc_ = Arena(R1B, R0B)
    cxnh = [rc_.take(f"cxnh{i}", [128, 8 * CC], BF16) for i in range(2)]
    cxnl = [rc_.take(f"cxnl{i}", [128, 8 * CC], BF16) for i in range(2)]
    hh_sb = rc_.take("hh_sb", [128, 32 * CC], BF16)
    hl_sb = rc_.take("hl_sb", [128, 32 * CC], BF16)
    h32b = rc_.take("h32b", [128, 4 * CC], F32)
    wq1b = [rc_.take(f"wq1b{i}", [128, 8 * 128], BF16) for i in range(4)]
    wq2b = [rc_.take(f"wq2b{i}", [128, D], BF16) for i in range(4)]
    xob = [rc_.take(f"xob{i}", [128, 2 * D], F32) for i in range(2)]
    outb = [rc_.take(f"outb{i}", [128, 2 * D], F32) for i in range(2)]
    wob_h = [rc_.take(f"wob_h{i}", [128, D], BF16) for i in range(2)]
    wob_l = [rc_.take(f"wob_l{i}", [128, D], BF16) for i in range(2)]

    PS = nc.alloc_psum_tensor
    ps_wa = PS("ps_wa", [128, 1024], F32)
    ps_wb = PS("ps_wb", [128, 1024], F32)
    ps_wc = PS("ps_wc", [128, 1024], F32)
    ps_av = PS("ps_av", [128, 512], F32)
    ps_x = PS("ps_x", [128, 512], F32)

    est = ExitStack()
    sems = {name: est.enter_context(nc.semaphore(name))
            for name in ["dc", "dx0", "dx1", "dwsp0", "dwsp1", "dwb0", "dwb1", "dwb2", "dstg", "dvsp0",
                         "dxnp", "dkv0", "dkv1", "dkw0", "dkw1", "dcx0", "dcx1", "dxo0", "dxo1",
                         "dw10", "dw11", "dw12", "dw13", "dw20", "dw21", "dw22", "dw23", "dwob0", "dwob1", "dout0", "dout1",
                         "dg", "dq", "t", "s", "v", "g"]}

    onescol = consts[:, 0:1]
    cmean = consts[:, 1:2]
    Fc = consts[:, 2:66]

    def Et(t):
        return consts[:, 66 + 16 * t: 82 + 16 * t]

    def Ft(t):
        return c16[:, 128 * t: 128 * (t + 1)]

    ones_r = c16[0:1, 1024:1152]

    def r3(ap3):
        return ap3.transpose([1, 0, 2])

    def walk(em: Em):
        qk_slots = [ps_wa[:, 0:512], ps_wa[:, 512:1024], ps_wb[:, 0:512],
                    ps_wb[:, 512:1024], ps_wc[:, 0:512]]
        slot_free = [0] * 5
        slot_free_s = [0] * 5

        for (sb, ap) in [(consts, consts_in), (c16, c16_in), (b1c, b1c_in),
                         (ls, ls_in)]:
            em.dma("y", sb[:], ap, "dc")
        ck_c0 = em.cnt["dc"]

        em.dma("y", xt[0][0:1, 0:D], brow_out, "dx0")
        ck_br = em.dma("y", xt[0][0:1, D:2 * D], brow2, "dx0")
        em.wait("v", "dx0", ck_br)
        em.do("v", lambda e: e.tensor_add(out=btot[:], in0=xt[0][0:1, 0:D], in1=xt[0][0:1, D:2 * D]), inc="v")
        ck_btot = em.cnt["v"]

        ck_w_ld = [0, 0]
        ck_w_v = [ck_btot, 0]
        ck_wspill = [0, 0]
        for dt in range(8):
            b = dt % 2
            em.wait("y", "v", ck_w_v[b])
            ck_w_ld[b] = em.dma("y", xt[b][:, 0:3 * D], w_qkv[dt * 128:(dt + 1) * 128, :], f"dx{b}")
            em.wait("v", f"dx{b}", ck_w_ld[b])
            em.wait("v", f"dwsp{b}", ck_wspill[b])
            em.do("v", lambda e, b=b: e.tensor_copy(out=wbh[b][:, 0:3 * D], in_=xt[b][:, 0:3 * D]), inc="v")
            em.selfw("v")
            em.do("v", lambda e, b=b: e.scalar_tensor_tensor(
                out=wbl[b][:, 0:3 * D], in0=xt[b][:, 0:3 * D], scalar=1.0,
                in1=wbh[b][:, 0:3 * D], op0=ALU.mult, op1=ALU.subtract), inc="v")
            ck_w_v[b] = em.cnt["v"]
            em.wait("y", "v", ck_w_v[b])
            em.dma("y", wscr_hi[dt], wbh[b][:, 0:3 * D], f"dwsp{b}")
            ck_wspill[b] = em.dma("y", wscr_lo[dt], wbl[b][:, 0:3 * D], f"dwsp{b}")

        for dt in range(8):
            b = dt % 2
            em.wait("y", "v", ck_w_v[b])
            ck_w_ld[b] = em.dma("y", xt[b][:, 0:D], w_out[dt * 128:(dt + 1) * 128, :], f"dx{b}")
            em.wait("v", f"dx{b}", ck_w_ld[b])
            em.wait("v", f"dwsp{b}", ck_wspill[b])
            em.do("v", lambda e, b=b: e.tensor_copy(out=wbh[b][:, 0:D], in_=xt[b][:, 0:D]), inc="v")
            em.selfw("v")
            em.do("v", lambda e, b=b: e.scalar_tensor_tensor(
                out=wbl[b][:, 0:D], in0=xt[b][:, 0:D], scalar=1.0,
                in1=wbh[b][:, 0:D], op0=ALU.mult, op1=ALU.subtract), inc="v")
            ck_w_v[b] = em.cnt["v"]
            em.wait("y", "v", ck_w_v[b])
            em.dma("y", woscr[0, dt], wbh[b][:, 0:D], f"dwsp{b}")
            ck_wspill[b] = em.dma("y", woscr[1, dt], wbl[b][:, 0:D], f"dwsp{b}")
        ck_wsp_tot = [em.cnt["dwsp0"], em.cnt["dwsp1"]]

        em.wait("v", "dc", ck_c0)
        em.do("v", lambda e: e.tensor_scalar_min(out=srow[:], in0=ls[:], scalar1=LOG100), inc="v")
        ck_ls = em.cnt["v"]
        em.wait("s", "v", ck_ls)
        em.do("s", lambda e: e.activation(out=srow[:], in_=srow[:], func=AF.Exp), inc="s")
        ck_s0 = em.cnt["s"]
        em.wait("t", "s", ck_s0)
        em.wait("t", "dc", ck_c0)
        for h in range(HH):
            em.do("t", lambda e, h=h: e.matmul(ps_x[:, h:h + 1], ones_r, srow[0:1, h:h + 1],
                                               start=True, stop=True),
                  inc=("t" if h == HH - 1 else None))
        ck_t0 = em.cnt["t"]
        em.wait("v", "t", ck_t0)
        em.do("v", lambda e: e.tensor_copy(out=onesb[:], in_=consts[:, 0:1]), inc="v")
        em.do("v", lambda e: e.tensor_copy(out=scols[:], in_=ps_x[:, 0:HH]), inc="v")
        em.selfw("v")
        em.do("v", lambda e: e.tensor_scalar_mul(out=nscols[:], in0=scols[:], scalar1=-1.0), inc="v")
        ck_scols = em.cnt["v"]
        ps_x_free_v = ck_scols

        gw_ck_s = [0]
        gw_ck_dg = [0]

        def gamma_tile(i):
            src_ap = W1T[i * 128:(i + 1) * 128, :] if i < 8 else W2T[(i - 8) * 128:(i - 7) * 128, :]
            width = FF if i < 8 else D
            em.wait("g", "s", gw_ck_s[0])
            gw_ck_dg[0] = em.dma("g", wt[:, 0:width], src_ap, "dg")
            em.wait("s", "dg", gw_ck_dg[0])
            em.do("s", lambda e, i=i, width=width: e.activation(
                out=wqt[:, 0:width], in_=wt[:, 0:width], func=AF.Abs,
                accum_out=gacc[:, i:i + 1]), inc="s")
            gw_ck_s[0] = em.cnt["s"]

        ck_xt_free = [em.cnt["v"], em.cnt["v"]]
        ck_xt = [0, 0]
        wb_i = [0]
        ck_wb_ld = [0, 0, 0]
        ck_wb_free_t = [0, 0, 0]
        wbhs = [wbh[0], wbh[1], wbh3]
        wbls = [wbl[0], wbl[1], wbl3]
        ck_kstg_sp = 0
        ck_vstg_sp = [0, 0]
        ck_xn_sp = 0
        ck_sqb_t = [0]
        ck_xn_v = [0, 0]
        ck_xn_done_t = [0, 0]
        ck_psz_t = 0
        ck_ss_free_s = [0, 0]
        ck_psav_v = 0
        ck_rnbcq_v = 0
        si = [0]

        def load_xT(c):
            bx = c % 2
            em.wait("y", "v", ck_xt_free[bx])
            ck_xt[bx] = em.dma("y", xt[bx][:].rearrange("p (a j) -> p a j", a=8),
                               r3(xT.rearrange("(a p) n -> a p n", p=128)[:, :, c * CH:(c + 1) * CH]),
                               f"dx{bx}")

        def header(c):
            nonlocal ck_xn_sp
            bx = c % 2
            xnh, xnl = xnh2[bx], xnl2[bx]
            em.wait("v", f"dx{bx}", ck_xt[bx])
            em.wait("v", "t", max(ck_sqb_t[0], ck_psz_t))
            em.selfw("v")
            for dc in range(8):
                dst = (sqb[:, dc * CH:(dc + 1) * CH] if dc < 5 else
                       [t1b2[0], t1b2[1], xn32b2[0]][dc - 5][:])
                em.do("v", lambda e, dc=dc, bx=bx, dst=dst: e.tensor_mul(
                    out=dst, in0=xt[bx][:, dc * CH:(dc + 1) * CH],
                    in1=xt[bx][:, dc * CH:(dc + 1) * CH]), inc=("v" if dc == 7 else None))
            sq_ck = em.cnt["v"]
            em.wait("t", f"dx{bx}", ck_xt[bx])
            em.wait("t", "v", ck_rnbcq_v)
            em.wait("t", "s", ck_ss_free_s[0])
            for dc in range(8):
                em.do("t", lambda e, dc=dc, bx=bx: e.matmul(
                    ps_x[0:1, 0:CH], cmean, xt[bx][:, dc * CH:(dc + 1) * CH],
                    start=(dc == 0), stop=(dc == 7)), inc=("t" if dc == 7 else None))
            em.wait("t", "v", sq_ck)
            em.wait("t", "v", ck_psav_v)
            for dc in range(8):
                s_ap = (sqb[:, dc * CH:(dc + 1) * CH] if dc < 5 else
                        [t1b2[0], t1b2[1], xn32b2[0]][dc - 5][:])
                em.do("t", lambda e, dc=dc, s_ap=s_ap: e.matmul(
                    ps_av[0:1, 0:CH], cmean, s_ap,
                    start=(dc == 0), stop=(dc == 7)), inc=("t" if dc == 7 else None))
            ck_sqb_t[0] = em.cnt["t"]
            ck_stats_t = em.cnt["t"]
            em.wait("v", "t", ck_stats_t)
            em.do("v", lambda e: e.tensor_copy(out=mrow[:], in_=ps_x[0:1, 0:CH]), inc="v")
            em.selfw("v")
            em.do("v", lambda e: e.tensor_mul(out=m2row[:], in0=mrow[:], in1=mrow[:]), inc="v")
            em.selfw("v")
            em.do("v", lambda e: e.scalar_tensor_tensor(out=rstdrow[:], in0=ps_av[0:1, 0:CH], scalar=1.0,
                                                        in1=m2row[:], op0=ALU.mult, op1=ALU.subtract), inc="v")
            ck_var = em.cnt["v"]
            em.wait("s", "v", ck_var)
            em.do("s", lambda e: e.activation(out=rstdrow[:], in_=rstdrow[:], func=AF.Sqrt, bias=consts[0:1, 194:195]), inc="s")
            ck_sqt = em.cnt["s"]
            em.wait("v", "s", ck_sqt)
            em.do("v", lambda e: e.reciprocal(out=m2row[:], in_=rstdrow[:]), inc="v")
            ck_rstd = em.cnt["v"]
            em.wait("t", "v", ck_rstd)
            em.do("t", lambda e: e.matmul(ps_x[:, 0:CH], ones_r, mrow[:], start=True, stop=True), inc="t")
            em.do("t", lambda e: e.matmul(ps_av[:, 0:CH], ones_r, m2row[:], start=True, stop=True), inc="t")
            ck_bc = em.cnt["t"]
            em.wait("v", "t", ck_bc)
            em.wait("v", "dxnp", ck_xn_sp)
            em.wait("v", "t", ck_xn_done_t[bx])
            for dc in range(8):
                pb = dc % 2
                em.selfw("v")
                em.do("v", lambda e, dc=dc, bx=bx, pb=pb: e.scalar_tensor_tensor(
                    out=t1b2[pb][:], in0=xt[bx][:, dc * CH:(dc + 1) * CH], scalar=1.0,
                    in1=ps_x[:, 0:CH], op0=ALU.mult, op1=ALU.subtract), inc="v")
                em.selfw("v")
                em.do("v", lambda e, pb=pb: e.tensor_mul(out=xn32b2[pb][:], in0=t1b2[pb][:],
                                                         in1=ps_av[:, 0:CH]), inc="v")
                ckv = em.cnt["v"]
                em.wait("s", "v", ckv)
                em.wait("s", "dxnp", ck_xn_sp)
                em.do("s", lambda e, dc=dc, pb=pb, xnh=xnh: e.activation(out=xnh[:, dc * CH:(dc + 1) * CH],
                                                                         in_=xn32b2[pb][:], func=AF.Copy), inc="s")
                cks = em.cnt["s"]
                em.wait("v", "s", cks)
                em.do("v", lambda e, dc=dc, pb=pb, xnh=xnh, xnl=xnl: e.scalar_tensor_tensor(
                    out=xnl[:, dc * CH:(dc + 1) * CH], in0=xn32b2[pb][:], scalar=1.0,
                    in1=xnh[:, dc * CH:(dc + 1) * CH], op0=ALU.mult, op1=ALU.subtract), inc="v")
            ck_xn_v[bx] = em.cnt["v"]
            ck_xt_free[bx] = em.cnt["v"]
            if c < OC:
                em.wait("s", "v", ck_xn_v[bx])
                em.dma("s", r3(xnscr_hi[c]), xnh[:].rearrange("p (a j) -> p a j", a=8), "dxnp")
                ck_xn_sp = em.dma("s", r3(xnscr_lo[c]), xnl[:].rearrange("p (a j) -> p a j", a=8), "dxnp")
                if debug:
                    em.dma("s", r3(dbg["xn_hi"][c]), xnh[:].rearrange("p (a j) -> p a j", a=8), "dxnp")
                    ck_xn_sp = em.dma("s", r3(dbg["xn_lo"][c]), xnl[:].rearrange("p (a j) -> p a j", a=8), "dxnp")

        def stream_w(cg):
            b = wb_i[0] % 3
            wb_i[0] += 1
            em.wait("y", "t", ck_wb_free_t[b])
            em.wait("y", "dwsp0", ck_wsp_tot[0])
            em.wait("y", "dwsp1", ck_wsp_tot[1])
            em.dma("y", wbhs[b][:].rearrange("p (a j) -> p a j", a=8),
                   r3(wscr_hi[:, :, cg * 512:(cg + 1) * 512]), f"dwb{b}")
            ck_wb_ld[b] = em.dma("y", wbls[b][:].rearrange("p (a j) -> p a j", a=8),
                                 r3(wscr_lo[:, :, cg * 512:(cg + 1) * 512]), f"dwb{b}")
            return b

        pending_ss = [None]
        ck_ss_done = [0, 0]

        def flush_ss():
            nonlocal ck_psz_t
            if pending_ss[0] is None:
                return
            t, ss_ap, kind, ckv, par = pending_ss[0]
            pending_ss[0] = None
            em.wait("t", "v", ckv)
            if t == 0:
                em.wait("t", "s", ck_ss_free_s[kind])
            em.do("t", lambda e, t=t, ss_ap=ss_ap, par=par: e.matmul(
                ss_ap, Et(t), sqb[:, par * CH:(par + 1) * CH],
                start=(t == 0), stop=(t == 7)), inc="t")
            ck_psz_t = em.cnt["t"]
            if t == 7:
                ck_ss_done[kind] = ck_psz_t

        def qk_group(c, bw, base_t, pbase, ss_ap, kind):
            nonlocal ck_psz_t
            bx = c % 2
            xnh, xnl = xnh2[bx], xnl2[bx]
            for i in range(4):
                t = base_t + i
                pi = (pbase + t) % 12
                slot = si[0] % 5
                si[0] += 1
                em.wait("t", f"dwb{bw}", ck_wb_ld[bw])
                em.wait("t", "v", max(slot_free[slot], ck_xn_v[bx]))
                em.wait("t", "s", slot_free_s[slot])
                seq = [(dc, wsb, xsb) for dc in range(8)
                       for (wsb, xsb) in ((wbhs[bw], xnh), (wbhs[bw], xnl), (wbls[bw], xnh))]
                for j, (dc, wsb, xsb) in enumerate(seq):
                    last = j == len(seq) - 1
                    em.do("t", lambda e, wsb=wsb, xsb=xsb, dc=dc, i=i, slot=slot, j=j, last=last: e.matmul(
                        qk_slots[slot], wsb[:, dc * 512 + i * 128: dc * 512 + (i + 1) * 128],
                        xsb[:, dc * CH:(dc + 1) * CH], start=(j == 0), stop=last),
                        inc=("t" if last else None))
                ck_mm = em.cnt["t"]
                flush_ss()
                em.wait("v", "t", ck_mm)
                em.do("v", lambda e, pi=pi, slot=slot: e.tensor_copy(out=p32[pi][:], in_=qk_slots[slot]), inc="v")
                em.selfw("v")
                par = t % 2
                em.do("v", lambda e, pi=pi, par=par: e.tensor_mul(
                    out=sqb[:, par * CH:(par + 1) * CH], in0=p32[pi][:], in1=p32[pi][:]), inc="v")
                ckv = em.cnt["v"]
                slot_free[slot] = ckv
                pending_ss[0] = (t, ss_ap, kind, ckv, par)
            ck_wb_free_t[bw] = ck_mm

        def qk_normalize(scr_hi, scr_lo, dhi, dlo, pbase, ss_ap, kind, ck_sst, rn_ps, ck_rnps):
            nonlocal ck_psav_v, ck_rnbcq_v, ck_kstg_sp
            if pending_ss[0] is not None and pending_ss[0][2] == kind:
                flush_ss()
            em.wait("s", "t", ck_ss_done[kind])
            em.do("s", lambda e, ss_ap=ss_ap: e.activation(out=nrmall[:, 0:CH],
                                                           in_=ss_ap, func=AF.Sqrt), inc="s")
            ck_ss_free_s[kind] = em.cnt["s"]
            em.wait("v", "s", ck_ss_free_s[kind])
            em.do("v", lambda e: e.tensor_scalar_max(out=rnall[:, 0:CH], in0=nrmall[:, 0:CH],
                                                     scalar1=EPS_NRM), inc="v")
            em.selfw("v")
            em.do("v", lambda e: e.reciprocal(out=nrmall[:, 0:CH], in_=rnall[:, 0:CH]), inc="v")
            ck_rn = em.cnt["v"]
            rn_b = [ps_x[:, 0:CH], ps_av[:, 0:CH]]
            for t in range(8):
                pb = t % 2
                em.wait("t", "v", max(ck_rn, rn_ck[pb]))
                em.do("t", lambda e, t=t, pb=pb: e.matmul(rn_b[pb], Ft(t), nrmall[:, 0:CH],
                                                          start=True, stop=True), inc="t")
                ckt = em.cnt["t"]
                em.wait("v", "t", ckt)
                em.wait("v", "dstg", ck_kstg_sp)
                em.selfw("v")
                em.do("v", lambda e, t=t, pb=pb, pbase=pbase: e.tensor_mul(
                    out=t1b2[pb][:], in0=p32[(pbase + t) % 12][:],
                    in1=rn_b[pb]), inc="v")
                ckv = em.cnt["v"]
                rn_ck[pb] = ckv
                em.wait("s", "v", ckv)
                em.wait("s", "dstg", ck_kstg_sp)
                em.do("s", lambda e, t=t, pb=pb: e.activation(out=kstg_h[:, (t % 2) * CH:(t % 2 + 1) * CH],
                                                              in_=t1b2[pb][:], func=AF.Copy), inc="s")
                cks = em.cnt["s"]
                em.wait("v", "s", cks)
                em.do("v", lambda e, t=t, pb=pb: e.scalar_tensor_tensor(
                    out=kstg_l[:, (t % 2) * CH:(t % 2 + 1) * CH], in0=t1b2[pb][:], scalar=1.0,
                    in1=kstg_h[:, (t % 2) * CH:(t % 2 + 1) * CH], op0=ALU.mult, op1=ALU.subtract), inc="v")
                if t % 2 == 1:
                    half = t - 1
                    ck_stg = em.cnt["v"]
                    em.wait("s", "v", ck_stg)
                    em.dma("s", r3(scr_hi[half:half + 2]), kstg_h[:].rearrange("p (a j) -> p a j", a=2), "dstg")
                    ck_kstg_sp = em.dma("s", r3(scr_lo[half:half + 2]), kstg_l[:].rearrange("p (a j) -> p a j", a=2), "dstg")
                    if debug and dhi is not None:
                        ck_kstg_sp = em.dma("s", r3(dhi[half:half + 2]), kstg_h[:].rearrange("p (a j) -> p a j", a=2), "dstg")
                    if debug and dlo is not None:
                        ck_kstg_sp = em.dma("s", r3(dlo[half:half + 2]), kstg_l[:].rearrange("p (a j) -> p a j", a=2), "dstg")
            ck_rnbcq_v = max(rn_ck)
            ck_psav_v = max(rn_ck)

        def v_groups(c, b4, b5):
            bx = c % 2
            xnh, xnl = xnh2[bx], xnl2[bx]
            for st in range(4):
                sl = 0
                for cg in (4, 5):
                    bw = b4 if cg == 4 else b5
                    vcg = cg - 4
                    slot = si[0] % 5
                    si[0] += 1
                    em.wait("t", f"dwb{bw}", ck_wb_ld[bw])
                    em.wait("t", "v", max(slot_free[slot], ck_xn_v[bx]))
                    em.wait("t", "s", slot_free_s[slot])
                    seq = [(dc, xsb, wsb) for dc in range(8)
                           for (xsb, wsb) in ((xnh, wbhs[bw]), (xnl, wbhs[bw]), (xnh, wbls[bw]))]
                    for j, (dc, xsb, wsb) in enumerate(seq):
                        last = j == len(seq) - 1
                        em.do("t", lambda e, xsb=xsb, wsb=wsb, dc=dc, st=st, slot=slot, j=j, last=last: e.matmul(
                            qk_slots[slot], xsb[:, dc * CH + st * 128: dc * CH + (st + 1) * 128],
                            wsb[:, dc * 512:(dc + 1) * 512], start=(j == 0), stop=last),
                            inc=("t" if last else None))
                    ck_mm = em.cnt["t"]
                    em.wait("s", "t", ck_mm)
                    em.wait("s", "dvsp0", ck_vstg_sp[0])
                    em.do("s", lambda e, vcg=vcg, slot=slot: e.activation(
                        out=vstg_h[:, vcg * 512:(vcg + 1) * 512],
                        in_=qk_slots[slot], func=AF.Copy), inc="s")
                    cks = em.cnt["s"]
                    em.wait("v", "s", cks)
                    em.wait("v", "dvsp0", ck_vstg_sp[0])
                    em.do("v", lambda e, vcg=vcg, slot=slot: e.scalar_tensor_tensor(
                        out=vstg_l[:, vcg * 512:(vcg + 1) * 512],
                        in0=qk_slots[slot], scalar=1.0,
                        in1=vstg_h[:, vcg * 512:(vcg + 1) * 512],
                        op0=ALU.mult, op1=ALU.subtract), inc="v")
                    slot_free[slot] = em.cnt["v"]
                    slot_free_s[slot] = cks
                    ck_wb_free_t[bw] = ck_mm
                ckv = em.cnt["v"]
                cks = em.cnt["s"]
                em.wait("s", "v", ckv)
                em.dma("s", vscr_hi[c, st], vstg_h[:, 0:D], "dvsp0")
                ck_vstg_sp[0] = em.dma("s", vscr_lo[c, st], vstg_l[:, 0:D], "dvsp0")
                if debug:
                    em.dma("s", dbg["v_hi"][c, st], vstg_h[:, 0:D], "dvsp0")
                    ck_vstg_sp[0] = em.dma("s", dbg["v_lo"][c, st], vstg_l[:, 0:D], "dvsp0")

        rn_ck = [0, 0]
        load_xT(0)
        header(0)
        nb = [stream_w(0), stream_w(1)]
        for c in range(NCH):
            own = c < OC
            bx = c % 2
            if c + 1 < NCH:
                load_xT(c + 1)
            if own:
                qk_group(c, nb[0], 0, 0, ps_x[0:16, 0:CH], 0)
                b2 = stream_w(2)
                qk_group(c, nb[1], 4, 0, ps_x[0:16, 0:CH], 0)
                ck_ss_q = ck_psz_t
                b3 = stream_w(3)
                qk_group(c, b2, 0, 8, ps_wc[0:16, 512:1024], 1)
                qk_normalize(qscr_hi[c], qscr_lo[c],
                             dbg["q_hi"][c] if debug else None,
                             dbg["q_lo"][c] if debug else None, 0,
                             ps_x[0:16, 0:CH], 0, ck_ss_q, None, None)
                b4 = stream_w(4)
                qk_group(c, b3, 4, 8, ps_wc[0:16, 512:1024], 1)
                ck_ss_k = ck_psz_t
                b5 = stream_w(5)
                if c + 1 < NCH:
                    nb0 = stream_w(0 if (c + 1) < OC else 2)
                qk_normalize(kscr_hi[c], kscr_lo[c],
                             dbg["k_hi"][c] if debug else None, None, 8,
                             ps_wc[0:16, 512:1024], 1, ck_ss_k, None, None)
            else:
                qk_group(c, nb[0], 0, 8, ps_wc[0:16, 512:1024], 1)
                b4 = stream_w(4)
                qk_group(c, nb[1], 4, 8, ps_wc[0:16, 512:1024], 1)
                ck_ss_k = ck_psz_t
                b5 = stream_w(5)
                if c + 1 < NCH:
                    nb0 = stream_w(0 if (c + 1) < OC else 2)
                qk_normalize(kscr_hi[c], kscr_lo[c],
                             dbg["k_hi"][c] if debug else None, None, 8,
                             ps_wc[0:16, 512:1024], 1, ck_ss_k, None, None)
            if c + 1 < NCH:
                header(c + 1)
            v_groups(c, b4, b5)
            ck_xn_done_t[bx] = em.cnt["t"]
            if c + 1 < NCH:
                nb = [nb0, stream_w(1 if (c + 1) < OC else 3)]
            for gi in range(5 * c, min(5 * (c + 1), 40)):
                gamma_tile(gi)

        ck_A_stg = em.cnt["dstg"]
        ck_A_vsp = [em.cnt["dvsp0"], 0]
        ck_A_xnp = em.cnt["dxnp"]
        ck_gamma_s = gw_ck_s[0]
        em.wait("v", "s", ck_gamma_s)
        em.do("v", lambda e: e.tensor_reduce(out=gacc[:, 40:41], in_=gacc[:, 0:8],
                                             axis=AX.X, op=ALU.add), inc="v")
        em.do("v", lambda e: e.tensor_reduce(out=gacc[:, 41:42], in_=gacc[:, 8:40],
                                             axis=AX.X, op=ALU.add), inc="v")
        ck_g1 = em.cnt["v"]
        ck_g2 = ck_g1
        ps_x_free_s = 0

        def gamma_finish(ck_g, gfin, nelem, rc):
            nonlocal ps_x_free_v, ps_x_free_s
            em.wait("t", "v", ck_g)
            em.wait("t", "v", ps_x_free_v)
            em.do("t", lambda e: e.matmul(ps_x[0:1, 0:1], gacc[:, gfin:gfin + 1], onescol,
                                          start=True, stop=True), inc="t")
            ckt = em.cnt["t"]
            em.wait("v", "t", ckt)
            em.do("v", lambda e: e.tensor_scalar(out=grow[:], in0=ps_x[0:1, 0:1],
                                                 scalar1=1.0 / nelem, scalar2=EPS_Q,
                                                 op0=ALU.mult, op1=ALU.add), inc="v")
            em.selfw("v")
            em.do("v", lambda e: e.reciprocal(out=g2row[:], in_=grow[:]), inc="v")
            ckv = em.cnt["v"]
            em.wait("t", "v", ckv)
            em.do("t", lambda e: e.matmul(ps_x[:, 1:2], ones_r, g2row[:], start=True, stop=True), inc="t")
            ckt2 = em.cnt["t"]
            em.wait("v", "t", ckt2)
            em.do("v", lambda e, rc=rc: e.tensor_copy(out=rc[:], in_=ps_x[:, 1:2]), inc="v")
            ps_x_free_v = em.cnt["v"]
            return em.cnt["v"]

        ck_rc1 = gamma_finish(ck_g1, 40, FF * D, rcol1)
        ck_rc2 = gamma_finish(ck_g2, 41, FF * D, rcol2)

        q2_ld = [0]
        q2_sp = [0]
        q2_v = [0]

        def qt_sel(i):
            if i < 8:
                return W1T[i * 128:(i + 1) * 128, :], FF, wq1s, rcol1, ck_rc1, "wq1d", i
            return W2T[(i - 8) * 128:(i - 7) * 128, :], D, wq2s, rcol2, ck_rc2, "wq2d", i - 8

        def qt_load(i):
            src_ap, width, dst, rc, ck_rc, dkey, ti = qt_sel(i)
            if i == 0:
                em.wait("y", "s", gw_ck_s[0])
            em.wait("y", "v", q2_v[0])
            em.wait("y", "dq", q2_sp[0])
            q2_ld[0] = em.dma("y", wt[:, 0:width], src_ap, "dq")

        def quant2_tile(i):
            src_ap, width, dst, rc, ck_rc, dkey, ti = qt_sel(i)
            em.wait("v", "dq", q2_ld[0])
            em.wait("v", "v", ck_rc)
            em.selfw("v")
            em.do("v", lambda e, rc=rc, width=width: e.tensor_scalar(
                out=wqt[:, 0:width], in0=wt[:, 0:width], scalar1=rc[:],
                scalar2=0.5, op0=ALU.mult, op1=ALU.is_gt), inc="v")
            em.selfw("v")
            em.do("v", lambda e, rc=rc, width=width: e.tensor_scalar(
                out=wt[:, 0:width], in0=wt[:, 0:width], scalar1=rc[:],
                scalar2=-0.5, op0=ALU.mult, op1=ALU.is_lt), inc="v")
            em.selfw("v")
            em.do("v", lambda e, width=width: e.tensor_sub(out=wqt[:, 0:width], in0=wqt[:, 0:width],
                                                           in1=wt[:, 0:width]), inc="v")
            q2_v[0] = em.cnt["v"]
            em.wait("y", "v", q2_v[0])
            q2_sp[0] = em.dma("y", dst[ti], wqt[:, 0:width], "dq")
            if debug:
                q2_sp[0] = em.dma("y", dbg[dkey][ti], wqt[:, 0:width], "dq")
            if i + 1 < 40:
                qt_load(i + 1)

        sB = em.cnt["s"]
        vB = em.cnt["v"]
        ck_kv_ld = [0, 0]
        ck_kv_free_t = [0, 0]
        ck_sgrp_s = [sB, sB, sB]
        ck_exp_s = [sB, sB, sB, sB]
        ck_sgrp_v = [vB, vB, vB, vB]
        ck_ab_t = [0, 0, 0, 0]
        ck_av_v = vB
        ck_dn_t = 0
        ck_dn_s = sB
        ck_rb_t = 0
        ck_rb_v = vB
        ck_fold_v = vB
        ck_fold_s = sB
        ck_at_s = 0
        ck_at_v = 0

        em.wait("s", "v", ck_scols)

        def load_head_y(h):
            bh = h % 2
            ct, hb = h // 2, h % 2
            em.wait("y", "t", ck_kv_free_t[bh])
            em.wait("y", "dstg", ck_A_stg)
            em.wait("y", "dvsp0", ck_A_vsp[0])
            em.wait("y", "dxnp", ck_A_xnp)
            sem = f"dkv{bh}"
            kin = r3(kscr_hi[:, ct, hb * 64:(hb + 1) * 64, :])
            em.dma("y", krh[bh][0:64, :].rearrange("p (a j) -> p a j", a=8), kin, sem)
            em.dma("y", krh[bh][64:128, :].rearrange("p (a j) -> p a j", a=8), kin, sem)
            ck = em.dma("y", qpk[bh][0:64, :].rearrange("p (a j) -> p a j", a=4),
                        r3(qscr_hi[:, ct, hb * 64:(hb + 1) * 64, :]), sem)
            return ck

        def load_head_s(h):
            bh = h % 2
            ct, hb = h // 2, h % 2
            em.wait("s", "t", ck_kv_free_t[bh])
            em.wait("s", "dstg", ck_A_stg)
            em.wait("s", "dvsp0", ck_A_vsp[0])
            em.wait("s", "dxnp", ck_A_xnp)
            sem2 = f"dkw{bh}"
            kil = r3(kscr_lo[:, ct, hb * 64:(hb + 1) * 64, :])
            em.dma("s", krl[bh][0:64, :].rearrange("p (a j) -> p a j", a=8), kil, sem2)
            em.dma("s", krl[bh][64:128, :].rearrange("p (a j) -> p a j", a=8), kil, sem2)
            ck_kw_ld[bh] = em.dma("s", qpk[bh][64:128, :].rearrange("p (a j) -> p a j", a=4),
                                  r3(qscr_lo[:, ct, hb * 64:(hb + 1) * 64, :]), sem2)
            em.wait("g", "t", ck_kv_free_t[bh])
            em.wait("g", "dstg", ck_A_stg)
            em.wait("g", "dvsp0", ck_A_vsp[0])
            em.wait("g", "dxnp", ck_A_xnp)
            vv = vpk[bh][:].rearrange("p (c st x d) -> p c st x d", c=8, st=4, x=2, d=64)
            em.wait("g", "dg", em.cnt["dg"])
            em.dma("g", vv[:, :, :, 0, :],
                   vscr_hi[:, :, :, h * 64:(h + 1) * 64].transpose([2, 0, 1, 3]), "dg")
            em.wait("g", "dg", em.cnt["dg"])
            ck_vpl_ld[bh] = em.dma("g", vv[:, :, :, 1, :],
                                   vscr_lo[:, :, :, h * 64:(h + 1) * 64].transpose([2, 0, 1, 3]), "dg")

        pending_tail = [None]
        ck_rb_t = 0
        ck_kw_ld = [0, 0]
        ck_vpl_ld = [0, 0]
        ck_kv_ld[0] = load_head_y(0)
        load_head_s(0)
        for h in range(HH):
            bh = h % 2
            ct, hb = h // 2, h % 2
            if h + 1 < HH:
                ck_kv_ld[(h + 1) % 2] = load_head_y(h + 1)
            for qc in range(4):
                slot_i = h * 4 + qc
                if slot_i < 40:
                    if slot_i == 0:
                        qt_load(0)
                    quant2_tile(slot_i)

                def emit_av(g):
                    pr = g % 4
                    em.wait("t", "v", ck_sgrp_v[pr])
                    if g == 0:
                        em.wait("t", "v", ck_av_v)
                        em.wait("t", "s", ck_dn_s)
                    for kt in (2 * g, 2 * g + 1):
                        ah = (kt % 2)
                        em.do("t", lambda e, kt=kt, pr=pr, ah=ah, bh=bh: e.matmul(
                            ps_av[:], vpk[bh][:, kt * 128:(kt + 1) * 128],
                            ahi[pr][:, ah * 512:(ah + 1) * 512],
                            start=(kt == 0), stop=False), inc=None)
                        em.do("t", lambda e, kt=kt, pr=pr, ah=ah, bh=bh: e.matmul(
                            ps_av[:], vpk[bh][:, kt * 128:(kt + 1) * 128],
                            alo[pr][:, ah * 512:(ah + 1) * 512],
                            start=False, stop=(kt == 31)), inc=None)
                        em.do("t", lambda e, kt=kt, pr=pr, ah=ah: e.matmul(
                            ps_x[0:1, :], onesb[:], ahi[pr][:, ah * 512:(ah + 1) * 512],
                            start=(kt == 0), stop=False), inc=None)
                        em.do("t", lambda e, kt=kt, pr=pr, ah=ah: e.matmul(
                            ps_x[0:1, :], onesb[:], alo[pr][:, ah * 512:(ah + 1) * 512],
                            start=False, stop=(kt == 31)),
                            inc=("t" if kt == 2 * g + 1 else None))
                    ck_ab_t[pr] = em.cnt["t"]

                sseq = [0, 1, 0, 1, 2, 0, 1, 2, 0, 1, 2, 0, 1, 2, 0, 1]
                for g in range(16):
                    bk = sseq[g]
                    sg = [ps_wa, ps_wb, ps_wc][bk]
                    pr = g % 4
                    em.wait("t", f"dkv{bh}", ck_kv_ld[bh])
                    em.wait("t", f"dkw{bh}", ck_kw_ld[bh])
                    em.wait("t", "dg", ck_vpl_ld[bh])
                    em.wait("t", "s", ck_sgrp_s[bk])
                    if g == 2 and pending_tail[0] is not None:
                        pending_tail[0]()
                        pending_tail[0] = None
                    if g == 2 and qc == 0 and h + 1 < HH:
                        load_head_s(h + 1)
                    if g == 4:
                        em.wait("t", "v", ck_fold_v)
                        em.wait("t", "v", ck_rb_v)
                    for kt in (2 * g, 2 * g + 1):
                        half = sg[:, (kt % 2) * 512:(kt % 2 + 1) * 512]
                        em.do("t", lambda e, kt=kt, half=half, bh=bh, qc=qc: e.matmul(
                            half, krh[bh][:, kt * 128:(kt + 1) * 128],
                            qpk[bh][:, qc * 512:(qc + 1) * 512], start=True, stop=False), inc=None)
                        em.do("t", lambda e, kt=kt, half=half, bh=bh, qc=qc: e.matmul(
                            half, krl[bh][:, kt * 128:(kt + 1) * 128],
                            qpk[bh][:, qc * 512:(qc + 1) * 512], start=False, stop=True),
                            inc=("t" if kt == 2 * g + 1 else None))
                    ck_smm = em.cnt["t"]
                    em.wait("s", "t", ck_smm)
                    em.wait("s", "t", ck_ab_t[pr])
                    em.wait("s", "v", ck_sgrp_v[pr])
                    em.do("s", lambda e, sg=sg, pr=pr, h=h: e.activation(
                        out=ahi[pr][:], in_=sg[:], func=AF.Exp,
                        scale=scols[:, h:h + 1], bias=nscols[:, h:h + 1]), inc="s")
                    em.do("s", lambda e, sg=sg, pr=pr, h=h: e.activation(
                        out=a32[pr][:], in_=sg[:], func=AF.Exp,
                        scale=scols[:, h:h + 1], bias=nscols[:, h:h + 1]), inc="s")
                    ck_sgrp_s[bk] = em.cnt["s"]
                    ck_exp_s[pr] = em.cnt["s"]
                    em.wait("v", "s", ck_exp_s[pr])
                    em.do("v", lambda e, pr=pr: e.scalar_tensor_tensor(
                        out=alo[pr][:], in0=a32[pr][:], scalar=1.0, in1=ahi[pr][:],
                        op0=ALU.mult, op1=ALU.subtract), inc="v")
                    ck_sgrp_v[pr] = em.cnt["v"]
                    if g >= 2:
                        emit_av(g - 2)
                emit_av(14)
                emit_av(15)
                ck_dn_t = em.cnt["t"]
                em.wait("v", "t", ck_dn_t)
                em.wait("v", "t", ck_rb_t)
                em.do("v", lambda e: e.tensor_copy(out=av32[:], in_=ps_av[:]), inc="v")
                ck_av_v = em.cnt["v"]
                em.wait("s", "t", ck_dn_t)
                em.do("s", lambda e: e.activation(out=lnrow[:], in_=ps_x[0:1, :], func=AF.Ln), inc="s")
                em.selfw("s")
                em.do("s", lambda e: e.activation(out=rdrow[:], in_=lnrow[:], func=AF.Exp, scale=-1.0), inc="s")
                ck_dn_s = em.cnt["s"]
                def make_tail(hb=hb, ct=ct, qc=qc):
                    def tail():
                        nonlocal ck_rb_t, ck_rb_v, ck_fold_v, ck_at_s, ck_fold_s, ck_at_v
                        em.wait("t", "v", ck_av_v)
                        em.wait("t", "v", ck_fold_v)
                        em.wait("t", "s", ck_fold_s)
                        em.do("t", lambda e: e.matmul(ps_wc[hb * 64:(hb + 1) * 64, 0:512], Fc, av32[:],
                                                      start=True, stop=True), inc="t")
                        em.wait("t", "s", ck_dn_s)
                        em.wait("t", "v", ck_rb_v)
                        em.do("t", lambda e: e.matmul(ps_wc[hb * 64:(hb + 1) * 64, 512:1024],
                                                      ones_r[0:1, 0:64], rdrow[:],
                                                      start=True, stop=True), inc="t")
                        ck_rb_t = em.cnt["t"]
                        em.wait("v", "t", ck_rb_t)
                        em.wait("v", "s", ck_at_s)
                        em.selfw("v")
                        em.do("v", lambda e: e.tensor_copy(out=rdbc[hb * 64:(hb + 1) * 64, :],
                                                           in_=ps_wc[hb * 64:(hb + 1) * 64, 512:1024]), inc="v")
                        em.selfw("v")
                        em.do("v", lambda e: e.tensor_mul(out=prod32[hb * 64:(hb + 1) * 64, :],
                                                          in0=ps_wc[hb * 64:(hb + 1) * 64, 0:512],
                                                          in1=rdbc[hb * 64:(hb + 1) * 64, :]), inc="v")
                        ck_rb_v = em.cnt["v"]
                        ck_fold_v = em.cnt["v"]
                        em.wait("s", "v", ck_rb_v)
                        off = ct * NQ + qc * 512
                        em.do("s", lambda e: e.activation(
                            out=attnT_hi[hb * 64:(hb + 1) * 64, off:off + 512],
                            in_=prod32[hb * 64:(hb + 1) * 64, :], func=AF.Copy), inc="s")
                        ck_at_s = em.cnt["s"]
                        ck_fold_s = em.cnt["s"]
                        em.wait("v", "s", ck_at_s)
                        em.do("v", lambda e: e.scalar_tensor_tensor(
                            out=attnT_lo[hb * 64:(hb + 1) * 64, off:off + 512],
                            in0=prod32[hb * 64:(hb + 1) * 64, :], scalar=1.0,
                            in1=attnT_hi[hb * 64:(hb + 1) * 64, off:off + 512],
                            op0=ALU.mult, op1=ALU.subtract), inc="v")
                        ck_at_v = em.cnt["v"]
                    return tail

                pending_tail[0] = make_tail()
            ck_kv_free_t[bh] = em.cnt["t"]

        if pending_tail[0] is not None:
            pending_tail[0]()
            pending_tail[0] = None
        ck_wq2 = em.cnt["dq"]
        ck_attn_v = em.cnt["v"]
        ck_attn_s = em.cnt["s"]
        if debug:
            em.wait("y", "v", ck_attn_v)
            em.wait("y", "s", ck_attn_s)
            em.dma("y", dbg["at_hi"], attnT_hi[:], "dstg")
            em.dma("y", dbg["at_lo"], attnT_lo[:], "dstg")

        tC = em.cnt["t"]
        em.wait("t", "v", ck_attn_v)
        em.wait("t", "s", ck_attn_s)
        em.wait("y", "t", tC)
        em.wait("y", "v", ck_attn_v)
        em.wait("y", "s", ck_attn_s)
        em.wait("s", "t", tC)
        em.wait("s", "v", ck_attn_v)
        em.wait("s", "dc", ck_c0)
        vC = em.cnt["v"]
        sC = em.cnt["s"]
        ck_cx_ld = [0, 0]
        ck_cx_free_t = [0, 0]
        ck_xo_ld = [0, 0]
        ck_xo_free_v = [0, 0]
        ck_w1_ld = [0, 0, 0, 0]
        ck_w1_free_t = [0, 0, 0, 0]
        ck_w2_ld = [0, 0, 0, 0]
        ck_w2_free_t = [0, 0, 0, 0]
        ck_h_ps = [vC, vC, vC, vC]
        ck_h_ps_s = [sC, sC, sC, sC]
        ck_out_ps_v = [vC, vC]
        outreg = [[ps_wc[:, 0:512], ps_wc[:, 512:1024]], [ps_av[:, 0:512], ps_x[:, 0:512]]]
        ck_out_st = [0, 0]
        ck_wob_ld = [0, 0]
        ck_wob_free_t = [0, 0]

        def load_cxn(cc):
            bc = cc % 2
            oc, sub = cc // 2, cc % 2
            em.wait("y", "t", ck_cx_free_t[bc])
            em.dma("y", cxnh[bc][:].rearrange("p (a j) -> p a j", a=8),
                   r3(xnscr_hi[oc][:, :, sub * CC:(sub + 1) * CC]), f"dcx{bc}")
            ck_cx_ld[bc] = em.dma("y", cxnl[bc][:].rearrange("p (a j) -> p a j", a=8),
                                  r3(xnscr_lo[oc][:, :, sub * CC:(sub + 1) * CC]), f"dcx{bc}")

        def load_xo(cc):
            bc = cc % 2
            em.wait("s", "v", ck_xo_free_v[bc])
            ck_xo_ld[bc] = em.dma("s", xob[bc][:].rearrange("p (a j) -> p a j", a=2),
                                  r3(x_own[cc * CC:(cc + 1) * CC, :].rearrange("(a p) j -> a p j", p=128)), f"dxo{bc}")

        def load_w1(ft, cc):
            bw = ft % 4
            em.wait("s", "t", ck_w1_free_t[bw])
            if cc == 0 and ft < 4:
                em.wait("s", "dq", ck_wq2)
            ck_w1_ld[bw] = em.dma("s", wq1b[bw][:].rearrange("p (a j) -> p a j", a=8),
                                  r3(wq1s[:, :, ft * 128:(ft + 1) * 128]), f"dw1{bw}")

        def load_w2(fc, cc):
            bw = fc % 4
            em.wait("s", "t", ck_w2_free_t[bw])
            if cc == 0 and fc < 4:
                em.wait("s", "dq", ck_wq2)
            ck_w2_ld[bw] = em.dma("s", wq2b[bw][:], wq2s[fc], f"dw2{bw}")

        def load_wob(ic, cc, mt):
            bo = ic % 2
            em.wait("s", "t", ck_wob_free_t[bo])
            if cc == 0 and mt == 0 and ic < 2:
                em.wait("s", "dwsp0", ck_wsp_tot[0])
                em.wait("s", "dwsp1", ck_wsp_tot[1])
            em.dma("s", wob_h[bo][:], woscr[0, ic], f"dwob{bo}")
            ck_wob_ld[bo] = em.dma("s", wob_l[bo][:], woscr[1, ic], f"dwob{bo}")

        load_cxn(0)
        load_xo(0)
        for cc in range(NCC):
            bc = cc % 2
            for pre in range(3):
                load_w1(pre, cc)
            if cc + 1 < NCC:
                load_cxn(cc + 1)
                load_xo(cc + 1)
            hslots = [ps_wa[:, 0:512], ps_wa[:, 512:1024], ps_wb[:, 0:512], ps_wb[:, 512:1024]]
            for ft in range(32):
                bw = ft % 4
                hp4 = ft % 4
                hps = hslots[hp4]
                em.wait("t", f"dw1{bw}", ck_w1_ld[bw])
                em.wait("t", f"dcx{bc}", ck_cx_ld[bc])
                em.wait("t", "v", ck_h_ps[hp4])
                em.wait("t", "s", ck_h_ps_s[hp4])
                seq = [(dc, xsb) for dc in range(8) for xsb in (cxnh[bc], cxnl[bc])]
                for j, (dc, xsb) in enumerate(seq):
                    last = j == len(seq) - 1
                    em.do("t", lambda e, dc=dc, xsb=xsb, bw=bw, hps=hps, j=j, last=last: e.matmul(
                        hps[:, 0:CC], wq1b[bw][:, dc * 128:(dc + 1) * 128],
                        xsb[:, dc * CC:(dc + 1) * CC], start=(j == 0), stop=last),
                        inc=("t" if last else None))
                ck_mm = em.cnt["t"]
                ck_w1_free_t[bw] = ck_mm
                if ft + 3 < 32:
                    load_w1(ft + 3, cc)
                em.wait("s", "t", ck_mm)
                em.wait("s", "v", ck_h_ps[hp4])
                em.do("s", lambda e, ft=ft, hps=hps: e.activation(
                    out=hh_sb[:, ft * CC:(ft + 1) * CC], in_=hps[:, 0:CC],
                    func=AF.Relu, bias=b1c[:, ft:ft + 1]), inc="s")
                em.do("s", lambda e, ft=ft, hps=hps, hp4=hp4: e.activation(
                    out=h32b[:, hp4 * CC:(hp4 + 1) * CC], in_=hps[:, 0:CC],
                    func=AF.Relu, bias=b1c[:, ft:ft + 1]), inc="s")
                ck_h_ps_s[hp4] = em.cnt["s"]
                em.wait("v", "s", ck_h_ps_s[hp4])
                em.do("v", lambda e, ft=ft, hp4=hp4: e.scalar_tensor_tensor(
                    out=hl_sb[:, ft * CC:(ft + 1) * CC], in0=h32b[:, hp4 * CC:(hp4 + 1) * CC],
                    scalar=1.0, in1=hh_sb[:, ft * CC:(ft + 1) * CC],
                    op0=ALU.mult, op1=ALU.subtract), inc="v")
                ck_h_ps[hp4] = em.cnt["v"]
            ck_h_done_v = em.cnt["v"]

            for mt in range(2):
                em.wait("t", "v", ck_out_ps_v[mt])
                load_wob(0, cc, mt)
                for ic in range(8):
                    bo = ic % 2
                    if ic + 1 < 8:
                        load_wob(ic + 1, cc, mt)
                    em.wait("t", f"dwob{bo}", ck_wob_ld[bo])
                    lhi = attnT_hi[:, ic * NQ + cc * CC + mt * 128: ic * NQ + cc * CC + (mt + 1) * 128]
                    llo = attnT_lo[:, ic * NQ + cc * CC + mt * 128: ic * NQ + cc * CC + (mt + 1) * 128]
                    for nh in range(2):
                        rhi = wob_h[bo][:, nh * 512:(nh + 1) * 512]
                        rlo = wob_l[bo][:, nh * 512:(nh + 1) * 512]
                        for (lt, rt) in ((lhi, rhi), (llo, rhi), (lhi, rlo)):
                            last = (nh == 1 and rt is rlo)
                            em.do("t", lambda e, lt=lt, rt=rt, mt=mt, nh=nh, ic=ic: e.matmul(
                                outreg[mt][nh], lt, rt,
                                start=(ic == 0 and lt is lhi and rt is rhi), stop=False),
                                inc=("t" if last else None))
                    ck_wob_free_t[bo] = em.cnt["t"]
                for nh in range(2):
                    em.do("t", lambda e, mt=mt, nh=nh: e.matmul(
                        outreg[mt][nh], ones_r,
                        btot[0:1, nh * 512:(nh + 1) * 512], start=False, stop=False), inc=None)
            em.wait("t", "v", ck_h_done_v)
            for pre in range(3):
                load_w2(pre, cc)
            for fc in range(32):
                bw = fc % 4
                em.wait("t", f"dw2{bw}", ck_w2_ld[bw])
                for mt in range(2):
                    for nh in range(2):
                        for hsb in (hh_sb, hl_sb):
                            stop = (fc == 31 and hsb is hl_sb)
                            fcl = (mt == 1 and nh == 1 and hsb is hl_sb)
                            em.do("t", lambda e, fc=fc, mt=mt, nh=nh, hsb=hsb, bw=bw, stop=stop: e.matmul(
                                outreg[mt][nh],
                                hsb[:, fc * CC + mt * 128: fc * CC + (mt + 1) * 128],
                                wq2b[bw][:, nh * 512:(nh + 1) * 512],
                                start=False, stop=stop),
                                inc=("t" if fcl else None))
                ck_w2_free_t[bw] = em.cnt["t"]
                if fc + 3 < 32:
                    load_w2(fc + 3, cc)
            ck_ffn2 = em.cnt["t"]
            em.wait("v", "t", ck_ffn2)
            em.wait("v", f"dxo{bc}", ck_xo_ld[bc])
            em.wait("v", f"dout{bc}", ck_out_st[bc])
            for mt in range(2):
                for nh in range(2):
                    em.do("v", lambda e, mt=mt, nh=nh, bc=bc: e.tensor_add(
                        out=outb[bc][:, mt * D + nh * 512: mt * D + (nh + 1) * 512],
                        in0=xob[bc][:, mt * D + nh * 512: mt * D + (nh + 1) * 512],
                        in1=outreg[mt][nh]), inc="v")
                ck_out_ps_v[mt] = em.cnt["v"]
            ck_outb_v = em.cnt["v"]
            ck_xo_free_v[bc] = ck_outb_v
            em.wait("y", "v", ck_outb_v)
            ck_out_st[bc] = em.dma("y", y_out[cc * CC:(cc + 1) * CC, :].rearrange("(a p) j -> a p j", p=128).transpose([1, 0, 2]),
                                   outb[bc][:].rearrange("p (a j) -> p a j", a=2), f"dout{bc}")
            ck_cx_free_t[bc] = em.cnt["t"]

        for snm in ["dout0", "dout1", "dg", "dq", "dstg", "dvsp0", "dxnp",
                    "dx0", "dx1", "dwb0", "dwb1", "dwb2", "dkv0", "dkv1", "dkw0", "dkw1", "dcx0", "dcx1",
                    "dxo0", "dxo1", "dw10", "dw11", "dw20", "dw21", "dwob0", "dwob1", "dc", "dwsp0", "dwsp1"]:
            em.wait("y", snm, em.cnt[snm])

    with nc.Block() as block:
        @block.sync
        def _(eng):
            walk(Em("y", eng, sems))

        @block.tensor
        def _(eng):
            walk(Em("t", eng, sems))

        @block.scalar
        def _(eng):
            walk(Em("s", eng, sems))

        @block.vector
        def _(eng):
            walk(Em("v", eng, sems))

        @block.gpsimd
        def _(eng):
            walk(Em("g", eng, sems))

    return nc


def _prep_inputs(L0_attn_x, L0_attn_w_qkv, L0_attn_w_out, L0_attn_b_out,
                 L0_attn_logit_scale, L0_ffn_W1, L0_ffn_b1, L0_ffn_W2, L0_ffn_b2):
    x = np.ascontiguousarray(np.asarray(L0_attn_x, dtype=np.float32))
    w_qkv = np.ascontiguousarray(np.asarray(L0_attn_w_qkv, dtype=np.float32))
    w_out = np.ascontiguousarray(np.asarray(L0_attn_w_out, dtype=np.float32))
    b_out = np.asarray(L0_attn_b_out, dtype=np.float32)
    lsc = np.asarray(L0_attn_logit_scale, dtype=np.float32).reshape(1, HH)
    W1 = np.asarray(L0_ffn_W1, dtype=np.float32)
    b1 = np.asarray(L0_ffn_b1, dtype=np.float32)
    W2 = np.asarray(L0_ffn_W2, dtype=np.float32)
    b2 = np.asarray(L0_ffn_b2, dtype=np.float32)
    W1T = np.ascontiguousarray(W1.T)
    W2T = np.ascontiguousarray(W2.T)
    b1c = np.ascontiguousarray(b1.reshape(32, 128).T)
    consts, c16 = _host_consts()
    in_maps = []
    for core in range(8):
        b, half = core // 2, core % 2
        own = x[b, half * NQ:(half + 1) * NQ]
        other = x[b, (1 - half) * NQ:(2 - half) * NQ]
        xTc = np.ascontiguousarray(np.concatenate([own, other], axis=0).T)
        in_maps.append({
            "xT": xTc,
            "x_own": np.ascontiguousarray(own),
            "w_qkv": w_qkv,
            "w_out": w_out,
            "W1T": W1T,
            "W2T": W2T,
            "b1c": b1c,
            "brow_out": b_out.reshape(1, D),
            "brow2": b2.reshape(1, D),
            "ls": lsc,
            "consts_in": consts,
            "consts16_in": c16,
        })
    return in_maps


def kernel(L0_attn_x, L0_attn_w_qkv, L0_attn_w_out, L0_attn_b_out,
           L0_attn_logit_scale, L0_ffn_W1, L0_ffn_b1, L0_ffn_W2, L0_ffn_b2):
    in_maps = _prep_inputs(L0_attn_x, L0_attn_w_qkv, L0_attn_w_out, L0_attn_b_out,
                           L0_attn_logit_scale, L0_ffn_W1, L0_ffn_b1, L0_ffn_W2, L0_ffn_b2)
    nc = build_nc(debug=False)
    res = run_bass_kernel_spmd(nc, in_maps, core_ids=list(range(8)))
    out = np.empty((B, N, D), dtype=np.float32)
    for core in range(8):
        b, half = core // 2, core % 2
        out[b, half * NQ:(half + 1) * NQ] = res.results[core]["y"]
    return out
